# version 20 (final)
# baseline (speedup 1.0000x reference)
import contextlib
import numpy as np
import ml_dtypes

BFNP = ml_dtypes.bfloat16

D1LITE = ""

import concourse.bass as bass
import concourse.mybir as mybir
from concourse.bass_utils import run_bass_kernel_spmd

F32 = mybir.dt.float32
BF16 = mybir.dt.bfloat16
AF = mybir.ActivationFunctionType
ALU = mybir.AluOpType

NT = 2048
QB = 512
NQB = NT // QB
DC = 8
XW = 2056
HID = 2730
HIDP = 2816
NJ = HIDP // 128
EPS = 1e-5
C_DENSE = 10.0
C_LOC = 8.0
MBIG = -30000.0

ENGINES = ("tensor", "scalar", "vector", "gpsimd", "sync")


class Prog:
    def __init__(self, nc, stack):
        self.nc = nc
        self.stack = stack
        self.sems = {}
        self.cnt = {}
        for s in ("Lg", "Ls", "Lw0", "Lw1", "Lx", "Lo", "Ld", "T", "S", "V", "G"):
            self.sems[s] = stack.enter_context(nc.semaphore(s))
            self.cnt[s] = 0


class Phase:
    def __init__(self, prog, name):
        self.prog = prog
        self.name = name
        self.ops = {e: [] for e in ENGINES}
        self.entry = {e: [] for e in ENGINES}

    def op(self, eng, fn, waits=(), inc=None, n=1):
        waits = list(waits)
        if eng == "vector":
            if inc is None:
                inc = "V"
            waits.append(("V", self.prog.cnt["V"]))
        elif eng == "scalar":
            if inc is None:
                inc = "S"
            waits.append(("S", self.prog.cnt["S"]))
        self.ops[eng].append((waits, fn, inc, n))
        if inc is not None:
            self.prog.cnt[inc] += n
            return self.prog.cnt[inc]
        return None

    def dma(self, eng, out, in_, waits=(), sem=None):
        if sem is None:
            sem = "Lg" if eng == "gpsimd" else "Ls"
        return self.op(eng, lambda e, o=out, i=in_: e.dma_start(out=o, in_=i),
                       waits, sem, 16)

    def tapdma(self, out, in_, waits=()):
        w = list(waits) + [("Ld", self.prog.cnt["Ld"])]
        return self.op("gpsimd", lambda e, o=out, i=in_: e.dma_start(out=o, in_=i),
                       w, "Ld", 16)

    def run(self):
        nc = self.prog.nc
        prog = self.prog
        with nc.Block() as blk:
            for eng in ENGINES:
                ops = self.ops[eng]
                if not ops:
                    continue

                def body(e, ops=ops, ent=self.entry[eng]):
                    with nc.allow_low_precision(reason="bf16 storage by design; fp32 accum"):
                        for s, v in ent:
                            e.wait_ge(prog.sems[s], v)
                        for waits, fn, inc, n in ops:
                            for s, v in waits:
                                e.wait_ge(prog.sems[s], v)
                            instr = fn(e)
                            if inc is not None:
                                instr.then_inc(prog.sems[inc], n)
                getattr(blk, eng)(body)


def bc(ap2d, mid):
    return ap2d.unsqueeze(1).broadcast_to([ap2d.shape[0], mid, ap2d.shape[1]])



def host_prep(I):
    wdt = np.asarray(I["L0_attn_x"]).dtype
    f8 = np.float64

    def cast(a):
        return np.ascontiguousarray(np.asarray(a, dtype=wdt))

    qkv_w = np.asarray(I["L0_attn_qkv_w"])
    out = {}
    out["wqT"] = cast(qkv_w[0:1024].T)
    out["wkT"] = cast(qkv_w[1024:2048].T)
    out["wvT"] = cast(qkv_w[2048:3072].T)
    out["projT"] = cast(np.asarray(I["L0_attn_proj_w"]).T)
    qkv_b = np.asarray(I["L0_attn_qkv_b"])
    out["bq"] = cast(qkv_b[0:1024].reshape(8, 128).T)
    out["bk"] = cast(qkv_b[1024:2048].reshape(8, 128).T)
    out["bv"] = cast(qkv_b[2048:3072].reshape(8, 128).T)
    out["bproj"] = cast(np.asarray(I["L0_attn_proj_b"]).reshape(8, 128).T)

    out["wqkv"] = cast(I["L0_ffn_wqkv"])
    out["wo"] = cast(I["L0_ffn_wo"])
    out["qs"] = cast(np.tile(np.asarray(I["L0_ffn_q_scale"]), (1, 2)).T)
    out["ks"] = cast(np.tile(np.asarray(I["L0_ffn_k_scale"]), (1, 2)).T)
    out["lng"] = cast(np.asarray(I["L0_ffn_ln_g"]).reshape(2, 8, 128).transpose(2, 0, 1))
    out["lnb"] = cast(np.asarray(I["L0_ffn_ln_b"]).reshape(2, 8, 128).transpose(2, 0, 1))

    w1 = np.asarray(I["L0_ffn_w1"])
    b1 = np.asarray(I["L0_ffn_b1"])
    a = np.zeros((2, 1024, HIDP), dtype=wdt); a[:, :, :HID] = w1[:, :, :HID]
    g = np.zeros((2, 1024, HIDP), dtype=wdt); g[:, :, :HID] = w1[:, :, HID:]
    w1i = np.zeros((2, 1024, 2 * HIDP), dtype=wdt)
    b1a = np.zeros((2, HIDP), dtype=wdt); b1a[:, :HID] = b1[:, :HID]
    b1g = np.zeros((2, HIDP), dtype=wdt); b1g[:, :HID] = b1[:, HID:]
    b1i = np.zeros((128, 2, 2 * NJ), dtype=wdt)
    for j in range(NJ):
        w1i[:, :, 256 * j:256 * j + 128] = a[:, :, 128 * j:128 * j + 128]
        w1i[:, :, 256 * j + 128:256 * j + 256] = g[:, :, 128 * j:128 * j + 128]
        b1i[:, :, 2 * j] = b1a[:, 128 * j:128 * j + 128].T
        b1i[:, :, 2 * j + 1] = b1g[:, 128 * j:128 * j + 128].T
    out["w1i"] = np.ascontiguousarray(w1i)
    out["b1i"] = np.ascontiguousarray(b1i)
    w2p = np.zeros((2, HIDP, 1024), dtype=wdt)
    w2p[:, :HID] = np.asarray(I["L0_ffn_w2"])
    out["w2p"] = w2p
    out["b2"] = cast(np.asarray(I["L0_ffn_b2"]).reshape(2, 8, 128).transpose(2, 0, 1))

    i32 = np.arange(32, dtype=f8)
    inv_freq = 10000.0 ** (-(2 * i32) / 64.0)
    t = np.arange(NT, dtype=f8)
    fr = t[:, None] * inv_freq[None, :]
    base = (2 * i32 + 0.4 * 64) / (1.4 * 64)
    power = (t - NT // 2) / (512 // 2)
    sc = base[None, :] ** power[:, None]
    cos = np.cos(fr); sin = np.sin(fr)
    def dup(v):
        w = np.concatenate([v, v], axis=1)
        w = np.concatenate([w, w], axis=1)
        return np.ascontiguousarray(w.T)
    out["csq"] = dup(8.0 * cos * sc).astype(BFNP)
    out["snq"] = dup(8.0 * sin * sc).astype(BFNP)
    out["csk"] = dup(cos / sc).astype(BFNP)
    out["snk"] = dup(sin / sc).astype(BFNP)

    out["iden"] = cast(np.eye(128))
    out["ones"] = cast(np.ones((128, 128)))
    R = np.zeros((128, 128), dtype=f8)
    for B in (0, 64):
        for d in range(32):
            R[B + d + 32, B + d] = -1.0
            R[B + d, B + d + 32] = 1.0
    out["rrot"] = cast(R)
    out["mself"] = cast(np.triu(np.full((128, 128), MBIG), 1))
    out["cneg"] = cast(np.full((1, 512), MBIG))
    out["mprev"] = cast(np.tril(np.full((128, 128), MBIG), -1))
    sel = np.zeros((16, 8, 128), dtype=f8)
    for h in range(16):
        for c in range(8):
            for p in range(128):
                if h == 2 * c + (1 if p >= 64 else 0):
                    sel[h, c, p] = 1.0
    out["sel"] = cast(sel.reshape(16, 1024))
    el2 = np.zeros((128, 4, 8), dtype=f8)
    for p in range(128):
        for c in range(4):
            el2[p, c, 2 * c + (1 if p >= 64 else 0)] = 1.0
    out["el2"] = cast(el2.reshape(128, 32))
    selh = np.zeros((16, 16, 128), dtype=f8)
    for h in range(16):
        selh[h, h, :] = 1.0
    out["selh"] = cast(selh.transpose(1, 0, 2).reshape(16, 2048))
    el2d = np.zeros((128, 8, 16), dtype=f8)
    for p in range(128):
        for c in range(8):
            el2d[p, c, 2 * c + (1 if p >= 64 else 0)] = 1.0
    out["el2d"] = cast(el2d.reshape(128, 128))
    cf = np.zeros((128, 4), dtype=f8)
    cf[:, 0] = EPS
    cf[:, 1] = 1e-24
    cf[:, 2] = -C_LOC
    out["constf"] = cast(cf)
    return out



class B:
    pass


def declare_dram(nc, stop):
    b = B()
    ap = lambda h: h.ap()
    d = lambda name, shape: ap(nc.dram_tensor(name, list(shape), F32, kind="ExternalInput"))
    b.xT = d("xT", (1024, NT))
    b.wqT, b.wkT, b.wvT, b.projT = d("wqT", (1024, 1024)), d("wkT", (1024, 1024)), d("wvT", (1024, 1024)), d("projT", (1024, 1024))
    b.bq, b.bk, b.bv, b.bproj = d("bq", (128, 8)), d("bk", (128, 8)), d("bv", (128, 8)), d("bproj", (128, 8))
    b.wqkv = d("wqkv", (2, 1024, 1536))
    b.wo = d("wo", (2, 512, 1024))
    b.qs, b.ks = d("qs", (128, 2)), d("ks", (128, 2))
    b.lng, b.lnb = d("lng", (128, 2, 8)), d("lnb", (128, 2, 8))
    b.w1i, b.b1i = d("w1i", (2, 1024, 2 * HIDP)), d("b1i", (128, 2, 2 * NJ))
    b.w2p, b.b2 = d("w2p", (2, HIDP, 1024)), d("b2", (128, 2, 8))
    b.csq, b.snq, b.csk, b.snk = (
        ap(nc.dram_tensor(n, [128, NT], BF16, kind="ExternalInput"))
        for n in ("csq", "snq", "csk", "snk"))
    b.iden, b.ones, b.rrot = d("iden", (128, 128)), d("ones", (128, 128)), d("rrot", (128, 128))
    b.mself, b.mprev = d("mself", (128, 128)), d("mprev", (128, 128))
    b.cneg = d("cneg", (1, 512))
    b.sel, b.el2 = d("sel", (16, 1024)), d("el2", (128, 32))
    b.constf = d("constf", (128, 4))
    b.el2d = d("el2d", (128, 128))
    b.selh = d("selh", (16, 2048))
    b.xsave = ap(nc.dram_tensor("xsave", [1024, NT], F32, kind="Internal"))
    b.yT = ap(nc.dram_tensor("yT", [1024, NT], F32, kind="ExternalOutput"))
    b.dbg = ap(nc.dram_tensor("dbg", [1024, NT], F32, kind="ExternalOutput"))
    return b


def rearr(ap_, p=128):
    return ap_.rearrange("(c p) t -> p c t", p=p)


def alloc_globals(nc, stack, b):
    sb = lambda name, shape, dt: stack.enter_context(nc.sbuf_tensor(name, list(shape), dt))
    b.x = sb("x_res", [128, DC, XW], F32)
    b.cast = sb("castbuf", [128, DC, 520], BF16)
    b.sq = sb("sqbuf", [128, 4, QB], BF16)
    b.sqm = sb("sqm", [128, QB], F32)
    b.stm = sb("stat_m", [128, NT], BF16)
    b.str_ = sb("stat_r", [128, NT], BF16)
    b.st2m = sb("stat2_m", [128, QB], BF16)
    b.st2r = sb("stat2_r", [128, QB], BF16)
    b.c_id = sb("c_id", [128, 128], BF16)
    b.c_on = sb("c_on", [128, 128], BF16)
    b.c_rr = sb("c_rr", [128, 128], BF16)
    b.c_ms = sb("c_ms", [128, 128], BF16)
    b.c_mp = sb("c_mp", [128, 128], BF16)
    b.c_ng = sb("c_ng", [1, 512], BF16)
    b.c_sel = sb("c_sel", [16, 1024], BF16)
    b.c_el2 = sb("c_el2", [128, 32], BF16)
    b.s_bq = sb("s_bq", [128, 8], F32)
    b.s_bk = sb("s_bk", [128, 8], F32)
    b.s_bv = sb("s_bv", [128, 8], F32)
    b.s_bp = sb("s_bp", [128, 8], F32)
    b.s_qs = sb("s_qs", [128, 2], F32)
    b.s_ks = sb("s_ks", [128, 2], F32)
    b.s_lng = sb("s_lng", [128, 2, 8], F32)
    b.s_lnb = sb("s_lnb", [128, 2, 8], F32)
    b.s_b1i = sb("s_b1i", [128, 2, 2 * NJ], F32)
    b.s_b2 = sb("s_b2", [128, 2, 8], F32)
    b.s_cf = sb("s_cf", [128, 4], F32)
    b.c_el2d = sb("c_el2d", [128, 128], BF16)
    b.c_selh = sb("c_selh", [16, 2048], BF16)


def phase_load(prog, b):
    ph = Phase(prog, "load")
    ph.dma("sync", b.x[:, :, 1:NT + 1], rearr(b.xT))
    for dst, src in ((b.s_bq, b.bq), (b.s_bk, b.bk), (b.s_bv, b.bv), (b.s_bp, b.bproj)):
        ph.dma("sync", dst[:], src[:, :])
    ph.dma("sync", b.s_qs[:], b.qs[:, :])
    ph.dma("sync", b.s_ks[:], b.ks[:, :])
    ph.dma("sync", b.s_lng[:], b.lng[:, :, :])
    ph.dma("sync", b.s_lnb[:], b.lnb[:, :, :])
    ph.dma("sync", b.s_b1i[:], b.b1i[:, :, :])
    ph.dma("sync", b.s_b2[:], b.b2[:, :, :])
    ph.dma("sync", b.s_cf[:], b.constf[:, :])
    for dst, src in ((b.c_id, b.iden), (b.c_on, b.ones), (b.c_rr, b.rrot),
                     (b.c_ms, b.mself), (b.c_mp, b.mprev), (b.c_ng, b.cneg)):
        ph.dma("gpsimd", dst[:], src[:, :])
    ph.dma("gpsimd", b.c_sel[:], b.sel[:, :])
    ph.dma("gpsimd", b.c_el2[:], b.el2[:, :])
    ph.dma("gpsimd", b.c_el2d[:], b.el2d[:, :])
    ph.dma("gpsimd", b.c_selh[:], b.selh[:, :])
    ph.op("vector", lambda e: e.memset(b.x[:, :, 0:1], 0.0), inc="V")
    ph.run()
    prog.mark_ls = prog.cnt["Ls"]
    prog.mark_lg = prog.cnt["Lg"]



def emit_stats_qb(ph, b, pb, qb, src_slices, dst_m, dst_r, last_cast_wait):
    s_sq1 = None
    for c in range(4):
        s_sq1 = ph.op("scalar", lambda e, c=c: e.activation(
            b.sq[:, c, :], src_slices(c), AF.Square), waits=[("V", last_cast_wait)],
            inc="S")
    for c in range(4):
        ph.op("tensor", lambda e, c=c: e.matmul(
            pb[:, 0, :], b.c_on[:], src_slices(c), start=(c == 0), stop=False),
            waits=([("V", last_cast_wait)] if c == 0 else []))
    t_x2a = None
    for c in range(4):
        t_x2a = ph.op("tensor", lambda e, c=c: e.matmul(
            pb[:, 1, :], b.c_on[:], b.sq[:, c, :], start=(c == 0), stop=False),
            waits=([("S", s_sq1)] if c == 0 else []), inc="T" if c == 3 else None)
    s_sq2 = None
    for c in range(4, 8):
        s_sq2 = ph.op("scalar", lambda e, c=c: e.activation(
            b.sq[:, c - 4, :], src_slices(c), AF.Square),
            waits=([("T", t_x2a)] if c == 4 else []), inc="S")
    t_x = None
    for c in range(4, 8):
        t_x = ph.op("tensor", lambda e, c=c: e.matmul(
            pb[:, 0, :], b.c_on[:], src_slices(c), start=False, stop=(c == 7)),
            inc="T" if c == 7 else None)
    t_x2 = None
    for c in range(4, 8):
        t_x2 = ph.op("tensor", lambda e, c=c: e.matmul(
            pb[:, 1, :], b.c_on[:], b.sq[:, c - 4, :], start=False, stop=(c == 7)),
            waits=([("S", s_sq2)] if c == 4 else []), inc="T" if c == 7 else None)
    s_m = ph.op("scalar", lambda e: e.activation(
        dst_m, pb[:, 0, :], AF.Copy, scale=1.0 / 1024), waits=[("T", t_x)], inc="S")
    s_m2 = ph.op("scalar", lambda e: e.activation(
        b.sqm[:], dst_m, AF.Square), waits=[("S", s_m)], inc="S")
    v_var = ph.op("vector", lambda e: e.scalar_tensor_tensor(
        b.sqm[:], pb[:, 1, :], 1.0 / 1024, b.sqm[:], ALU.mult, ALU.subtract),
        waits=[("S", s_m2), ("T", t_x2)], inc="V")
    s_sd = ph.op("scalar", lambda e: e.activation(
        b.sqm[:], b.sqm[:], AF.Sqrt, bias=b.s_cf[:, 0:1]), waits=[("V", v_var)], inc="S")
    v_r = ph.op("vector", lambda e: e.reciprocal(dst_r, b.sqm[:]),
                waits=[("S", s_sd)], inc="V")
    return v_r



def phase_dense(prog, b, stack, taps, stop=None):
    nc = prog.nc
    with contextlib.ExitStack() as dstack:
        sb = lambda name, shape, dt: dstack.enter_context(nc.sbuf_tensor(name, list(shape), dt))
        kT = sb("kT", [128, DC, NT], BF16)
        vTM = sb("vTM", [128, 16, 16, 66], BF16)
        v_sb = sb("v_sb", [128, QB], BF16)
        kmax = sb("kmax", [16, 2], F32)
        knrm = sb("knrm", [16, QB], F32)

        wstack = contextlib.ExitStack()
        sbw = lambda name, shape, dt: wstack.enter_context(nc.sbuf_tensor(name, list(shape), dt))
        wA = sbw("wA", [128, DC, 1024], BF16)
        wB = sbw("wB", [128, DC, 1024], BF16)
        ph = Phase(prog, "D0")
        for _e in ("vector", "tensor", "scalar"):
            ph.entry[_e] = [("Ls", prog.mark_ls), ("Lg", prog.mark_lg)]
        l_wk = ph.dma("gpsimd", wA[:], rearr(b.wkT), waits=[("Lg", prog.mark_lg)])
        l_wv = ph.dma("gpsimd", wB[:], rearr(b.wvT))
        d0stack = contextlib.ExitStack()
        pb = d0stack.enter_context(nc.psum_tensor("psD0", [128, 2, QB], F32))
        for qb in range(NQB):
            vw = ph.op("vector", lambda e, qb=qb: e.tensor_copy(
                b.cast[:, :, 0:513], b.x[:, :, qb * QB:qb * QB + 513]), inc="V")
            emit_stats_qb(ph, b, pb, qb,
                          lambda c: b.cast[:, c, 1:513],
                          b.stm[:, qb * QB:(qb + 1) * QB],
                          b.str_[:, qb * QB:(qb + 1) * QB], vw)
        ph.run()
        d0stack.close()
        if stop == "d0":
            tap_x(prog, b)
            return

        ph = Phase(prog, "D1")
        d1stack = contextlib.ExitStack()
        pb = d1stack.enter_context(nc.psum_tensor("psD1", [128, 8, QB], F32))
        t_evt = {}
        last_users = 0
        tr_evt = {0: 0, 1: 0}
        s_last = {0: None, 1: None}
        v_tr = {0: None, 1: None}
        t_tr = {0: None, 1: None}
        s_knrm = None
        t_kn_hist = {}
        prev_mm = {}
        for qb in range(NQB):
            vw = ph.op("vector", lambda e, qb=qb: e.tensor_copy(
                b.cast[:, :, 0:513], b.x[:, :, qb * QB:qb * QB + 513]),
                waits=([("T", prev_mm.get("last", 0))] if qb else []), inc="V")
            vw = ph.op("vector", lambda e, qb=qb: e.tensor_tensor(
                b.cast[:, :, 1:513], b.cast[:, :, 1:513],
                bc(b.stm[:, qb * QB:(qb + 1) * QB], DC), ALU.subtract), inc="V")
            vw = ph.op("vector", lambda e, qb=qb: e.tensor_tensor(
                b.cast[:, :, 1:513], b.cast[:, :, 1:513],
                bc(b.str_[:, qb * QB:(qb + 1) * QB], DC), ALU.mult), inc="V")
            tlast = None
            for co in range(16):
                par = co % 2
                wsb = wA if co < 8 else wB
                wcol = co * 128 if co < 8 else (co - 8) * 128
                waits = [("V", vw)]
                if co == 0 and qb == 0:
                    waits.append(("Lg", l_wv))
                if s_last[par] is not None:
                    waits.append(("S", s_last[par]))
                for ci in range(8):
                    tlast = ph.op("tensor", lambda e, ci=ci, par=par, wsb=wsb, wcol=wcol: e.matmul(
                        pb[:, par, :], wsb[:, ci, wcol:wcol + 128],
                        b.cast[:, ci, 1:513], start=(ci == 0), stop=(ci == 7)),
                        waits=(waits if ci == 0 else []), inc="T" if ci == 7 else None)
                if co < 8:
                    s_ev = ph.op("scalar", lambda e, co=co, par=par, qb=qb: e.activation(
                        kT[:, co, qb * QB:(qb + 1) * QB], pb[:, par, :], AF.Identity,
                        bias=b.s_bk[:, co:co + 1]), waits=[("T", tlast)], inc="S")
                    if D1LITE in ("proj", "tr"):
                        s_last[par] = s_ev
                        continue
                    sqw = []
                    if co >= 4 and t_kn_hist.get(co - 4):
                        sqw.append(("T", t_kn_hist[co - 4]))
                    s_ksq = ph.op("scalar", lambda e, co=co, qb=qb: e.activation(
                        b.sq[:, co % 4, :], kT[:, co, qb * QB:(qb + 1) * QB],
                        AF.Square), waits=sqw, inc="S")
                    s_last[par] = s_ksq
                    kw = [("S", s_ksq)]
                    if co == 0 and s_knrm is not None:
                        kw.append(("S", s_knrm))
                    t_kn = ph.op("tensor", lambda e, co=co: e.matmul(
                        pb[0:16, 2, :], b.c_el2d[:, co * 16:co * 16 + 16],
                        b.sq[:, co % 4, :], start=(co == 0), stop=(co == 7)),
                        waits=kw, inc="T")
                    t_kn_hist[co] = t_kn
                    if co == 7:
                        s_knrm = ph.op("scalar", lambda e: e.activation(
                            knrm[:], pb[0:16, 2, :], AF.Sqrt), waits=[("T", t_kn)], inc="S")
                        if qb == 0:
                            v_km = ph.op("vector", lambda e: e.reduce_max(
                                kmax[:, 0:1], knrm[:], axis=mybir.AxisListType.X),
                                waits=[("S", s_knrm)], inc="V")
                        else:
                            v_km = ph.op("vector", lambda e: e.reduce_max(
                                kmax[:, 1:2], knrm[:], axis=mybir.AxisListType.X),
                                waits=[("S", s_knrm)], inc="V")
                            v_km = ph.op("vector", lambda e: e.tensor_max(
                                kmax[:, 0:1], kmax[:, 0:1], kmax[:, 1:2]), inc="V")
                else:
                    cv = co - 8
                    s_ev = ph.op("scalar", lambda e, par=par, cv=cv: e.activation(
                        v_sb[:], pb[:, par, :], AF.Identity, bias=b.s_bv[:, cv:cv + 1]),
                        waits=[("T", tlast)] + ([("T", t_tr.get("v_done", 0))] if False else []),
                        inc="S")
                    s_last[par] = s_ev
                    for tt in range(4):
                        if D1LITE == "proj":
                            continue
                        slot = tt % 2
                        twaits = [("S", s_ev)]
                        if v_tr[slot] is not None:
                            twaits.append(("V", v_tr[slot]))
                        t_tr[slot] = ph.op("tensor", lambda e, tt=tt, slot=slot: e.transpose(
                            pb[:, 4 + slot, 0:64].bitcast(BF16),
                            v_sb[:, tt * 128:tt * 128 + 128], b.c_id[:]),
                            waits=twaits, inc="T")
                        v_tr[slot] = ph.op("vector", lambda e, qb=qb, tt=tt, cv=cv, slot=slot: e.tensor_copy(
                            vTM[:, qb * 4 + tt, 2 * cv:2 * cv + 2, 0:64],
                            pb[:, 4 + slot, 0:64].bitcast(BF16)),
                            waits=[("T", t_tr[slot])], inc="V")
            prev_mm["last"] = tlast
        ph.op("vector", lambda e: e.memset(vTM[:, :, :, 64:65], 1.0), inc="V")
        ph.run()
        d1stack.close()
        wstack.close()
        if stop == "d1":
            tap_x(prog, b)
            return

        if "kT" in taps:
            tp = Phase(prog, "tap_kT")
            tp.dma("gpsimd", rearr(b.yT), kT[:, :, :])
            tp.op("sync", lambda e: e.nop(), waits=[("Lg", prog.cnt["Lg"])])
            tp.run()
            return

        ph = Phase(prog, "D2")
        d2stack = contextlib.ExitStack()
        sb2 = lambda name, shape, dt: d2stack.enter_context(nc.sbuf_tensor(name, list(shape), dt))
        wQ = sb2("wQ", [128, 2, DC, 128], BF16)
        wP = sb2("wP", [128, 2, DC, 128], BF16)
        qto = sb2("qTO", [128, DC, QB], BF16)
        qn = knrm
        bq16 = sb2("bq16", [16, QB], BF16)
        bnegb = b.sq
        tmp2 = sb2("tmp2", [128, 4, QB], F32)
        attb = sb2("attb", [128, DC, QB], BF16)
        sums = b.cast[:, 4:8, 0:512]
        pt = sb2("PTbuf", [128, 8, QB], BF16)
        sumsb = sb2("sumsb", [16, QB], BF16)
        pb = d2stack.enter_context(nc.psum_tensor("psD2", [128, 8, QB], F32))
        s_evq = {0: None, 1: None}
        s_att = {0: None, 1: None}
        t_wp_cons = {0: 0, 1: 0}
        t_wq_cons = {0: 0, 1: 0}
        l_sc_prev = 0
        v_resid = 0
        t_rn_prev = 0
        s_qn_prev = 0
        for qb in range(NQB):
            qc = qb * QB
            cw = []
            if qb:
                cw.append(("T", t_wq_cons[1] or 0))
                if l_sc_prev:
                    cw.append(("Lx", l_sc_prev))
            vw = ph.op("vector", lambda e, qb=qb: e.tensor_copy(
                b.cast[:, :, 1:513], b.x[:, :, qb * QB + 1:qb * QB + 513]),
                waits=cw, inc="V")
            vw = ph.op("vector", lambda e, qc=qc: e.tensor_tensor(
                b.cast[:, :, 1:513], b.cast[:, :, 1:513],
                bc(b.stm[:, qc:qc + QB], DC), ALU.subtract), inc="V")
            vw = ph.op("vector", lambda e, qc=qc: e.tensor_tensor(
                b.cast[:, :, 1:513], b.cast[:, :, 1:513],
                bc(b.str_[:, qc:qc + QB], DC), ALU.mult), inc="V")
            l_taph = 0
            if "d2" in taps and qb == 0:
                l_taph = ph.tapdma(rearr(b.dbg)[:, 0:1, 1536:2048],
                                   b.stm[0:128, 0:512].unsqueeze(1), waits=[("V", vw)])
                l_taph = ph.tapdma(rearr(b.dbg)[:, 1:2, 1536:2048],
                                   b.str_[0:128, 0:512].unsqueeze(1), waits=[("V", vw)])
                l_taph = ph.tapdma(rearr(b.dbg)[:, 2:4, 1536:2048],
                                   b.cast[:, 0:2, 1:513], waits=[("V", vw)])
            for co in range(8):
                par = co % 2
                qw = []
                if t_wq_cons[par]:
                    qw.append(("T", t_wq_cons[par]))
                if t_wp_cons[par]:
                    qw.append(("T", t_wp_cons[par]))
                l_wq = ph.dma("gpsimd", wQ[:, par, :, :],
                              b.wqT[:, 128 * co:128 * co + 128].rearrange(
                                  "(c p) f -> p c f", p=128), waits=qw, sem="Lw%d" % par)
                waits = [("V", vw), ("Lw%d" % par, l_wq)]
                if s_evq[par] is not None:
                    waits.append(("S", s_evq[par]))
                if s_att[par] is not None:
                    waits.append(("S", s_att[par]))
                tlast = None
                for ci in range(8):
                    tlast = ph.op("tensor", lambda e, ci=ci, par=par, co=co: e.matmul(
                        pb[:, par, :], wQ[:, par, ci, :],
                        b.cast[:, ci, 1:513], start=(ci == 0), stop=(ci == 7)),
                        waits=(waits if ci == 0 else []), inc="T" if ci == 7 else None)
                t_wq_cons[par] = tlast
                s_evq[par] = ph.op("scalar", lambda e, co=co, par=par: e.activation(
                    qto[:, co, :], pb[:, par, :], AF.Identity, bias=b.s_bq[:, co:co + 1]),
                    waits=[("T", tlast)], inc="S")
            l_tapq = 0
            if "d2" in taps and qb == 0:
                l_tapq = ph.tapdma(rearr(b.dbg)[:, :, 512:1024], qto[:, :, :],
                                   waits=[("S", s_evq[0]), ("S", s_evq[1])])
            t_qn = None
            s_qsq = None
            for grp in range(2):
                for c4 in range(4):
                    c = grp * 4 + c4
                    sqw = [("S", s_evq[c % 2])] if grp == 0 and c4 == 0 else []
                    if grp == 1 and c4 == 0:
                        sqw.append(("T", t_qn))
                    s_qsq = ph.op("scalar", lambda e, c=c, c4=c4: e.activation(
                        b.sq[:, c4, :], qto[:, c, :], AF.Square), waits=sqw, inc="S")
                for c4 in range(4):
                    c = grp * 4 + c4
                    qw = [("S", s_qsq)] if c4 == 0 else []
                    if grp == 0 and c4 == 0:
                        qw.append(("S", s_qn_prev))
                        if t_rn_prev:
                            qw.append(("T", t_rn_prev))
                    t_qn = ph.op("tensor", lambda e, c=c, c4=c4: e.matmul(
                        pb[0:16, 6, :], b.c_el2d[:, c * 16:c * 16 + 16],
                        b.sq[:, c4, :], start=(c == 0), stop=(c == 7)),
                        waits=qw, inc="T")
            s_qn_prev = ph.op("scalar", lambda e: e.activation(
                qn[:], pb[0:16, 6, :], AF.Sqrt), waits=[("T", t_qn)], inc="S")
            v_bq = ph.op("vector", lambda e: e.tensor_scalar(
                bq16[:], qn[:], kmax[:, 0:1], -1.0, ALU.mult, ALU.mult),
                waits=[("S", s_qn_prev)] + ([("T", t_rn_prev)] if t_rn_prev else []),
                inc="V")
            v_oev = None
            v_last_of_pair = {}
            s_expg = {}
            v_addg = {}
            v_bb = {}
            t_av = {i: 0 for i in range(8)}
            for c in range(8):
                obase = 4 + 2 * (c % 2)
                for par_h in range(2):
                    h = 2 * c + par_h
                    bslot = (c % 2) * 2 + par_h
                    wts = [("V", v_bq)]
                    if v_addg.get((7, par_h, c - 1)) is not None:
                        wts.append(("V", v_addg[(7, par_h, c - 1)]))
                    t_s = ph.op("tensor", lambda e, h=h, par_h=par_h: e.matmul(
                        pb[:, par_h * 2, :], b.c_selh[:, h * 128:h * 128 + 128],
                        bq16[:], start=True, stop=True), waits=wts, inc="T")
                    v_bb[bslot] = ph.op("vector", lambda e, bslot=bslot, par_h=par_h: e.tensor_copy(
                        bnegb[:, bslot, :], pb[:, par_h * 2, :]), waits=[("T", t_s)], inc="V")
                t_avl = {}
                for kg in range(9):
                    if kg < 8:
                        tsm = {}
                        for par_h in range(2):
                            for j2 in range(2):
                                kc = 2 * kg + j2
                                bank = par_h * 2 + j2
                                waits = []
                                if s_evq[0] is not None and kg == 0 and par_h == 0 and j2 == 0:
                                    waits += [("S", s_evq[0]), ("S", s_evq[1])]
                                prev_add = v_addg.get((kg - 1, par_h, c)) or v_addg.get((7, par_h, c - 1))
                                if kg == 0:
                                    waits.append(("V", v_bb[(c % 2) * 2 + par_h]))
                                if prev_add is not None:
                                    waits.append(("V", prev_add))
                                tsm[par_h] = ph.op("tensor", lambda e, kc=kc, r0=64 * par_h, bank=bank, c=c: e.matmul(
                                    pb[:, bank, :], kT[r0:r0 + 64, c, kc * 128:kc * 128 + 128],
                                    qto[r0:r0 + 64, c, :], start=True, stop=True,
                                    tile_position=(r0, 0)), waits=waits, inc="T")
                        for par_h in range(2):
                            bslot = (c % 2) * 2 + par_h
                            aw = [("T", tsm[par_h])]
                            prev_exp = s_expg.get((kg - 1, par_h, c)) or s_expg.get((7, par_h, c - 1))
                            if prev_exp is not None:
                                aw.append(("S", prev_exp))
                            v_addg[(kg, par_h, c)] = ph.op("vector", lambda e, par_h=par_h, bslot=bslot: e.tensor_tensor(
                                tmp2[:, par_h * 2:par_h * 2 + 2, :],
                                pb[:, par_h * 2:par_h * 2 + 2, :],
                                bnegb[:, bslot:bslot + 1, :].broadcast_to([128, 2, QB]),
                                ALU.add), waits=aw, inc="V")
                            ptbase = (kg % 2) * 4 + par_h * 2
                            wts = [("V", v_addg[(kg, par_h, c)])]
                            if t_av[ptbase] or t_av[ptbase + 1]:
                                wts.append(("T", max(t_av[ptbase], t_av[ptbase + 1])))
                            s_expg[(kg, par_h, c)] = ph.op("scalar", lambda e, par_h=par_h, ptbase=ptbase: e.activation(
                                pt[:, ptbase:ptbase + 2, :], tmp2[:, par_h * 2:par_h * 2 + 2, :],
                                AF.Exp, scale=0.125), waits=wts, inc="S")
                    if kg >= 1:
                        kga = kg - 1
                        for par_h in range(2):
                            for j2 in range(2):
                                ka = 2 * kga + j2
                                slot = (kga % 2) * 4 + par_h * 2 + j2
                                obank = obase + par_h
                                avw = [("S", s_expg[(kga, par_h, c)])]
                                if ka == 0:
                                    if v_last_of_pair.get(c - 2) is not None:
                                        avw.append(("V", v_last_of_pair[c - 2]))
                                    if c == 1 and s_qn_prev:
                                        avw.append(("S", s_qn_prev))
                                t_av[slot] = ph.op("tensor", lambda e, c=c, ka=ka, slot=slot, obank=obank, par_h=par_h: e.matmul(
                                    pb[0:65, obank, :], vTM[:, ka, 2 * c + par_h, 0:65],
                                    pt[:, slot, :], start=(ka == 0), stop=(ka == 15)),
                                    waits=avw, inc="T")
                                t_avl[par_h] = t_av[slot]
                h0, h1 = 2 * c, 2 * c + 1
                ph.op("vector", lambda e, h=h0, obase=obase: e.tensor_copy(
                    sums[32 * (h // 4):32 * (h // 4) + 1, h % 4, :], pb[64:65, obase, :]),
                    waits=[("T", t_avl[0])], inc="V")
                ph.op("vector", lambda e, h=h1, obase=obase: e.tensor_copy(
                    sums[32 * (h // 4):32 * (h // 4) + 1, h % 4, :], pb[64:65, obase + 1, :]),
                    waits=[("T", t_avl[1])], inc="V")
                ph.op("vector", lambda e, c=c, obase=obase: e.tensor_copy(
                    qto[0:64, c, :], pb[0:64, obase, :]),
                    waits=([("Ld", l_tapq)] if (c == 0 and l_tapq) else []), inc="V")
                v_oev = ph.op("vector", lambda e, c=c, obase=obase: e.tensor_copy(
                    qto[64:128, c, :], pb[0:64, obase + 1, :]), inc="V")
                v_last_of_pair[c] = v_oev
            l_sc = None
            for g in range(4):
                gw = [("V", v_oev)]
                if g == 0 and t_rn_prev:
                    gw.append(("T", t_rn_prev))
                l_sc = ph.dma("sync", sumsb[4 * g:4 * g + 4, :],
                              sums[32 * g:32 * g + 1, :, :], waits=gw, sem="Lx")
            l_sc_prev = l_sc
            v_rc = ph.op("vector", lambda e: e.reciprocal(sumsb[:], sumsb[:]),
                         waits=[("Lx", l_sc)], inc="V")
            v_nrm = None
            t_rn = None
            for c in range(8):
                wts = [("V", v_rc)]
                if v_nrm is not None:
                    wts.append(("V", v_nrm))
                t_rn = ph.op("tensor", lambda e, c=c: e.matmul(
                    pb[:, 6, :], b.c_sel[:, c * 128:c * 128 + 128], sumsb[:],
                    start=True, stop=True), waits=wts, inc="T")
                v_nrm = ph.op("vector", lambda e, c=c: e.tensor_tensor(
                    qto[:, c, :], qto[:, c, :], pb[:, 6, :], ALU.mult),
                    waits=[("T", t_rn)], inc="V")
            t_rn_prev = t_rn
            if "d2" in taps and qb == 0:
                ph.tapdma(rearr(b.dbg)[:, :, 0:512], qto[:, :, :],
                          waits=[("V", v_nrm)])
            for f in range(8):
                par = f % 2
                wts = []
                if t_wp_cons[par]:
                    wts.append(("T", t_wp_cons[par]))
                if t_wq_cons[par]:
                    wts.append(("T", t_wq_cons[par]))
                l_wp = ph.dma("gpsimd", wP[:, par, :, :],
                              b.projT[:, 128 * f:128 * f + 128].rearrange(
                                  "(c p) f -> p c f", p=128), waits=wts,
                              sem="Lw%d" % par)
                tlast = None
                waits = [("V", v_nrm), ("Lw%d" % par, l_wp)]
                if s_att[par] is not None:
                    waits.append(("S", s_att[par]))
                for ci in range(8):
                    tlast = ph.op("tensor", lambda e, ci=ci, par=par: e.matmul(
                        pb[:, par, :], wP[:, par, ci, :],
                        qto[:, ci, :], start=(ci == 0), stop=(ci == 7)),
                        waits=(waits if ci == 0 else []), inc="T" if ci == 7 else None)
                t_wp_cons[par] = tlast
                s_att[par] = ph.op("scalar", lambda e, f=f, par=par: e.activation(
                    attb[:, f, :], pb[:, par, :], AF.Identity,
                    bias=b.s_bp[:, f:f + 1]),
                    waits=[("T", tlast)] + ([("Ld", l_taph)] if (l_taph and f < 2) else []),
                    inc="S")
            l_tapa = 0
            if "d2" in taps and qb == 0:
                l_tapa = ph.tapdma(rearr(b.dbg)[:, :, 1024:1536], attb[:, :, :],
                                   waits=[("S", s_att[0]), ("S", s_att[1])])
            s_sq1 = None
            for c in range(4):
                sqw2 = []
                if c == 0:
                    sqw2 = [("S", s_att[0]), ("S", s_att[1])]
                    for ph_ in range(2):
                        la = v_addg.get((7, ph_, 7))
                        if la is not None:
                            sqw2.append(("V", la))
                s_sq1 = ph.op("scalar", lambda e, c=c: e.activation(
                    b.sq[:, c, :], attb[:, c, :], AF.Square),
                    waits=sqw2, inc="S")
            for c in range(4):
                ph.op("tensor", lambda e, c=c: e.matmul(
                    pb[:, 6, :], b.c_on[:], attb[:, c, :],
                    start=(c == 0), stop=False),
                    waits=([("S", s_att[0]), ("S", s_att[1]), ("V", v_nrm)] if c == 0 else []))
            t_x2a = None
            for c in range(4):
                t_x2a = ph.op("tensor", lambda e, c=c: e.matmul(
                    pb[:, 7, :], b.c_on[:], b.sq[:, c, :], start=(c == 0), stop=False),
                    waits=([("S", s_sq1)] if c == 0 else []), inc="T" if c == 3 else None)
            s_sq2 = None
            for c in range(4, 8):
                s_sq2 = ph.op("scalar", lambda e, c=c: e.activation(
                    b.sq[:, c - 4, :], attb[:, c, :], AF.Square),
                    waits=([("T", t_x2a)] if c == 4 else []), inc="S")
            t_x = None
            for c in range(4, 8):
                t_x = ph.op("tensor", lambda e, c=c: e.matmul(
                    pb[:, 6, :], b.c_on[:], attb[:, c, :], start=False,
                    stop=(c == 7)), inc="T" if c == 7 else None)
            t_x2 = None
            for c in range(4, 8):
                t_x2 = ph.op("tensor", lambda e, c=c: e.matmul(
                    pb[:, 7, :], b.c_on[:], b.sq[:, c - 4, :], start=False,
                    stop=(c == 7)),
                    waits=([("S", s_sq2)] if c == 4 else []), inc="T" if c == 7 else None)
            s_m = ph.op("scalar", lambda e: e.activation(
                b.st2m[:], pb[:, 6, :], AF.Copy, scale=1.0 / 1024),
                waits=[("T", t_x)], inc="S")
            s_m2 = ph.op("scalar", lambda e: e.activation(
                b.sqm[:], b.st2m[:], AF.Square), waits=[("S", s_m)], inc="S")
            v_var = ph.op("vector", lambda e: e.scalar_tensor_tensor(
                b.sqm[:], pb[:, 7, :], 1.0 / 1024, b.sqm[:], ALU.mult, ALU.subtract),
                waits=[("S", s_m2), ("T", t_x2)], inc="V")
            s_sd = ph.op("scalar", lambda e: e.activation(
                b.sqm[:], b.sqm[:], AF.Sqrt, bias=b.s_cf[:, 0:1]), waits=[("V", v_var)], inc="S")
            v_r = ph.op("vector", lambda e: e.reciprocal(b.st2r[:], b.sqm[:]),
                        waits=[("S", s_sd)], inc="V")
            v_mr = ph.op("vector", lambda e: e.tensor_tensor(
                b.sqm[:], b.st2m[:], b.st2r[:], ALU.mult), waits=[("V", v_r)], inc="V")
            ph.op("vector", lambda e: e.tensor_tensor(
                attb[:, :, :], attb[:, :, :], bc(b.st2r[:], DC), ALU.mult),
                waits=[("V", v_r)] + ([("Ld", l_tapa)] if l_tapa else []), inc="V")
            ph.op("vector", lambda e, qc=qc: e.tensor_tensor(
                b.x[:, :, qc + 1:qc + 513], b.x[:, :, qc + 1:qc + 513],
                attb[:, :, :], ALU.add), inc="V")
            v_resid = ph.op("vector", lambda e, qc=qc: e.tensor_tensor(
                b.x[:, :, qc + 1:qc + 513], b.x[:, :, qc + 1:qc + 513],
                bc(b.sqm[:], DC), ALU.subtract), waits=[("V", v_mr)], inc="V")
        ph.run()
        d2stack.close()


def phase_h2(prog, b, stack, pre=None):
    nc = prog.nc
    with contextlib.ExitStack() as hstack:
        pb = hstack.enter_context(nc.psum_tensor("psH2", [128, 2, QB], F32))
        ph = Phase(prog, "H2")
        if pre:
            pre(ph)
        l_sv = {}
        prevq = {"sync": prog.cnt["Ls"], "gpsimd": prog.cnt["Lg"]}
        for qb in range(NQB):
            qc = qb * QB
            eng = "sync" if qb % 2 == 0 else "gpsimd"
            sem = "Ls" if eng == "sync" else "Lg"
            l_sv[qb] = (sem, ph.dma(eng, rearr(b.xsave)[:, :, qc:qc + QB],
                                    b.x[:, :, qc + 1:qc + 513],
                                    waits=[(sem, prevq[eng])]))
            prevq[eng] = l_sv[qb][1]
        vws = []
        for qb in range(NQB):
            vw = ph.op("vector", lambda e, qb=qb: e.tensor_copy(
                b.cast[:, :, 0:513], b.x[:, :, qb * QB:qb * QB + 513]), inc="V")
            v_r = emit_stats_qb(ph, b, pb, qb,
                                lambda c: b.cast[:, c, 1:513],
                                b.stm[:, qb * QB:(qb + 1) * QB],
                                b.str_[:, qb * QB:(qb + 1) * QB], vw)
            vws.append(v_r)
        for qb in range(NQB):
            qc = qb * QB
            ph.op("vector", lambda e, qc=qc, qb=qb: e.tensor_tensor(
                b.x[:, :, qc + 1:qc + 513], b.x[:, :, qc + 1:qc + 513],
                bc(b.stm[:, qc:qc + QB], DC), ALU.subtract),
                waits=[l_sv[qb], ("V", vws[qb])], inc="V")
            ph.op("vector", lambda e, qc=qc: e.tensor_tensor(
                b.x[:, :, qc + 1:qc + 513], b.x[:, :, qc + 1:qc + 513],
                bc(b.str_[:, qc:qc + QB], DC), ALU.mult), inc="V")
        ph.run()


def phase_inner_attn(prog, b, stack, l, wkv=None, l_wkv_pre=None):
    nc = prog.nc
    with contextlib.ExitStack() as astack:
        sb = lambda name, shape, dt: astack.enter_context(nc.sbuf_tensor(name + str(l), list(shape), dt))
        r_csq = sb("r_csq", [128, NT], BF16)
        r_snq = sb("r_snq", [128, NT], BF16)
        r_csk = sb("r_csk", [128, NT], BF16)
        r_snk = sb("r_snk", [128, NT], BF16)
        kb = sb("kb", [128, 4, NT], BF16)
        vTMl = sb("vTMl", [128, 16, 8, 66], BF16)
        qv2 = sb("qv2", [128, 2, 4, QB], BF16)
        wosb = sb("wosb", [128, 4, 1024], BF16)
        v_sb = sb("v_sbA", [128, QB], BF16)
        u1b = sb("u1b", [128, 2, QB], BF16)
        u2b = sb("u2b", [128, 2, QB], BF16)
        pt = sb("PTA", [128, 8, QB], BF16)
        rnb = sb("rnb", [8, 2, QB], BF16)
        lsum = b.sqm[0:8, :]
        lsums = sb("lsumsA", [128, 4, QB], BF16)
        lsumb = b.st2m[0:8, :]
        xbnd = sb("xbndA", [128, DC, 4], F32)
        pb = astack.enter_context(nc.psum_tensor("psA" + str(l), [128, 8, QB], F32))
        pl = pb[0:8, 5, :]

        ph = Phase(prog, "A%d" % l)
        for dst, src in ((r_csq, b.csq), (r_snq, b.snq), (r_csk, b.csk), (r_snk, b.snk)):
            l_rc = ph.dma("sync", dst[:], src[:, :], sem="Lo")
        if wkv is None:
            wkv = astack.enter_context(nc.sbuf_tensor("wkvA" + str(l), [128, DC, 1536], BF16))
            l_wkv_pre = ph.dma("gpsimd", wkv[:],
                               b.wqkv[l, :, :].rearrange("(c p) f -> p c f", p=128),
                               waits=[("Lg", prog.cnt["Lg"])])
        l_wo = ph.dma("gpsimd", wosb[:], b.wo[l, :, :].rearrange("(c p) f -> p c f", p=128),
                      waits=[("Lg", l_wkv_pre)])
        l_wkv = l_wkv_pre
        v_ones = ph.op("vector", lambda e: e.memset(vTMl[:, :, :, 64:65], 1.0), inc="V")
        ph.entry["vector"] = [("Lo", l_rc)]
        v_bnd = ph.op("vector", lambda e: e.tensor_copy(
            xbnd[:, :, 0:3], b.x[:, :, 512:1537:512]), inc="V")

        st = {
            "s_prep": {0: None, 1: None},
            "v_tr": {0: None, 1: None},
            "s_ln": {},
            "t_pl": {},
            "s_pl_read": 0,
            "rb": {2: None, 3: None},
            "t_cast": 0,
            "l_lc": 0,
            "t_rn_prev": 0,
            "vw": {},
        }

        def emit_cast(qb):
            qc = qb * QB
            vw = ph.op("vector", lambda e, qb=qb: e.tensor_copy(
                b.cast[:, :, 0:513], b.x[:, :, qb * QB:qb * QB + 513]),
                waits=([("T", st["t_cast"])] if qb else []), inc="V")
            if qb > 0:
                vw = ph.op("vector", lambda e, qb=qb: e.tensor_copy(
                    b.cast[:, :, 0:1], xbnd[:, :, qb - 1:qb]), inc="V")
            st["vw"][qb] = vw

        def emit_co_group(qb, co):
            qc = qb * QB
            grp = 0 if co < 4 else 1
            c4 = co % 4
            waits = [("V", st["vw"][qb])]
            if qb == 0 and co == 0:
                waits.append(("Lg", l_wkv))
            if st["s_prep"][0] is not None:
                waits.append(("S", st["s_prep"][0]))
            tlast = None
            for ci in range(8):
                sl = (b.cast[:, ci, 1:513] if ci < 4 else b.cast[:, ci, 0:512])
                tlast = ph.op("tensor", lambda e, ci=ci, co=co, sl=sl: e.matmul(
                    pb[:, 4, :], wkv[:, ci, co * 128:co * 128 + 128], sl,
                    start=(ci == 0), stop=(ci == 7)),
                    waits=(waits if ci == 0 else []), inc="T" if ci == 7 else None)
            st["t_cast"] = tlast
            dst = (qv2[:, qb % 2, c4, :] if grp == 0 else kb[:, c4, qc:qc + QB])
            s_ev = ph.op("scalar", lambda e, dst=dst: e.activation(
                dst, pb[:, 4, :], AF.Copy), waits=[("T", tlast)], inc="S")
            sq_w = []
            if co == 0 and st["l_lc"]:
                sq_w.append(("Lx", st["l_lc"]))
            s_sq = ph.op("scalar", lambda e, c4=c4: e.activation(
                b.sq[:, c4, :], pb[:, 4, :], AF.Square), waits=sq_w, inc="S")
            st["s_prep"][0] = s_sq
            kw = [("S", s_sq)]
            if c4 == 0:
                kw.append(("S", st["s_pl_read"]))
            t_pl = ph.op("tensor", lambda e, c4=c4: e.matmul(
                pl, b.c_el2[:, c4 * 8:c4 * 8 + 8], b.sq[:, c4, :],
                start=(c4 == 0), stop=(c4 == 3)), waits=kw, inc="T")
            if c4 == 3:
                st["t_pl"][(qb, grp)] = t_pl
                s1 = ph.op("scalar", lambda e: e.activation(
                    lsum, pl, AF.Ln, bias=b.s_cf[0:8, 1:2]),
                    waits=[("T", t_pl)], inc="S")
                st["s_pl_read"] = s1
                st["s_ln"][(qb, grp)] = ph.op("scalar", lambda e, grp=grp: e.activation(
                    rnb[:, grp, :], lsum, AF.Exp, scale=-0.5),
                    waits=[("S", s1)], inc="S")

        def emit_post(qb, grp):
            qc = qb * QB
            scl = b.s_qs if grp == 0 else b.s_ks
            cs_ = r_csq if grp == 0 else r_csk
            sn_ = r_snq if grp == 0 else r_snk
            v_pr_last = None
            for c2 in range(4):
                rbank = 2 + (c2 % 2)
                wts2 = [("S", st["s_ln"][(qb, grp)])]
                if st["rb"][rbank] is not None:
                    wts2.append(("V", st["rb"][rbank]))
                t_rn = ph.op("tensor", lambda e, c2=c2, rbank=rbank, grp=grp: e.matmul(
                    pb[:, rbank, :], b.c_sel[0:8, c2 * 128:c2 * 128 + 128],
                    rnb[:, grp, :], start=True, stop=True), waits=wts2, inc="T")
                if grp == 0:
                    dst2 = qv2[:, qb % 2, c2, :]
                else:
                    dst2 = kb[:, c2, qc:qc + QB]
                v_a = ph.op("vector", lambda e, dst2=dst2, scl=scl, rbank=rbank: e.scalar_tensor_tensor(
                    dst2, dst2, scl[:, l:l + 1], pb[:, rbank, :],
                    ALU.mult, ALU.mult), waits=[("T", t_rn)], inc="V")
                g_u1 = ph.op("gpsimd", lambda e, dst2=dst2, cs_=cs_, c2=c2 % 2, qc=qc: e.tensor_tensor(
                    u1b[:, c2, :], dst2, cs_[:, qc:qc + QB], ALU.mult),
                    waits=[("V", v_a), ("Lo", l_rc)], inc="G")
                g_u2 = ph.op("gpsimd", lambda e, dst2=dst2, sn_=sn_, c2=c2 % 2, qc=qc: e.tensor_tensor(
                    u2b[:, c2, :], dst2, sn_[:, qc:qc + QB], ALU.mult),
                    inc="G")
                t_rt = ph.op("tensor", lambda e, c2=c2 % 2, rbank=rbank: e.matmul(
                    pb[:, rbank, :], b.c_rr[:], u2b[:, c2, :], start=True, stop=True),
                    waits=[("G", g_u2)], inc="T")
                v_pr_last = ph.op("vector", lambda e, dst2=dst2, c2=c2 % 2, rbank=rbank: e.tensor_tensor(
                    dst2, u1b[:, c2, :], pb[:, rbank, :], ALU.add),
                    waits=[("T", t_rt), ("G", g_u1)], inc="V")
                st["rb"][rbank] = v_pr_last
            return v_pr_last

        def emit_v(qb):
            qc = qb * QB
            for c4 in range(4):
                co = 8 + c4
                waits = [("V", st["vw"][qb])]
                if st["s_prep"][0] is not None:
                    waits.append(("S", st["s_prep"][0]))
                tlast = None
                for ci in range(8):
                    sl = (b.cast[:, ci, 1:513] if ci < 4 else b.cast[:, ci, 0:512])
                    tlast = ph.op("tensor", lambda e, ci=ci, co=co, sl=sl: e.matmul(
                        pb[:, 4, :], wkv[:, ci, co * 128:co * 128 + 128], sl,
                        start=(ci == 0), stop=(ci == 7)),
                        waits=(waits if ci == 0 else []), inc="T" if ci == 7 else None)
                st["t_cast"] = tlast
                s_ev = ph.op("scalar", lambda e: e.activation(
                    v_sb[:], pb[:, 4, :], AF.Copy), waits=[("T", tlast)], inc="S")
                st["s_prep"][0] = s_ev
                for tt in range(4):
                    slot = tt % 2
                    twaits = [("S", s_ev)]
                    if qb == 0 and c4 == 0 and tt == 0:
                        twaits.append(("V", v_ones))
                    if st["v_tr"][slot] is not None:
                        twaits.append(("V", st["v_tr"][slot]))
                    t_tr = ph.op("tensor", lambda e, tt=tt, slot=slot: e.transpose(
                        pb[:, 6 + slot, 0:64].bitcast(BF16),
                        v_sb[:, tt * 128:tt * 128 + 128], b.c_id[:]),
                        waits=twaits, inc="T")
                    st["v_tr"][slot] = ph.op("vector", lambda e, qb=qb, tt=tt, c4=c4, slot=slot: e.tensor_copy(
                        vTMl[:, qb * 4 + tt, 2 * c4:2 * c4 + 2, 0:64],
                        pb[:, 6 + slot, 0:64].bitcast(BF16)),
                        waits=[("T", t_tr)], inc="V")

        emit_cast(0)
        for co in range(8):
            emit_co_group(0, co)
        v_q0 = emit_post(0, 0)
        v_k0 = emit_post(0, 1)
        emit_v(0)
        prep_done = {0: (v_q0, v_k0)}

        s_wo = {0: None, 1: None}
        for qb in range(NQB):
            qc = qb * QB
            qpar = qb % 2
            kts = []
            if qb > 0:
                kts += [("prev", j, 4 * (qb - 1) + j) for j in range(4)]
            kts += [("self", j, 4 * qb + j) for j in range(4)]
            nk = len(kts)
            ngr = nk // 2
            if qb < NQB - 1:
                emit_cast(qb + 1)
            inter = [[] for _ in range(4)]
            if qb < NQB - 1:
                for co in range(8):
                    inter[co // 2].append(co)
            v_oev = None
            v_last_pair = {}
            s_expg = {}
            t_av = {i: 0 for i in range(8)}
            v_q_done, v_k_done = prep_done[qb]
            v_rot_done = [("V", v_q_done), ("V", v_k_done),
                          ("V", st["v_tr"][0]), ("V", st["v_tr"][1])]
            for par in range(2):
                if s_wo[par] is not None:
                    v_rot_done.append(("V", s_wo[par]))
            for c in range(4):
                t_avl = {}
                for kg in range(ngr + 1):
                    if kg < ngr:
                        tsm = {}
                        for par_h in range(2):
                            for j2 in range(2):
                                ji = 2 * kg + j2
                                side, j, kt = kts[ji]
                                bank = par_h * 2 + j2
                                r0 = 64 * par_h
                                waits = []
                                if c == 0 and kg == 0 and par_h == 0 and j2 == 0:
                                    waits += v_rot_done
                                prev_exp = s_expg.get((kg - 1, par_h, c)) or s_expg.get((ngr - 1, par_h, c - 1))
                                if prev_exp is not None:
                                    waits.append(("S", prev_exp))
                                ph.op("tensor", lambda e, c=c, kt=kt, r0=r0, bank=bank, qpar=qpar: e.matmul(
                                    pb[:, bank, :], kb[r0:r0 + 64, c, kt * 128:kt * 128 + 128],
                                    qv2[r0:r0 + 64, qpar, c, :], start=True, stop=False,
                                    tile_position=(r0, 0)), waits=waits)
                                msk = b.c_ms if side == "self" else b.c_mp
                                if side == "self":
                                    z0, z1 = 0, 128 * j
                                else:
                                    z0, z1 = 128 * j + 128, 512
                                has_span = z1 > z0
                                ph.op("tensor", lambda e, bank=bank, j=j, msk=msk, stp=not has_span: e.matmul(
                                    pb[:, bank, 128 * j:128 * j + 128], msk[:], b.c_id[:],
                                    start=False, stop=stp), inc=None if has_span else "T")
                                if has_span:
                                    tmm = ph.op("tensor", lambda e, bank=bank, z0=z0, z1=z1: e.matmul(
                                        pb[:, bank, z0:z1], b.c_on[0:1, 0:128],
                                        b.c_ng[0:1, 0:z1 - z0], start=False, stop=True),
                                        inc="T")
                                else:
                                    tmm = prog.cnt["T"]
                                tsm[(par_h, j2)] = tmm
                        for par_h in range(2):
                            ptbase = (kg % 2) * 4 + par_h * 2
                            wts = [("T", tsm[(par_h, 0)]), ("T", tsm[(par_h, 1)])]
                            if t_av[ptbase] or t_av[ptbase + 1]:
                                wts.append(("T", max(t_av[ptbase], t_av[ptbase + 1])))
                            s_expg[(kg, par_h, c)] = ph.op("scalar", lambda e, par_h=par_h, ptbase=ptbase: e.activation(
                                pt[:, ptbase:ptbase + 2, :], pb[:, par_h * 2:par_h * 2 + 2, :],
                                AF.Exp, bias=b.s_cf[:, 2:3]), waits=wts, inc="S")
                    if kg >= 1:
                        kga = kg - 1
                        for par_h in range(2):
                            for j2 in range(2):
                                ka = 2 * kga + j2
                                kt = kts[ka][2]
                                slot = (kga % 2) * 4 + par_h * 2 + j2
                                obank = 6 + par_h
                                avw = [("S", s_expg[(kga, par_h, c)])]
                                if ka == 0:
                                    if v_last_pair.get(c - 1) is not None:
                                        avw.append(("V", v_last_pair[c - 1]))
                                    elif qb and c == 0:
                                        avw.append(("V", st["v_tr"][par_h]))
                                t_av[slot] = ph.op("tensor", lambda e, c=c, kt=kt, slot=slot, obank=obank, par_h=par_h, ka=ka, nk=nk, qpar=qpar: e.matmul(
                                    pb[0:65, obank, :], vTMl[:, kt, 2 * c + par_h, 0:65],
                                    pt[:, slot, :], start=(ka == 0), stop=(ka == nk - 1)),
                                    waits=avw, inc="T")
                                t_avl[par_h] = t_av[slot]
                h0, h1 = 2 * c, 2 * c + 1
                ph.op("vector", lambda e, h=h0: e.tensor_copy(
                    lsums[32 * (h // 4):32 * (h // 4) + 1, h % 4, :], pb[64:65, 6, :]),
                    waits=[("T", t_avl[0])], inc="V")
                ph.op("vector", lambda e, h=h1: e.tensor_copy(
                    lsums[32 * (h // 4):32 * (h // 4) + 1, h % 4, :], pb[64:65, 7, :]),
                    waits=[("T", t_avl[1])], inc="V")
                ph.op("vector", lambda e, c=c, qpar=qpar: e.tensor_copy(
                    qv2[0:64, qpar, c, :], pb[0:64, 6, :]), inc="V")
                v_oev = ph.op("vector", lambda e, c=c, qpar=qpar: e.tensor_copy(
                    qv2[64:128, qpar, c, :], pb[0:64, 7, :]), inc="V")
                v_last_pair[c] = v_oev
                for co in inter[c]:
                    emit_co_group(qb + 1, co)
            l_lc = None
            for g in range(2):
                gw = [("V", v_oev)]
                if g == 0 and st["t_rn_prev"]:
                    gw.append(("T", st["t_rn_prev"]))
                l_lc = ph.dma("sync", lsumb[4 * g:4 * g + 4, :],
                              lsums[32 * g:32 * g + 1, :, :], waits=gw, sem="Lx")
            st["l_lc"] = l_lc
            v_rcl = ph.op("vector", lambda e: e.reciprocal(lsumb, lsumb),
                          waits=[("Lx", l_lc)], inc="V")
            v_nrm = None
            for c in range(4):
                wts = [("V", v_rcl)]
                if v_nrm is not None:
                    wts.append(("V", v_nrm))
                if c == 0 and st["rb"][2] is not None:
                    wts.append(("V", st["rb"][2]))
                t_rn = ph.op("tensor", lambda e, c=c: e.matmul(
                    pb[:, 2, :], b.c_sel[0:8, c * 128:c * 128 + 128], lsumb,
                    start=True, stop=True), waits=wts, inc="T")
                v_nrm = ph.op("vector", lambda e, c=c, qpar=qpar: e.tensor_tensor(
                    qv2[:, qpar, c, :], qv2[:, qpar, c, :], pb[:, 2, :], ALU.mult),
                    waits=[("T", t_rn)], inc="V")
            st["t_rn_prev"] = t_rn
            st["rb"][2] = v_nrm
            for f in range(8):
                par = f % 2
                tlast = None
                waits = [("V", v_nrm)]
                if f == 0 and qb == 0:
                    waits.append(("Lg", l_wo))
                if f <= 1:
                    le = s_expg.get((ngr - 1, par, 3))
                    if le is not None:
                        waits.append(("S", le))
                if s_wo[par] is not None:
                    waits.append(("V", s_wo[par]))
                for ci in range(4):
                    tlast = ph.op("tensor", lambda e, ci=ci, par=par, f=f, qpar=qpar: e.matmul(
                        pb[:, par, :], wosb[:, ci, f * 128:f * 128 + 128],
                        qv2[:, qpar, ci, :], start=(ci == 0), stop=(ci == 3)),
                        waits=(waits if ci == 0 else []), inc="T" if ci == 3 else None)
                s_wo[par] = ph.op("vector", lambda e, f=f, par=par, qc=qc: e.tensor_tensor(
                    b.x[:, f, qc + 1:qc + 513], b.x[:, f, qc + 1:qc + 513],
                    pb[:, par, :], ALU.add), waits=[("T", tlast)], inc="V")
            if qb < NQB - 1:
                v_q = emit_post(qb + 1, 0)
                v_k = emit_post(qb + 1, 1)
                emit_v(qb + 1)
                prep_done[qb + 1] = (v_q, v_k)
        ph.run()


def phase_inner_ffn(prog, b, stack, l, pre=None):
    nc = prog.nc
    with contextlib.ExitStack() as fstack:
        sb = lambda name, shape, dt: fstack.enter_context(nc.sbuf_tensor(name + str(l), list(shape), dt))
        xh = sb("xhat", [128, DC, 1024], BF16)
        G = sb("Gbuf", [128, NJ, 1024], BF16)
        w1b = sb("w1b", [128, 2, DC, 256], BF16)
        w2b = sb("w2b", [128, 2, NJ, 128], BF16)
        gel = sb("gel", [128, 2, QB], BF16)
        xbndf = sb("xbndf", [128, DC, 2], F32)
        pb = fstack.enter_context(nc.psum_tensor("psF" + str(l), [128, 8, QB], F32))

        for hf in range(2):
            ph = Phase(prog, "F%d_%d" % (l, hf))
            if hf == 1 and pre:
                pre(ph)
            if hf == 0:
                ph.op("vector", lambda e: e.tensor_copy(
                    xbndf[:, :, 0:1], b.x[:, :, 1024:1025]), inc="V")
            v_app = None
            for sg in range(2):
                qb = 2 * hf + sg
                qc = qb * QB
                vw = ph.op("vector", lambda e, qb=qb: e.tensor_copy(
                    b.cast[:, :, 0:513], b.x[:, :, qb * QB:qb * QB + 513]),
                    waits=([("V", v_app)] if v_app else []), inc="V")
                if hf == 1 and sg == 0:
                    vw = ph.op("vector", lambda e: e.tensor_copy(
                        b.cast[:, :, 0:1], xbndf[:, :, 0:1]), inc="V")
                def src(c, qb=qb):
                    return (b.cast[:, c, 1:513] if c < 4 else b.cast[:, c, 0:512])
                v_r = emit_stats_qb(ph, b, pb, qb, src,
                                    b.stm[:, qc:qc + QB], b.str_[:, qc:qc + QB], vw)
                o = sg * QB
                v_a = ph.op("vector", lambda e, o=o, qc=qc: e.tensor_tensor(
                    xh[:, 0:4, o:o + QB], b.cast[:, 0:4, 1:513],
                    bc(b.stm[:, qc:qc + QB], 4), ALU.subtract),
                    waits=[("V", v_r)], inc="V")
                v_a = ph.op("vector", lambda e, o=o, qc=qc: e.tensor_tensor(
                    xh[:, 4:8, o:o + QB], b.cast[:, 4:8, 0:512],
                    bc(b.stm[:, qc:qc + QB], 4), ALU.subtract), inc="V")
                v_a = ph.op("vector", lambda e, o=o, qc=qc: e.tensor_tensor(
                    xh[:, :, o:o + QB], xh[:, :, o:o + QB],
                    bc(b.str_[:, qc:qc + QB], DC), ALU.mult), inc="V")
                for c in range(DC):
                    v_a = ph.op("vector", lambda e, c=c, o=o: e.tensor_scalar(
                        xh[:, c, o:o + QB], xh[:, c, o:o + QB],
                        b.s_lng[:, l, c:c + 1], b.s_lnb[:, l, c:c + 1],
                        ALU.mult, ALU.add), inc="V")
                v_app = v_a
            v_g = {0: None, 1: None}
            t_wcons = {0: 0, 1: 0}
            l_w1 = {}
            for j in range(NJ):
                jp = j % 2
                wts = []
                if t_wcons[jp]:
                    wts.append(("T", t_wcons[jp]))
                l_w1[j] = ph.dma("gpsimd", w1b[:, jp, :, :],
                                 b.w1i[l, :, 256 * j:256 * j + 256].rearrange(
                                     "(c p) f -> p c f", p=128), waits=wts,
                                 sem="Lw%d" % jp)
                for sg in range(2):
                    o = sg * QB
                    bA = sg
                    bG = 2 + sg
                    waits = [("Lw%d" % jp, l_w1[j]), ("V", v_app)]
                    if v_g[sg] is not None:
                        waits = [("Lw%d" % jp, l_w1[j]), ("V", v_g[sg])]
                    tl_a = None
                    for ci in range(8):
                        tl_a = ph.op("tensor", lambda e, ci=ci, jp=jp, bA=bA, o=o: e.matmul(
                            pb[:, bA, :], w1b[:, jp, ci, 0:128], xh[:, ci, o:o + QB],
                            start=(ci == 0), stop=(ci == 7)),
                            waits=(waits if ci == 0 else []), inc="T" if ci == 7 else None)
                    tl_g = None
                    for ci in range(8):
                        tl_g = ph.op("tensor", lambda e, ci=ci, jp=jp, bG=bG, o=o: e.matmul(
                            pb[:, bG, :], w1b[:, jp, ci, 128:256], xh[:, ci, o:o + QB],
                            start=(ci == 0), stop=(ci == 7)), inc="T" if ci == 7 else None)
                    t_wcons[jp] = tl_g
                    gw = [("T", tl_g)]
                    if v_g[sg] is not None:
                        gw.append(("V", v_g[sg]))
                    s_g = ph.op("scalar", lambda e, bG=bG, j=j, sg=sg: e.activation(
                        gel[:, sg, :], pb[:, bG, :], AF.Gelu, bias=b.s_b1i[:, l, 2 * j + 1:2 * j + 2]),
                        waits=gw, inc="S")
                    v_g[sg] = ph.op("vector", lambda e, bA=bA, j=j, o=o, s_g=s_g, sg=sg: e.scalar_tensor_tensor(
                        G[:, j, o:o + QB], pb[:, bA, :], b.s_b1i[:, l, 2 * j:2 * j + 1],
                        gel[:, sg, :], ALU.add, ALU.mult),
                        waits=[("S", s_g), ("T", tl_a)], inc="V")
            v_u = {0: None, 1: None, 2: None, 3: None}
            t_w2c = {0: 0, 1: 0}
            for f in range(8):
                fp = f % 2
                wts = []
                if t_w2c[fp]:
                    wts.append(("T", t_w2c[fp]))
                elif t_wcons[fp]:
                    wts.append(("T", t_wcons[fp]))
                l_w2 = ph.dma("gpsimd", w2b[:, fp, :, :],
                              b.w2p[l, :, 128 * f:128 * f + 128].rearrange(
                                  "(c p) f -> p c f", p=128), waits=wts,
                              sem="Lw%d" % fp)
                for sg in range(2):
                    o = sg * QB
                    bank = 4 + 2 * fp + sg
                    tl = None
                    waits = [("Lw%d" % fp, l_w2), ("V", v_g[0]), ("V", v_g[1])]
                    if v_u[bank - 4] is not None:
                        waits.append(("V", v_u[bank - 4]))
                    for j in range(NJ):
                        tl = ph.op("tensor", lambda e, j=j, fp=fp, bank=bank, o=o: e.matmul(
                            pb[:, bank, :], w2b[:, fp, j, :], G[:, j, o:o + QB],
                            start=(j == 0), stop=(j == NJ - 1)),
                            waits=(waits if j == 0 else []), inc="T" if j == NJ - 1 else None)
                    t_w2c[fp] = tl
                    qc = (2 * hf + sg) * QB
                    v_u[bank - 4] = ph.op("vector", lambda e, f=f, bank=bank, qc=qc: e.scalar_tensor_tensor(
                        b.x[:, f, qc + 1:qc + 513], pb[:, bank, :],
                        b.s_b2[:, l, f:f + 1], b.x[:, f, qc + 1:qc + 513],
                        ALU.add, ALU.add), waits=[("T", tl)], inc="V")
            ph.run()


def phase_out(prog, b, stack):
    nc = prog.nc
    with contextlib.ExitStack() as ostack:
        sb = lambda name, shape, dt: ostack.enter_context(nc.sbuf_tensor(name, list(shape), dt))
        xo = sb("xo", [128, 2, DC, QB], F32)
        tmp = sb("tmpo", [128, 2, DC, QB], F32)
        pb = ostack.enter_context(nc.psum_tensor("psO", [128, 2, QB], F32))
        ph = Phase(prog, "OUT")
        vws = []
        for qb in range(NQB):
            vw = ph.op("vector", lambda e, qb=qb: e.tensor_copy(
                b.cast[:, :, 0:513], b.x[:, :, qb * QB:qb * QB + 513]), inc="V")
            v_r = emit_stats_qb(ph, b, pb, qb,
                                lambda c: b.cast[:, c, 1:513],
                                b.stm[:, qb * QB:(qb + 1) * QB],
                                b.str_[:, qb * QB:(qb + 1) * QB], vw)
            vws.append(v_r)
        v_ds = {}
        st_st = {}
        l_prev = 0
        for qb in range(NQB):
            qc = qb * QB
            sl = qb % 2
            ssem = "Lx" if sl == 0 else "Ld"
            lw = [("Lo", l_prev)] if l_prev else []
            if qb >= 2:
                lw.append(("V", v_ds[qb - 2]))
            l_xo = ph.dma("sync", xo[:, sl], rearr(b.xsave)[:, :, qc:qc + QB],
                          waits=lw, sem="Lo")
            l_prev = l_xo
            wts = [("V", vws[qb])]
            if qb >= 2:
                wts.append((ssem, st_st[qb - 2]))
            ph.op("vector", lambda e, qc=qc, sl=sl: e.tensor_tensor(
                tmp[:, sl], b.x[:, :, qc + 1:qc + 513], bc(b.stm[:, qc:qc + QB], DC),
                ALU.subtract), waits=wts, inc="V")
            ph.op("vector", lambda e, qc=qc, sl=sl: e.tensor_tensor(
                tmp[:, sl], tmp[:, sl], bc(b.str_[:, qc:qc + QB], DC), ALU.mult), inc="V")
            v_ds[qb] = ph.op("vector", lambda e, sl=sl: e.tensor_tensor(
                tmp[:, sl], tmp[:, sl], xo[:, sl], ALU.add), waits=[("Lo", l_xo)], inc="V")
            st_st[qb] = ph.dma("sync", rearr(b.yT)[:, :, qc:qc + QB], tmp[:, sl],
                               waits=[("V", v_ds[qb])], sem=ssem)
        ph.op("sync", lambda e: e.nop(),
              waits=[("Lx", prog.cnt["Lx"]), ("Ld", prog.cnt["Ld"])])
        ph.run()


def tap_x(prog, b, name="tapx"):
    ph = Phase(prog, name)
    ph.dma("sync", rearr(b.yT), b.x[:, :, 1:NT + 1])
    ph.op("sync", lambda e: e.nop(), waits=[("Ls", prog.cnt["Ls"])])
    ph.run()


def build(stop=None, taps=()):
    nc = bass.Bass("TRN2", target_bir_lowering=False)
    with contextlib.ExitStack() as stack:
        b = declare_dram(nc, stop)
        prog = Prog(nc, stack)
        alloc_globals(nc, stack, b)
        phase_load(prog, b)
        if stop == "load":
            tap_x(prog, b)
            return nc
        phase_dense(prog, b, stack, taps, stop)
        if stop in ("d0", "d1"):
            return nc
        if "kT" in taps:
            return nc
        if stop == "dense":
            tap_x(prog, b)
            return nc
        pf = {}

        def mk_pre(l):
            def pre(ph):
                pf[l] = ph.dma("gpsimd", pf[("buf", l)][:],
                               b.wqkv[l, :, :].rearrange("(c p) f -> p c f", p=128),
                               waits=[("Lg", prog.cnt["Lg"])])
            return pre

        with contextlib.ExitStack() as s_a0:
            pf[("buf", 0)] = s_a0.enter_context(
                nc.sbuf_tensor("wkvA0", [128, DC, 1536], BF16))
            phase_h2(prog, b, stack, pre=mk_pre(0))
            if stop == "h2":
                tap_x(prog, b)
                return nc
            phase_inner_attn(prog, b, stack, 0, wkv=pf[("buf", 0)], l_wkv_pre=pf[0])
        if stop == "attn0":
            tap_x(prog, b)
            return nc
        with contextlib.ExitStack() as s_a1:
            pf[("buf", 1)] = s_a1.enter_context(
                nc.sbuf_tensor("wkvA1", [128, DC, 1536], BF16))
            phase_inner_ffn(prog, b, stack, 0, pre=mk_pre(1))
            if stop == "ffn0":
                tap_x(prog, b)
                return nc
            phase_inner_attn(prog, b, stack, 1, wkv=pf[("buf", 1)], l_wkv_pre=pf[1])
        if stop == "attn1":
            tap_x(prog, b)
            return nc
        phase_inner_ffn(prog, b, stack, 1)
        if stop == "ffn1":
            tap_x(prog, b)
            return nc
        phase_out(prog, b, stack)
    return nc


def make_in_maps(I, n_cores=8):
    shared = host_prep(I)
    x = np.asarray(I["L0_attn_x"])
    maps = []
    for c in range(n_cores):
        m = dict(shared)
        m["xT"] = np.ascontiguousarray(x[c].T)
        maps.append(m)
    return maps


def kernel(**I):
    nc = build(stop=None, taps=())
    maps = make_in_maps(I, n_cores=8)
    res = run_bass_kernel_spmd(nc, maps, core_ids=list(range(8)))
    out = np.stack([np.asarray(r["yT"]).T for r in res.results])
    return np.ascontiguousarray(out, dtype=np.float32)


# version 21
# speedup vs baseline: 1.0135x; 1.0135x over previous
import contextlib
import numpy as np
import ml_dtypes

BFNP = ml_dtypes.bfloat16

D1LITE = ""

import concourse.bass as bass
import concourse.mybir as mybir
from concourse.bass_utils import run_bass_kernel_spmd

F32 = mybir.dt.float32
BF16 = mybir.dt.bfloat16
AF = mybir.ActivationFunctionType
ALU = mybir.AluOpType

NT = 2048
QB = 512
NQB = NT // QB
DC = 8
XW = 2056
HID = 2730
HIDP = 2816
NJ = HIDP // 128
EPS = 1e-5
C_DENSE = 10.0
C_LOC = 8.0
MBIG = -30000.0

ENGINES = ("tensor", "scalar", "vector", "gpsimd", "sync")


class Prog:
    def __init__(self, nc, stack):
        self.nc = nc
        self.stack = stack
        self.sems = {}
        self.cnt = {}
        for s in ("Lg", "Ls", "Lw0", "Lw1", "Lx", "Lo", "Ld", "T", "S", "V", "G"):
            self.sems[s] = stack.enter_context(nc.semaphore(s))
            self.cnt[s] = 0


class Phase:
    def __init__(self, prog, name):
        self.prog = prog
        self.name = name
        self.ops = {e: [] for e in ENGINES}
        self.entry = {e: [] for e in ENGINES}

    def op(self, eng, fn, waits=(), inc=None, n=1):
        waits = list(waits)
        if eng == "vector":
            if inc is None:
                inc = "V"
            waits.append(("V", self.prog.cnt["V"]))
        elif eng == "scalar":
            if inc is None:
                inc = "S"
            waits.append(("S", self.prog.cnt["S"]))
        self.ops[eng].append((waits, fn, inc, n))
        if inc is not None:
            self.prog.cnt[inc] += n
            return self.prog.cnt[inc]
        return None

    def dma(self, eng, out, in_, waits=(), sem=None):
        if sem is None:
            sem = "Lg" if eng == "gpsimd" else "Ls"
        return self.op(eng, lambda e, o=out, i=in_: e.dma_start(out=o, in_=i),
                       waits, sem, 16)

    def tapdma(self, out, in_, waits=()):
        w = list(waits) + [("Ld", self.prog.cnt["Ld"])]
        return self.op("gpsimd", lambda e, o=out, i=in_: e.dma_start(out=o, in_=i),
                       w, "Ld", 16)

    def run(self):
        nc = self.prog.nc
        prog = self.prog
        with nc.Block() as blk:
            for eng in ENGINES:
                ops = self.ops[eng]
                if not ops:
                    continue

                def body(e, ops=ops, ent=self.entry[eng]):
                    with nc.allow_low_precision(reason="bf16 storage by design; fp32 accum"):
                        for s, v in ent:
                            e.wait_ge(prog.sems[s], v)
                        for waits, fn, inc, n in ops:
                            for s, v in waits:
                                e.wait_ge(prog.sems[s], v)
                            instr = fn(e)
                            if inc is not None:
                                instr.then_inc(prog.sems[inc], n)
                getattr(blk, eng)(body)


def bc(ap2d, mid):
    return ap2d.unsqueeze(1).broadcast_to([ap2d.shape[0], mid, ap2d.shape[1]])



def host_prep(I):
    wdt = np.asarray(I["L0_attn_x"]).dtype
    f8 = np.float64

    def cast(a):
        return np.ascontiguousarray(np.asarray(a, dtype=wdt))

    qkv_w = np.asarray(I["L0_attn_qkv_w"])
    out = {}
    out["wqT"] = cast(qkv_w[0:1024].T)
    out["wkT"] = cast(qkv_w[1024:2048].T)
    out["wvT"] = cast(qkv_w[2048:3072].T)
    out["projT"] = cast(np.asarray(I["L0_attn_proj_w"]).T)
    qkv_b = np.asarray(I["L0_attn_qkv_b"])
    out["bq"] = cast(qkv_b[0:1024].reshape(8, 128).T)
    out["bk"] = cast(qkv_b[1024:2048].reshape(8, 128).T)
    out["bv"] = cast(qkv_b[2048:3072].reshape(8, 128).T)
    out["bproj"] = cast(np.asarray(I["L0_attn_proj_b"]).reshape(8, 128).T)

    out["wqkv"] = cast(I["L0_ffn_wqkv"])
    out["wo"] = cast(I["L0_ffn_wo"])
    out["qs"] = cast(np.tile(np.asarray(I["L0_ffn_q_scale"]), (1, 2)).T)
    out["ks"] = cast(np.tile(np.asarray(I["L0_ffn_k_scale"]), (1, 2)).T)
    out["lng"] = cast(np.asarray(I["L0_ffn_ln_g"]).reshape(2, 8, 128).transpose(2, 0, 1))
    out["lnb"] = cast(np.asarray(I["L0_ffn_ln_b"]).reshape(2, 8, 128).transpose(2, 0, 1))

    w1 = np.asarray(I["L0_ffn_w1"])
    b1 = np.asarray(I["L0_ffn_b1"])
    a = np.zeros((2, 1024, HIDP), dtype=wdt); a[:, :, :HID] = w1[:, :, :HID]
    g = np.zeros((2, 1024, HIDP), dtype=wdt); g[:, :, :HID] = w1[:, :, HID:]
    w1i = np.zeros((2, 1024, 2 * HIDP), dtype=wdt)
    b1a = np.zeros((2, HIDP), dtype=wdt); b1a[:, :HID] = b1[:, :HID]
    b1g = np.zeros((2, HIDP), dtype=wdt); b1g[:, :HID] = b1[:, HID:]
    b1i = np.zeros((128, 2, 2 * NJ), dtype=wdt)
    for j in range(NJ):
        w1i[:, :, 256 * j:256 * j + 128] = a[:, :, 128 * j:128 * j + 128]
        w1i[:, :, 256 * j + 128:256 * j + 256] = g[:, :, 128 * j:128 * j + 128]
        b1i[:, :, 2 * j] = b1a[:, 128 * j:128 * j + 128].T
        b1i[:, :, 2 * j + 1] = b1g[:, 128 * j:128 * j + 128].T
    out["w1i"] = np.ascontiguousarray(w1i)
    out["b1i"] = np.ascontiguousarray(b1i)
    w2p = np.zeros((2, HIDP, 1024), dtype=wdt)
    w2p[:, :HID] = np.asarray(I["L0_ffn_w2"])
    out["w2p"] = w2p
    out["b2"] = cast(np.asarray(I["L0_ffn_b2"]).reshape(2, 8, 128).transpose(2, 0, 1))

    i32 = np.arange(32, dtype=f8)
    inv_freq = 10000.0 ** (-(2 * i32) / 64.0)
    t = np.arange(NT, dtype=f8)
    fr = t[:, None] * inv_freq[None, :]
    base = (2 * i32 + 0.4 * 64) / (1.4 * 64)
    power = (t - NT // 2) / (512 // 2)
    sc = base[None, :] ** power[:, None]
    cos = np.cos(fr); sin = np.sin(fr)
    def dup(v):
        w = np.concatenate([v, v], axis=1)
        w = np.concatenate([w, w], axis=1)
        return np.ascontiguousarray(w.T)
    out["csq"] = dup(8.0 * cos * sc).astype(BFNP)
    out["snq"] = dup(8.0 * sin * sc).astype(BFNP)
    out["csk"] = dup(cos / sc).astype(BFNP)
    out["snk"] = dup(sin / sc).astype(BFNP)

    out["iden"] = cast(np.eye(128))
    out["ones"] = cast(np.ones((128, 128)))
    R = np.zeros((128, 128), dtype=f8)
    for B in (0, 64):
        for d in range(32):
            R[B + d + 32, B + d] = -1.0
            R[B + d, B + d + 32] = 1.0
    out["rrot"] = cast(R)
    out["mself"] = cast(np.triu(np.full((128, 128), MBIG), 1))
    out["cneg"] = cast(np.full((1, 512), MBIG))
    out["mprev"] = cast(np.tril(np.full((128, 128), MBIG), -1))
    sel = np.zeros((16, 8, 128), dtype=f8)
    for h in range(16):
        for c in range(8):
            for p in range(128):
                if h == 2 * c + (1 if p >= 64 else 0):
                    sel[h, c, p] = 1.0
    out["sel"] = cast(sel.reshape(16, 1024))
    el2 = np.zeros((128, 4, 8), dtype=f8)
    for p in range(128):
        for c in range(4):
            el2[p, c, 2 * c + (1 if p >= 64 else 0)] = 1.0
    out["el2"] = cast(el2.reshape(128, 32))
    selh = np.zeros((16, 16, 128), dtype=f8)
    for h in range(16):
        selh[h, h, :] = 1.0
    out["selh"] = cast(selh.transpose(1, 0, 2).reshape(16, 2048))
    el2d = np.zeros((128, 8, 16), dtype=f8)
    for p in range(128):
        for c in range(8):
            el2d[p, c, 2 * c + (1 if p >= 64 else 0)] = 1.0
    out["el2d"] = cast(el2d.reshape(128, 128))
    cf = np.zeros((128, 4), dtype=f8)
    cf[:, 0] = EPS
    cf[:, 1] = 1e-24
    cf[:, 2] = -C_LOC
    out["constf"] = cast(cf)
    return out



class B:
    pass


def declare_dram(nc, stop):
    b = B()
    ap = lambda h: h.ap()
    d = lambda name, shape: ap(nc.dram_tensor(name, list(shape), F32, kind="ExternalInput"))
    b.xT = d("xT", (1024, NT))
    b.wqT, b.wkT, b.wvT, b.projT = d("wqT", (1024, 1024)), d("wkT", (1024, 1024)), d("wvT", (1024, 1024)), d("projT", (1024, 1024))
    b.bq, b.bk, b.bv, b.bproj = d("bq", (128, 8)), d("bk", (128, 8)), d("bv", (128, 8)), d("bproj", (128, 8))
    b.wqkv = d("wqkv", (2, 1024, 1536))
    b.wo = d("wo", (2, 512, 1024))
    b.qs, b.ks = d("qs", (128, 2)), d("ks", (128, 2))
    b.lng, b.lnb = d("lng", (128, 2, 8)), d("lnb", (128, 2, 8))
    b.w1i, b.b1i = d("w1i", (2, 1024, 2 * HIDP)), d("b1i", (128, 2, 2 * NJ))
    b.w2p, b.b2 = d("w2p", (2, HIDP, 1024)), d("b2", (128, 2, 8))
    b.csq, b.snq, b.csk, b.snk = (
        ap(nc.dram_tensor(n, [128, NT], BF16, kind="ExternalInput"))
        for n in ("csq", "snq", "csk", "snk"))
    b.iden, b.ones, b.rrot = d("iden", (128, 128)), d("ones", (128, 128)), d("rrot", (128, 128))
    b.mself, b.mprev = d("mself", (128, 128)), d("mprev", (128, 128))
    b.cneg = d("cneg", (1, 512))
    b.sel, b.el2 = d("sel", (16, 1024)), d("el2", (128, 32))
    b.constf = d("constf", (128, 4))
    b.el2d = d("el2d", (128, 128))
    b.selh = d("selh", (16, 2048))
    b.xsave = ap(nc.dram_tensor("xsave", [1024, NT], F32, kind="Internal"))
    b.yT = ap(nc.dram_tensor("yT", [1024, NT], F32, kind="ExternalOutput"))
    b.dbg = ap(nc.dram_tensor("dbg", [1024, NT], F32, kind="ExternalOutput"))
    return b


def rearr(ap_, p=128):
    return ap_.rearrange("(c p) t -> p c t", p=p)


def alloc_globals(nc, stack, b):
    sb = lambda name, shape, dt: stack.enter_context(nc.sbuf_tensor(name, list(shape), dt))
    b.x = sb("x_res", [128, DC, XW], F32)
    b.cast = sb("castbuf", [128, DC, 520], BF16)
    b.sq = sb("sqbuf", [128, 4, QB], BF16)
    b.sqm = sb("sqm", [128, QB], F32)
    b.stm = sb("stat_m", [128, NT], BF16)
    b.str_ = sb("stat_r", [128, NT], BF16)
    b.st2m = sb("stat2_m", [128, QB], BF16)
    b.st2r = sb("stat2_r", [128, QB], BF16)
    b.c_id = sb("c_id", [128, 128], BF16)
    b.c_on = sb("c_on", [128, 128], BF16)
    b.c_rr = sb("c_rr", [128, 128], BF16)
    b.c_ms = sb("c_ms", [128, 128], BF16)
    b.c_mp = sb("c_mp", [128, 128], BF16)
    b.c_ng = sb("c_ng", [1, 512], BF16)
    b.c_sel = sb("c_sel", [16, 1024], BF16)
    b.c_el2 = sb("c_el2", [128, 32], BF16)
    b.s_bq = sb("s_bq", [128, 8], F32)
    b.s_bk = sb("s_bk", [128, 8], F32)
    b.s_bv = sb("s_bv", [128, 8], F32)
    b.s_bp = sb("s_bp", [128, 8], F32)
    b.s_qs = sb("s_qs", [128, 2], F32)
    b.s_ks = sb("s_ks", [128, 2], F32)
    b.s_lng = sb("s_lng", [128, 2, 8], F32)
    b.s_lnb = sb("s_lnb", [128, 2, 8], F32)
    b.s_b1i = sb("s_b1i", [128, 2, 2 * NJ], F32)
    b.s_b2 = sb("s_b2", [128, 2, 8], F32)
    b.s_cf = sb("s_cf", [128, 4], F32)
    b.c_el2d = sb("c_el2d", [128, 128], BF16)
    b.c_selh = sb("c_selh", [16, 2048], BF16)


def phase_load(prog, b):
    ph = Phase(prog, "load")
    ph.dma("sync", b.x[:, :, 1:NT + 1], rearr(b.xT))
    for dst, src in ((b.s_bq, b.bq), (b.s_bk, b.bk), (b.s_bv, b.bv), (b.s_bp, b.bproj)):
        ph.dma("sync", dst[:], src[:, :])
    ph.dma("sync", b.s_qs[:], b.qs[:, :])
    ph.dma("sync", b.s_ks[:], b.ks[:, :])
    ph.dma("sync", b.s_lng[:], b.lng[:, :, :])
    ph.dma("sync", b.s_lnb[:], b.lnb[:, :, :])
    ph.dma("sync", b.s_b1i[:], b.b1i[:, :, :])
    ph.dma("sync", b.s_b2[:], b.b2[:, :, :])
    ph.dma("sync", b.s_cf[:], b.constf[:, :])
    for dst, src in ((b.c_id, b.iden), (b.c_on, b.ones), (b.c_rr, b.rrot),
                     (b.c_ms, b.mself), (b.c_mp, b.mprev), (b.c_ng, b.cneg)):
        ph.dma("gpsimd", dst[:], src[:, :])
    ph.dma("gpsimd", b.c_sel[:], b.sel[:, :])
    ph.dma("gpsimd", b.c_el2[:], b.el2[:, :])
    ph.dma("gpsimd", b.c_el2d[:], b.el2d[:, :])
    ph.dma("gpsimd", b.c_selh[:], b.selh[:, :])
    ph.op("vector", lambda e: e.memset(b.x[:, :, 0:1], 0.0), inc="V")
    ph.run()
    prog.mark_ls = prog.cnt["Ls"]
    prog.mark_lg = prog.cnt["Lg"]



def emit_stats_qb(ph, b, pb, qb, src_slices, dst_m, dst_r, last_cast_wait):
    s_sq1 = None
    for c in range(4):
        s_sq1 = ph.op("scalar", lambda e, c=c: e.activation(
            b.sq[:, c, :], src_slices(c), AF.Square), waits=[("V", last_cast_wait)],
            inc="S")
    for c in range(4):
        ph.op("tensor", lambda e, c=c: e.matmul(
            pb[:, 0, :], b.c_on[:], src_slices(c), start=(c == 0), stop=False),
            waits=([("V", last_cast_wait)] if c == 0 else []))
    t_x2a = None
    for c in range(4):
        t_x2a = ph.op("tensor", lambda e, c=c: e.matmul(
            pb[:, 1, :], b.c_on[:], b.sq[:, c, :], start=(c == 0), stop=False),
            waits=([("S", s_sq1)] if c == 0 else []), inc="T" if c == 3 else None)
    s_sq2 = None
    for c in range(4, 8):
        s_sq2 = ph.op("scalar", lambda e, c=c: e.activation(
            b.sq[:, c - 4, :], src_slices(c), AF.Square),
            waits=([("T", t_x2a)] if c == 4 else []), inc="S")
    t_x = None
    for c in range(4, 8):
        t_x = ph.op("tensor", lambda e, c=c: e.matmul(
            pb[:, 0, :], b.c_on[:], src_slices(c), start=False, stop=(c == 7)),
            inc="T" if c == 7 else None)
    t_x2 = None
    for c in range(4, 8):
        t_x2 = ph.op("tensor", lambda e, c=c: e.matmul(
            pb[:, 1, :], b.c_on[:], b.sq[:, c - 4, :], start=False, stop=(c == 7)),
            waits=([("S", s_sq2)] if c == 4 else []), inc="T" if c == 7 else None)
    s_m = ph.op("scalar", lambda e: e.activation(
        dst_m, pb[:, 0, :], AF.Copy, scale=1.0 / 1024), waits=[("T", t_x)], inc="S")
    s_m2 = ph.op("scalar", lambda e: e.activation(
        b.sqm[:], dst_m, AF.Square), waits=[("S", s_m)], inc="S")
    v_var = ph.op("vector", lambda e: e.scalar_tensor_tensor(
        b.sqm[:], pb[:, 1, :], 1.0 / 1024, b.sqm[:], ALU.mult, ALU.subtract),
        waits=[("S", s_m2), ("T", t_x2)], inc="V")
    s_sd = ph.op("scalar", lambda e: e.activation(
        b.sqm[:], b.sqm[:], AF.Sqrt, bias=b.s_cf[:, 0:1]), waits=[("V", v_var)], inc="S")
    v_r = ph.op("vector", lambda e: e.reciprocal(dst_r, b.sqm[:]),
                waits=[("S", s_sd)], inc="V")
    return v_r



def phase_dense(prog, b, stack, taps, stop=None):
    nc = prog.nc
    with contextlib.ExitStack() as dstack:
        sb = lambda name, shape, dt: dstack.enter_context(nc.sbuf_tensor(name, list(shape), dt))
        kT = sb("kT", [128, DC, NT], BF16)
        vTM = sb("vTM", [128, 16, 16, 66], BF16)
        v_sb = sb("v_sb", [128, QB], BF16)
        kmax = sb("kmax", [16, 2], F32)
        knrm = sb("knrm", [16, QB], F32)

        wstack = contextlib.ExitStack()
        sbw = lambda name, shape, dt: wstack.enter_context(nc.sbuf_tensor(name, list(shape), dt))
        wA = sbw("wA", [128, DC, 1024], BF16)
        wB = sbw("wB", [128, DC, 1024], BF16)
        ph = Phase(prog, "D0")
        for _e in ("vector", "tensor", "scalar"):
            ph.entry[_e] = [("Ls", prog.mark_ls), ("Lg", prog.mark_lg)]
        l_wk = ph.dma("gpsimd", wA[:], rearr(b.wkT), waits=[("Lg", prog.mark_lg)])
        l_wv = ph.dma("gpsimd", wB[:], rearr(b.wvT))
        d0stack = contextlib.ExitStack()
        pb = d0stack.enter_context(nc.psum_tensor("psD0", [128, 2, QB], F32))
        for qb in range(NQB):
            vw = ph.op("vector", lambda e, qb=qb: e.tensor_copy(
                b.cast[:, :, 0:513], b.x[:, :, qb * QB:qb * QB + 513]), inc="V")
            emit_stats_qb(ph, b, pb, qb,
                          lambda c: b.cast[:, c, 1:513],
                          b.stm[:, qb * QB:(qb + 1) * QB],
                          b.str_[:, qb * QB:(qb + 1) * QB], vw)
        ph.run()
        d0stack.close()
        if stop == "d0":
            tap_x(prog, b)
            return

        ph = Phase(prog, "D1")
        d1stack = contextlib.ExitStack()
        pb = d1stack.enter_context(nc.psum_tensor("psD1", [128, 8, QB], F32))
        t_evt = {}
        last_users = 0
        tr_evt = {0: 0, 1: 0}
        s_last = {0: None, 1: None}
        v_tr = {0: None, 1: None}
        t_tr = {0: None, 1: None}
        s_knrm = None
        t_kn_hist = {}
        prev_mm = {}
        for qb in range(NQB):
            vw = ph.op("vector", lambda e, qb=qb: e.tensor_copy(
                b.cast[:, :, 0:513], b.x[:, :, qb * QB:qb * QB + 513]),
                waits=([("T", prev_mm.get("last", 0))] if qb else []), inc="V")
            vw = ph.op("vector", lambda e, qb=qb: e.tensor_tensor(
                b.cast[:, :, 1:513], b.cast[:, :, 1:513],
                bc(b.stm[:, qb * QB:(qb + 1) * QB], DC), ALU.subtract), inc="V")
            vw = ph.op("vector", lambda e, qb=qb: e.tensor_tensor(
                b.cast[:, :, 1:513], b.cast[:, :, 1:513],
                bc(b.str_[:, qb * QB:(qb + 1) * QB], DC), ALU.mult), inc="V")
            tlast = None
            for co in range(16):
                par = co % 2
                wsb = wA if co < 8 else wB
                wcol = co * 128 if co < 8 else (co - 8) * 128
                waits = [("V", vw)]
                if co == 0 and qb == 0:
                    waits.append(("Lg", l_wv))
                if s_last[par] is not None:
                    waits.append(("S", s_last[par]))
                for ci in range(8):
                    tlast = ph.op("tensor", lambda e, ci=ci, par=par, wsb=wsb, wcol=wcol: e.matmul(
                        pb[:, par, :], wsb[:, ci, wcol:wcol + 128],
                        b.cast[:, ci, 1:513], start=(ci == 0), stop=(ci == 7)),
                        waits=(waits if ci == 0 else []), inc="T" if ci == 7 else None)
                if co < 8:
                    s_ev = ph.op("scalar", lambda e, co=co, par=par, qb=qb: e.activation(
                        kT[:, co, qb * QB:(qb + 1) * QB], pb[:, par, :], AF.Identity,
                        bias=b.s_bk[:, co:co + 1]), waits=[("T", tlast)], inc="S")
                    if D1LITE in ("proj", "tr"):
                        s_last[par] = s_ev
                        continue
                    sqw = []
                    if co >= 4 and t_kn_hist.get(co - 4):
                        sqw.append(("T", t_kn_hist[co - 4]))
                    s_ksq = ph.op("scalar", lambda e, co=co, qb=qb: e.activation(
                        b.sq[:, co % 4, :], kT[:, co, qb * QB:(qb + 1) * QB],
                        AF.Square), waits=sqw, inc="S")
                    s_last[par] = s_ksq
                    kw = [("S", s_ksq)]
                    if co == 0 and s_knrm is not None:
                        kw.append(("S", s_knrm))
                    t_kn = ph.op("tensor", lambda e, co=co: e.matmul(
                        pb[0:16, 2, :], b.c_el2d[:, co * 16:co * 16 + 16],
                        b.sq[:, co % 4, :], start=(co == 0), stop=(co == 7)),
                        waits=kw, inc="T")
                    t_kn_hist[co] = t_kn
                    if co == 7:
                        s_knrm = ph.op("scalar", lambda e: e.activation(
                            knrm[:], pb[0:16, 2, :], AF.Sqrt), waits=[("T", t_kn)], inc="S")
                        if qb == 0:
                            v_km = ph.op("vector", lambda e: e.reduce_max(
                                kmax[:, 0:1], knrm[:], axis=mybir.AxisListType.X),
                                waits=[("S", s_knrm)], inc="V")
                        else:
                            v_km = ph.op("vector", lambda e: e.reduce_max(
                                kmax[:, 1:2], knrm[:], axis=mybir.AxisListType.X),
                                waits=[("S", s_knrm)], inc="V")
                            v_km = ph.op("vector", lambda e: e.tensor_max(
                                kmax[:, 0:1], kmax[:, 0:1], kmax[:, 1:2]), inc="V")
                else:
                    cv = co - 8
                    s_ev = ph.op("scalar", lambda e, par=par, cv=cv: e.activation(
                        v_sb[:], pb[:, par, :], AF.Identity, bias=b.s_bv[:, cv:cv + 1]),
                        waits=[("T", tlast)] + ([("T", t_tr.get("v_done", 0))] if False else []),
                        inc="S")
                    s_last[par] = s_ev
                    for tt in range(4):
                        if D1LITE == "proj":
                            continue
                        slot = tt % 2
                        twaits = [("S", s_ev)]
                        if v_tr[slot] is not None:
                            twaits.append(("V", v_tr[slot]))
                        t_tr[slot] = ph.op("tensor", lambda e, tt=tt, slot=slot: e.transpose(
                            pb[:, 4 + slot, 0:64].bitcast(BF16),
                            v_sb[:, tt * 128:tt * 128 + 128], b.c_id[:]),
                            waits=twaits, inc="T")
                        v_tr[slot] = ph.op("vector", lambda e, qb=qb, tt=tt, cv=cv, slot=slot: e.tensor_copy(
                            vTM[:, qb * 4 + tt, 2 * cv:2 * cv + 2, 0:64],
                            pb[:, 4 + slot, 0:64].bitcast(BF16)),
                            waits=[("T", t_tr[slot])], inc="V")
            prev_mm["last"] = tlast
        ph.op("vector", lambda e: e.memset(vTM[:, :, :, 64:65], 1.0), inc="V")
        ph.run()
        d1stack.close()
        wstack.close()
        if stop == "d1":
            tap_x(prog, b)
            return

        if "kT" in taps:
            tp = Phase(prog, "tap_kT")
            tp.dma("gpsimd", rearr(b.yT), kT[:, :, :])
            tp.op("sync", lambda e: e.nop(), waits=[("Lg", prog.cnt["Lg"])])
            tp.run()
            return

        ph = Phase(prog, "D2")
        d2stack = contextlib.ExitStack()
        sb2 = lambda name, shape, dt: d2stack.enter_context(nc.sbuf_tensor(name, list(shape), dt))
        wQ = sb2("wQ", [128, 2, DC, 128], BF16)
        wP = sb2("wP", [128, 2, DC, 128], BF16)
        qto = sb2("qTO", [128, DC, QB], BF16)
        qn = knrm
        bq16 = sb2("bq16", [16, QB], BF16)
        bnegb = b.sq
        tmp2 = sb2("tmp2", [128, 4, QB], F32)
        attb = sb2("attb", [128, DC, QB], BF16)
        sums = b.cast[:, 4:8, 0:512]
        pt = sb2("PTbuf", [128, 8, QB], BF16)
        sumsb = sb2("sumsb", [16, QB], BF16)
        pb = d2stack.enter_context(nc.psum_tensor("psD2", [128, 8, QB], F32))
        s_evq = {0: None, 1: None}
        s_att = {0: None, 1: None}
        t_wp_cons = {0: 0, 1: 0}
        t_wq_cons = {0: 0, 1: 0}
        l_sc_prev = 0
        v_resid = 0
        t_rn_prev = 0
        s_qn_prev = 0
        for qb in range(NQB):
            qc = qb * QB
            cw = []
            if qb:
                cw.append(("T", t_wq_cons[1] or 0))
                if l_sc_prev:
                    cw.append(("Lx", l_sc_prev))
            vw = ph.op("vector", lambda e, qb=qb: e.tensor_copy(
                b.cast[:, :, 1:513], b.x[:, :, qb * QB + 1:qb * QB + 513]),
                waits=cw, inc="V")
            vw = ph.op("vector", lambda e, qc=qc: e.tensor_tensor(
                b.cast[:, :, 1:513], b.cast[:, :, 1:513],
                bc(b.stm[:, qc:qc + QB], DC), ALU.subtract), inc="V")
            vw = ph.op("vector", lambda e, qc=qc: e.tensor_tensor(
                b.cast[:, :, 1:513], b.cast[:, :, 1:513],
                bc(b.str_[:, qc:qc + QB], DC), ALU.mult), inc="V")
            l_taph = 0
            if "d2" in taps and qb == 0:
                l_taph = ph.tapdma(rearr(b.dbg)[:, 0:1, 1536:2048],
                                   b.stm[0:128, 0:512].unsqueeze(1), waits=[("V", vw)])
                l_taph = ph.tapdma(rearr(b.dbg)[:, 1:2, 1536:2048],
                                   b.str_[0:128, 0:512].unsqueeze(1), waits=[("V", vw)])
                l_taph = ph.tapdma(rearr(b.dbg)[:, 2:4, 1536:2048],
                                   b.cast[:, 0:2, 1:513], waits=[("V", vw)])
            for co in range(8):
                par = co % 2
                qw = []
                if t_wq_cons[par]:
                    qw.append(("T", t_wq_cons[par]))
                if t_wp_cons[par]:
                    qw.append(("T", t_wp_cons[par]))
                l_wq = ph.dma("gpsimd", wQ[:, par, :, :],
                              b.wqT[:, 128 * co:128 * co + 128].rearrange(
                                  "(c p) f -> p c f", p=128), waits=qw, sem="Lw%d" % par)
                waits = [("V", vw), ("Lw%d" % par, l_wq)]
                if s_evq[par] is not None:
                    waits.append(("S", s_evq[par]))
                if s_att[par] is not None:
                    waits.append(("S", s_att[par]))
                tlast = None
                for ci in range(8):
                    tlast = ph.op("tensor", lambda e, ci=ci, par=par, co=co: e.matmul(
                        pb[:, par, :], wQ[:, par, ci, :],
                        b.cast[:, ci, 1:513], start=(ci == 0), stop=(ci == 7)),
                        waits=(waits if ci == 0 else []), inc="T" if ci == 7 else None)
                t_wq_cons[par] = tlast
                s_evq[par] = ph.op("scalar", lambda e, co=co, par=par: e.activation(
                    qto[:, co, :], pb[:, par, :], AF.Identity, bias=b.s_bq[:, co:co + 1]),
                    waits=[("T", tlast)], inc="S")
            l_tapq = 0
            if "d2" in taps and qb == 0:
                l_tapq = ph.tapdma(rearr(b.dbg)[:, :, 512:1024], qto[:, :, :],
                                   waits=[("S", s_evq[0]), ("S", s_evq[1])])
            t_qn = None
            s_qsq = None
            for grp in range(2):
                for c4 in range(4):
                    c = grp * 4 + c4
                    sqw = [("S", s_evq[c % 2])] if grp == 0 and c4 == 0 else []
                    if grp == 1 and c4 == 0:
                        sqw.append(("T", t_qn))
                    s_qsq = ph.op("scalar", lambda e, c=c, c4=c4: e.activation(
                        b.sq[:, c4, :], qto[:, c, :], AF.Square), waits=sqw, inc="S")
                for c4 in range(4):
                    c = grp * 4 + c4
                    qw = [("S", s_qsq)] if c4 == 0 else []
                    if grp == 0 and c4 == 0:
                        qw.append(("S", s_qn_prev))
                        if t_rn_prev:
                            qw.append(("T", t_rn_prev))
                    t_qn = ph.op("tensor", lambda e, c=c, c4=c4: e.matmul(
                        pb[0:16, 6, :], b.c_el2d[:, c * 16:c * 16 + 16],
                        b.sq[:, c4, :], start=(c == 0), stop=(c == 7)),
                        waits=qw, inc="T")
            s_qn_prev = ph.op("scalar", lambda e: e.activation(
                qn[:], pb[0:16, 6, :], AF.Sqrt), waits=[("T", t_qn)], inc="S")
            v_bq = ph.op("vector", lambda e: e.tensor_scalar(
                bq16[:], qn[:], kmax[:, 0:1], -1.0, ALU.mult, ALU.mult),
                waits=[("S", s_qn_prev)] + ([("T", t_rn_prev)] if t_rn_prev else []),
                inc="V")
            v_oev = None
            v_last_of_pair = {}
            s_expg = {}
            v_addg = {}
            v_bb = {}
            t_av = {i: 0 for i in range(8)}
            for c in range(8):
                obase = 4 + 2 * (c % 2)
                for par_h in range(2):
                    h = 2 * c + par_h
                    bslot = (c % 2) * 2 + par_h
                    wts = [("V", v_bq)]
                    if v_addg.get((7, par_h, c - 1)) is not None:
                        wts.append(("V", v_addg[(7, par_h, c - 1)]))
                    t_s = ph.op("tensor", lambda e, h=h, par_h=par_h: e.matmul(
                        pb[:, par_h * 2, :], b.c_selh[:, h * 128:h * 128 + 128],
                        bq16[:], start=True, stop=True), waits=wts, inc="T")
                    v_bb[bslot] = ph.op("vector", lambda e, bslot=bslot, par_h=par_h: e.tensor_copy(
                        bnegb[:, bslot, :], pb[:, par_h * 2, :]), waits=[("T", t_s)], inc="V")
                t_avl = {}
                for kg in range(9):
                    if kg < 8:
                        tsm = {}
                        for par_h in range(2):
                            for j2 in range(2):
                                kc = 2 * kg + j2
                                bank = par_h * 2 + j2
                                waits = []
                                if s_evq[0] is not None and kg == 0 and par_h == 0 and j2 == 0:
                                    waits += [("S", s_evq[0]), ("S", s_evq[1])]
                                prev_add = v_addg.get((kg - 1, par_h, c)) or v_addg.get((7, par_h, c - 1))
                                if kg == 0:
                                    waits.append(("V", v_bb[(c % 2) * 2 + par_h]))
                                if prev_add is not None:
                                    waits.append(("V", prev_add))
                                tsm[par_h] = ph.op("tensor", lambda e, kc=kc, r0=64 * par_h, bank=bank, c=c: e.matmul(
                                    pb[:, bank, :], kT[r0:r0 + 64, c, kc * 128:kc * 128 + 128],
                                    qto[r0:r0 + 64, c, :], start=True, stop=True,
                                    tile_position=(r0, 0)), waits=waits, inc="T")
                        for par_h in range(2):
                            bslot = (c % 2) * 2 + par_h
                            aw = [("T", tsm[par_h])]
                            prev_exp = s_expg.get((kg - 1, par_h, c)) or s_expg.get((7, par_h, c - 1))
                            if prev_exp is not None:
                                aw.append(("S", prev_exp))
                            v_addg[(kg, par_h, c)] = ph.op("vector", lambda e, par_h=par_h, bslot=bslot: e.tensor_tensor(
                                tmp2[:, par_h * 2:par_h * 2 + 2, :],
                                pb[:, par_h * 2:par_h * 2 + 2, :],
                                bnegb[:, bslot:bslot + 1, :].broadcast_to([128, 2, QB]),
                                ALU.add), waits=aw, inc="V")
                            ptbase = (kg % 2) * 4 + par_h * 2
                            wts = [("V", v_addg[(kg, par_h, c)])]
                            if t_av[ptbase] or t_av[ptbase + 1]:
                                wts.append(("T", max(t_av[ptbase], t_av[ptbase + 1])))
                            s_expg[(kg, par_h, c)] = ph.op("scalar", lambda e, par_h=par_h, ptbase=ptbase: e.activation(
                                pt[:, ptbase:ptbase + 2, :], tmp2[:, par_h * 2:par_h * 2 + 2, :],
                                AF.Exp, scale=0.125), waits=wts, inc="S")
                    if kg >= 1:
                        kga = kg - 1
                        for par_h in range(2):
                            for j2 in range(2):
                                ka = 2 * kga + j2
                                slot = (kga % 2) * 4 + par_h * 2 + j2
                                obank = obase + par_h
                                avw = [("S", s_expg[(kga, par_h, c)])]
                                if ka == 0:
                                    if v_last_of_pair.get(c - 2) is not None:
                                        avw.append(("V", v_last_of_pair[c - 2]))
                                    if c == 1 and s_qn_prev:
                                        avw.append(("S", s_qn_prev))
                                t_av[slot] = ph.op("tensor", lambda e, c=c, ka=ka, slot=slot, obank=obank, par_h=par_h: e.matmul(
                                    pb[0:65, obank, :], vTM[:, ka, 2 * c + par_h, 0:65],
                                    pt[:, slot, :], start=(ka == 0), stop=(ka == 15)),
                                    waits=avw, inc="T")
                                t_avl[par_h] = t_av[slot]
                h0, h1 = 2 * c, 2 * c + 1
                ph.op("vector", lambda e, h=h0, obase=obase: e.tensor_copy(
                    sums[32 * (h // 4):32 * (h // 4) + 1, h % 4, :], pb[64:65, obase, :]),
                    waits=[("T", t_avl[0])], inc="V")
                ph.op("vector", lambda e, h=h1, obase=obase: e.tensor_copy(
                    sums[32 * (h // 4):32 * (h // 4) + 1, h % 4, :], pb[64:65, obase + 1, :]),
                    waits=[("T", t_avl[1])], inc="V")
                ph.op("vector", lambda e, c=c, obase=obase: e.tensor_copy(
                    qto[0:64, c, :], pb[0:64, obase, :]),
                    waits=([("Ld", l_tapq)] if (c == 0 and l_tapq) else []), inc="V")
                v_oev = ph.op("vector", lambda e, c=c, obase=obase: e.tensor_copy(
                    qto[64:128, c, :], pb[0:64, obase + 1, :]), inc="V")
                v_last_of_pair[c] = v_oev
            l_sc = None
            for g in range(4):
                gw = [("V", v_oev)]
                if g == 0 and t_rn_prev:
                    gw.append(("T", t_rn_prev))
                l_sc = ph.dma("sync", sumsb[4 * g:4 * g + 4, :],
                              sums[32 * g:32 * g + 1, :, :], waits=gw, sem="Lx")
            l_sc_prev = l_sc
            v_rc = ph.op("vector", lambda e: e.reciprocal(sumsb[:], sumsb[:]),
                         waits=[("Lx", l_sc)], inc="V")
            v_nrm = None
            t_rn = None
            for c in range(8):
                wts = [("V", v_rc)]
                if v_nrm is not None:
                    wts.append(("V", v_nrm))
                t_rn = ph.op("tensor", lambda e, c=c: e.matmul(
                    pb[:, 6, :], b.c_sel[:, c * 128:c * 128 + 128], sumsb[:],
                    start=True, stop=True), waits=wts, inc="T")
                v_nrm = ph.op("vector", lambda e, c=c: e.tensor_tensor(
                    qto[:, c, :], qto[:, c, :], pb[:, 6, :], ALU.mult),
                    waits=[("T", t_rn)], inc="V")
            t_rn_prev = t_rn
            if "d2" in taps and qb == 0:
                ph.tapdma(rearr(b.dbg)[:, :, 0:512], qto[:, :, :],
                          waits=[("V", v_nrm)])
            for f in range(8):
                par = f % 2
                wts = []
                if t_wp_cons[par]:
                    wts.append(("T", t_wp_cons[par]))
                if t_wq_cons[par]:
                    wts.append(("T", t_wq_cons[par]))
                l_wp = ph.dma("gpsimd", wP[:, par, :, :],
                              b.projT[:, 128 * f:128 * f + 128].rearrange(
                                  "(c p) f -> p c f", p=128), waits=wts,
                              sem="Lw%d" % par)
                tlast = None
                waits = [("V", v_nrm), ("Lw%d" % par, l_wp)]
                if s_att[par] is not None:
                    waits.append(("S", s_att[par]))
                for ci in range(8):
                    tlast = ph.op("tensor", lambda e, ci=ci, par=par: e.matmul(
                        pb[:, par, :], wP[:, par, ci, :],
                        qto[:, ci, :], start=(ci == 0), stop=(ci == 7)),
                        waits=(waits if ci == 0 else []), inc="T" if ci == 7 else None)
                t_wp_cons[par] = tlast
                s_att[par] = ph.op("scalar", lambda e, f=f, par=par: e.activation(
                    attb[:, f, :], pb[:, par, :], AF.Identity,
                    bias=b.s_bp[:, f:f + 1]),
                    waits=[("T", tlast)] + ([("Ld", l_taph)] if (l_taph and f < 2) else []),
                    inc="S")
            l_tapa = 0
            if "d2" in taps and qb == 0:
                l_tapa = ph.tapdma(rearr(b.dbg)[:, :, 1024:1536], attb[:, :, :],
                                   waits=[("S", s_att[0]), ("S", s_att[1])])
            s_sq1 = None
            for c in range(4):
                sqw2 = []
                if c == 0:
                    sqw2 = [("S", s_att[0]), ("S", s_att[1])]
                    for ph_ in range(2):
                        la = v_addg.get((7, ph_, 7))
                        if la is not None:
                            sqw2.append(("V", la))
                s_sq1 = ph.op("scalar", lambda e, c=c: e.activation(
                    b.sq[:, c, :], attb[:, c, :], AF.Square),
                    waits=sqw2, inc="S")
            for c in range(4):
                ph.op("tensor", lambda e, c=c: e.matmul(
                    pb[:, 6, :], b.c_on[:], attb[:, c, :],
                    start=(c == 0), stop=False),
                    waits=([("S", s_att[0]), ("S", s_att[1]), ("V", v_nrm)] if c == 0 else []))
            t_x2a = None
            for c in range(4):
                t_x2a = ph.op("tensor", lambda e, c=c: e.matmul(
                    pb[:, 7, :], b.c_on[:], b.sq[:, c, :], start=(c == 0), stop=False),
                    waits=([("S", s_sq1)] if c == 0 else []), inc="T" if c == 3 else None)
            s_sq2 = None
            for c in range(4, 8):
                s_sq2 = ph.op("scalar", lambda e, c=c: e.activation(
                    b.sq[:, c - 4, :], attb[:, c, :], AF.Square),
                    waits=([("T", t_x2a)] if c == 4 else []), inc="S")
            t_x = None
            for c in range(4, 8):
                t_x = ph.op("tensor", lambda e, c=c: e.matmul(
                    pb[:, 6, :], b.c_on[:], attb[:, c, :], start=False,
                    stop=(c == 7)), inc="T" if c == 7 else None)
            t_x2 = None
            for c in range(4, 8):
                t_x2 = ph.op("tensor", lambda e, c=c: e.matmul(
                    pb[:, 7, :], b.c_on[:], b.sq[:, c - 4, :], start=False,
                    stop=(c == 7)),
                    waits=([("S", s_sq2)] if c == 4 else []), inc="T" if c == 7 else None)
            s_m = ph.op("scalar", lambda e: e.activation(
                b.st2m[:], pb[:, 6, :], AF.Copy, scale=1.0 / 1024),
                waits=[("T", t_x)], inc="S")
            s_m2 = ph.op("scalar", lambda e: e.activation(
                b.sqm[:], b.st2m[:], AF.Square), waits=[("S", s_m)], inc="S")
            v_var = ph.op("vector", lambda e: e.scalar_tensor_tensor(
                b.sqm[:], pb[:, 7, :], 1.0 / 1024, b.sqm[:], ALU.mult, ALU.subtract),
                waits=[("S", s_m2), ("T", t_x2)], inc="V")
            s_sd = ph.op("scalar", lambda e: e.activation(
                b.sqm[:], b.sqm[:], AF.Sqrt, bias=b.s_cf[:, 0:1]), waits=[("V", v_var)], inc="S")
            v_r = ph.op("vector", lambda e: e.reciprocal(b.st2r[:], b.sqm[:]),
                        waits=[("S", s_sd)], inc="V")
            v_mr = ph.op("vector", lambda e: e.tensor_tensor(
                b.sqm[:], b.st2m[:], b.st2r[:], ALU.mult), waits=[("V", v_r)], inc="V")
            ph.op("vector", lambda e: e.tensor_tensor(
                attb[:, :, :], attb[:, :, :], bc(b.st2r[:], DC), ALU.mult),
                waits=[("V", v_r)] + ([("Ld", l_tapa)] if l_tapa else []), inc="V")
            ph.op("vector", lambda e, qc=qc: e.tensor_tensor(
                b.x[:, :, qc + 1:qc + 513], b.x[:, :, qc + 1:qc + 513],
                attb[:, :, :], ALU.add), inc="V")
            v_resid = ph.op("vector", lambda e, qc=qc: e.tensor_tensor(
                b.x[:, :, qc + 1:qc + 513], b.x[:, :, qc + 1:qc + 513],
                bc(b.sqm[:], DC), ALU.subtract), waits=[("V", v_mr)], inc="V")
        ph.run()
        d2stack.close()


def phase_h2(prog, b, stack, pre=None):
    nc = prog.nc
    with contextlib.ExitStack() as hstack:
        pb = hstack.enter_context(nc.psum_tensor("psH2", [128, 2, QB], F32))
        ph = Phase(prog, "H2")
        if pre:
            pre(ph)
        l_sv = {}
        prevq = {"sync": prog.cnt["Ls"], "gpsimd": prog.cnt["Lg"]}
        for qb in range(NQB):
            qc = qb * QB
            eng = "sync" if qb % 2 == 0 else "gpsimd"
            sem = "Ls" if eng == "sync" else "Lg"
            l_sv[qb] = (sem, ph.dma(eng, rearr(b.xsave)[:, :, qc:qc + QB],
                                    b.x[:, :, qc + 1:qc + 513],
                                    waits=[(sem, prevq[eng])]))
            prevq[eng] = l_sv[qb][1]
        vws = []
        for qb in range(NQB):
            vw = ph.op("vector", lambda e, qb=qb: e.tensor_copy(
                b.cast[:, :, 0:513], b.x[:, :, qb * QB:qb * QB + 513]), inc="V")
            v_r = emit_stats_qb(ph, b, pb, qb,
                                lambda c: b.cast[:, c, 1:513],
                                b.stm[:, qb * QB:(qb + 1) * QB],
                                b.str_[:, qb * QB:(qb + 1) * QB], vw)
            vws.append(v_r)
        for qb in range(NQB):
            qc = qb * QB
            ph.op("vector", lambda e, qc=qc, qb=qb: e.tensor_tensor(
                b.x[:, :, qc + 1:qc + 513], b.x[:, :, qc + 1:qc + 513],
                bc(b.stm[:, qc:qc + QB], DC), ALU.subtract),
                waits=[l_sv[qb], ("V", vws[qb])], inc="V")
            ph.op("vector", lambda e, qc=qc: e.tensor_tensor(
                b.x[:, :, qc + 1:qc + 513], b.x[:, :, qc + 1:qc + 513],
                bc(b.str_[:, qc:qc + QB], DC), ALU.mult), inc="V")
        ph.run()


def phase_inner_attn(prog, b, stack, l, wkv=None, l_wkv_pre=None):
    nc = prog.nc
    with contextlib.ExitStack() as astack:
        sb = lambda name, shape, dt: astack.enter_context(nc.sbuf_tensor(name + str(l), list(shape), dt))
        r_csq = sb("r_csq", [128, NT], BF16)
        r_snq = sb("r_snq", [128, NT], BF16)
        r_csk = sb("r_csk", [128, NT], BF16)
        r_snk = sb("r_snk", [128, NT], BF16)
        kb = sb("kb", [128, 4, NT], BF16)
        vTMl = sb("vTMl", [128, 16, 8, 66], BF16)
        qv2 = sb("qv2", [128, 2, 4, QB], BF16)
        wosb = sb("wosb", [128, 4, 1024], BF16)
        v_sb = sb("v_sbA", [128, QB], BF16)
        u1b = sb("u1b", [128, 2, QB], BF16)
        u2b = sb("u2b", [128, 2, QB], BF16)
        pt = sb("PTA", [128, 8, QB], BF16)
        rnb = sb("rnb", [8, 2, QB], BF16)
        lsum = b.sqm[0:8, :]
        lsums = sb("lsumsA", [128, 4, QB], BF16)
        lsumb = b.st2m[0:8, :]
        xbnd = sb("xbndA", [128, DC, 4], F32)
        pb = astack.enter_context(nc.psum_tensor("psA" + str(l), [128, 8, QB], F32))
        pl = pb[0:8, 5, :]

        ph = Phase(prog, "A%d" % l)
        for dst, src in ((r_csq, b.csq), (r_snq, b.snq), (r_csk, b.csk), (r_snk, b.snk)):
            l_rc = ph.dma("sync", dst[:], src[:, :], sem="Lo")
        if wkv is None:
            wkv = astack.enter_context(nc.sbuf_tensor("wkvA" + str(l), [128, DC, 1536], BF16))
            l_wkv_pre = ph.dma("gpsimd", wkv[:],
                               b.wqkv[l, :, :].rearrange("(c p) f -> p c f", p=128),
                               waits=[("Lg", prog.cnt["Lg"])])
        l_wo = ph.dma("gpsimd", wosb[:], b.wo[l, :, :].rearrange("(c p) f -> p c f", p=128),
                      waits=[("Lg", l_wkv_pre)])
        l_wkv = l_wkv_pre
        v_ones = ph.op("vector", lambda e: e.memset(vTMl[:, :, :, 64:65], 1.0), inc="V")
        ph.entry["vector"] = [("Lo", l_rc)]
        v_bnd = ph.op("vector", lambda e: e.tensor_copy(
            xbnd[:, :, 0:3], b.x[:, :, 512:1537:512]), inc="V")

        st = {
            "s_prep": {0: None, 1: None},
            "v_tr": {0: None, 1: None},
            "s_ln": {},
            "t_pl": {},
            "s_pl_read": 0,
            "rb": {2: None, 3: None},
            "t_cast": 0,
            "l_lc": 0,
            "t_rn_prev": 0,
            "vw": {},
        }

        def emit_cast(qb):
            qc = qb * QB
            vw = ph.op("vector", lambda e, qb=qb: e.tensor_copy(
                b.cast[:, :, 0:513], b.x[:, :, qb * QB:qb * QB + 513]),
                waits=([("T", st["t_cast"])] if qb else []), inc="V")
            if qb > 0:
                vw = ph.op("vector", lambda e, qb=qb: e.tensor_copy(
                    b.cast[:, :, 0:1], xbnd[:, :, qb - 1:qb]), inc="V")
            st["vw"][qb] = vw

        def emit_co_group(qb, co):
            qc = qb * QB
            grp = 0 if co < 4 else 1
            c4 = co % 4
            waits = [("V", st["vw"][qb])]
            if qb == 0 and co == 0:
                waits.append(("Lg", l_wkv))
            if st["s_prep"][0] is not None:
                waits.append(("S", st["s_prep"][0]))
            tlast = None
            for ci in range(8):
                sl = (b.cast[:, ci, 1:513] if ci < 4 else b.cast[:, ci, 0:512])
                tlast = ph.op("tensor", lambda e, ci=ci, co=co, sl=sl: e.matmul(
                    pb[:, 4, :], wkv[:, ci, co * 128:co * 128 + 128], sl,
                    start=(ci == 0), stop=(ci == 7)),
                    waits=(waits if ci == 0 else []), inc="T" if ci == 7 else None)
            st["t_cast"] = tlast
            dst = (qv2[:, qb % 2, c4, :] if grp == 0 else kb[:, c4, qc:qc + QB])
            s_ev = ph.op("scalar", lambda e, dst=dst: e.activation(
                dst, pb[:, 4, :], AF.Copy), waits=[("T", tlast)], inc="S")
            sq_w = []
            if co == 0 and st["l_lc"]:
                sq_w.append(("Lx", st["l_lc"]))
            s_sq = ph.op("scalar", lambda e, c4=c4: e.activation(
                b.sq[:, c4, :], pb[:, 4, :], AF.Square), waits=sq_w, inc="S")
            st["s_prep"][0] = s_sq
            kw = [("S", s_sq)]
            if c4 == 0:
                kw.append(("S", st["s_pl_read"]))
            t_pl = ph.op("tensor", lambda e, c4=c4: e.matmul(
                pl, b.c_el2[:, c4 * 8:c4 * 8 + 8], b.sq[:, c4, :],
                start=(c4 == 0), stop=(c4 == 3)), waits=kw, inc="T")
            if c4 == 3:
                st["t_pl"][(qb, grp)] = t_pl
                s1 = ph.op("scalar", lambda e: e.activation(
                    lsum, pl, AF.Ln, bias=b.s_cf[0:8, 1:2]),
                    waits=[("T", t_pl)], inc="S")
                st["s_pl_read"] = s1
                st["s_ln"][(qb, grp)] = ph.op("scalar", lambda e, grp=grp: e.activation(
                    rnb[:, grp, :], lsum, AF.Exp, scale=-0.5),
                    waits=[("S", s1)], inc="S")

        def emit_post(qb, grp):
            qc = qb * QB
            scl = b.s_qs if grp == 0 else b.s_ks
            cs_ = r_csq if grp == 0 else r_csk
            sn_ = r_snq if grp == 0 else r_snk
            v_pr_last = None
            for c2 in range(4):
                rbank = 2 + (c2 % 2)
                wts2 = [("S", st["s_ln"][(qb, grp)])]
                if st["rb"][rbank] is not None:
                    wts2.append(("V", st["rb"][rbank]))
                t_rn = ph.op("tensor", lambda e, c2=c2, rbank=rbank, grp=grp: e.matmul(
                    pb[:, rbank, :], b.c_sel[0:8, c2 * 128:c2 * 128 + 128],
                    rnb[:, grp, :], start=True, stop=True), waits=wts2, inc="T")
                if grp == 0:
                    dst2 = qv2[:, qb % 2, c2, :]
                else:
                    dst2 = kb[:, c2, qc:qc + QB]
                v_a = ph.op("vector", lambda e, dst2=dst2, scl=scl, rbank=rbank: e.scalar_tensor_tensor(
                    dst2, dst2, scl[:, l:l + 1], pb[:, rbank, :],
                    ALU.mult, ALU.mult), waits=[("T", t_rn)], inc="V")
                g_u1 = ph.op("gpsimd", lambda e, dst2=dst2, cs_=cs_, c2=c2 % 2, qc=qc: e.tensor_tensor(
                    u1b[:, c2, :], dst2, cs_[:, qc:qc + QB], ALU.mult),
                    waits=[("V", v_a), ("Lo", l_rc)], inc="G")
                g_u2 = ph.op("gpsimd", lambda e, dst2=dst2, sn_=sn_, c2=c2 % 2, qc=qc: e.tensor_tensor(
                    u2b[:, c2, :], dst2, sn_[:, qc:qc + QB], ALU.mult),
                    inc="G")
                t_rt = ph.op("tensor", lambda e, c2=c2 % 2, rbank=rbank: e.matmul(
                    pb[:, rbank, :], b.c_rr[:], u2b[:, c2, :], start=True, stop=True),
                    waits=[("G", g_u2)], inc="T")
                v_pr_last = ph.op("vector", lambda e, dst2=dst2, c2=c2 % 2, rbank=rbank: e.tensor_tensor(
                    dst2, u1b[:, c2, :], pb[:, rbank, :], ALU.add),
                    waits=[("T", t_rt), ("G", g_u1)], inc="V")
                st["rb"][rbank] = v_pr_last
            return v_pr_last

        def emit_v(qb):
            qc = qb * QB
            for c4 in range(4):
                co = 8 + c4
                waits = [("V", st["vw"][qb])]
                if st["s_prep"][0] is not None:
                    waits.append(("S", st["s_prep"][0]))
                tlast = None
                for ci in range(8):
                    sl = (b.cast[:, ci, 1:513] if ci < 4 else b.cast[:, ci, 0:512])
                    tlast = ph.op("tensor", lambda e, ci=ci, co=co, sl=sl: e.matmul(
                        pb[:, 4, :], wkv[:, ci, co * 128:co * 128 + 128], sl,
                        start=(ci == 0), stop=(ci == 7)),
                        waits=(waits if ci == 0 else []), inc="T" if ci == 7 else None)
                st["t_cast"] = tlast
                s_ev = ph.op("scalar", lambda e: e.activation(
                    v_sb[:], pb[:, 4, :], AF.Copy), waits=[("T", tlast)], inc="S")
                st["s_prep"][0] = s_ev
                for tt in range(4):
                    slot = tt % 2
                    twaits = [("S", s_ev)]
                    if qb == 0 and c4 == 0 and tt == 0:
                        twaits.append(("V", v_ones))
                    if st["v_tr"][slot] is not None:
                        twaits.append(("V", st["v_tr"][slot]))
                    t_tr = ph.op("tensor", lambda e, tt=tt, slot=slot: e.transpose(
                        pb[:, 6 + slot, 0:64].bitcast(BF16),
                        v_sb[:, tt * 128:tt * 128 + 128], b.c_id[:]),
                        waits=twaits, inc="T")
                    st["v_tr"][slot] = ph.op("vector", lambda e, qb=qb, tt=tt, c4=c4, slot=slot: e.tensor_copy(
                        vTMl[:, qb * 4 + tt, 2 * c4:2 * c4 + 2, 0:64],
                        pb[:, 6 + slot, 0:64].bitcast(BF16)),
                        waits=[("T", t_tr)], inc="V")

        emit_cast(0)
        for co in range(8):
            emit_co_group(0, co)
        v_q0 = emit_post(0, 0)
        v_k0 = emit_post(0, 1)
        emit_v(0)
        prep_done = {0: (v_q0, v_k0)}

        s_wo = {0: None, 1: None}
        for qb in range(NQB):
            qc = qb * QB
            qpar = qb % 2
            kts = []
            if qb > 0:
                kts += [("prev", j, 4 * (qb - 1) + j) for j in range(4)]
            kts += [("self", j, 4 * qb + j) for j in range(4)]
            nk = len(kts)
            ngr = nk // 2
            if qb < NQB - 1:
                emit_cast(qb + 1)
            inter = [[] for _ in range(4)]
            if qb < NQB - 1:
                for co in range(8):
                    inter[co // 2].append(co)
            v_oev = None
            v_last_pair = {}
            s_expg = {}
            t_av = {i: 0 for i in range(8)}
            v_q_done, v_k_done = prep_done[qb]
            v_rot_done = [("V", v_q_done), ("V", v_k_done),
                          ("V", st["v_tr"][0]), ("V", st["v_tr"][1])]
            for par in range(2):
                if s_wo[par] is not None:
                    v_rot_done.append(("V", s_wo[par]))
            for c in range(4):
                t_avl = {}
                for kg in range(ngr + 1):
                    if kg < ngr:
                        tsm = {}
                        for par_h in range(2):
                            for j2 in range(2):
                                ji = 2 * kg + j2
                                side, j, kt = kts[ji]
                                bank = par_h * 2 + j2
                                r0 = 64 * par_h
                                waits = []
                                if c == 0 and kg == 0 and par_h == 0 and j2 == 0:
                                    waits += v_rot_done
                                prev_exp = s_expg.get((kg - 1, par_h, c)) or s_expg.get((ngr - 1, par_h, c - 1))
                                if prev_exp is not None:
                                    waits.append(("S", prev_exp))
                                ph.op("tensor", lambda e, c=c, kt=kt, r0=r0, bank=bank, qpar=qpar: e.matmul(
                                    pb[:, bank, :], kb[r0:r0 + 64, c, kt * 128:kt * 128 + 128],
                                    qv2[r0:r0 + 64, qpar, c, :], start=True, stop=False,
                                    tile_position=(r0, 0)), waits=waits)
                                msk = b.c_ms if side == "self" else b.c_mp
                                if side == "self":
                                    z0, z1 = 0, 128 * j
                                else:
                                    z0, z1 = 128 * j + 128, 512
                                has_span = z1 > z0
                                ph.op("tensor", lambda e, bank=bank, j=j, msk=msk, stp=not has_span: e.matmul(
                                    pb[:, bank, 128 * j:128 * j + 128], msk[:], b.c_id[:],
                                    start=False, stop=stp), inc=None if has_span else "T")
                                if has_span:
                                    tmm = ph.op("tensor", lambda e, bank=bank, z0=z0, z1=z1: e.matmul(
                                        pb[:, bank, z0:z1], b.c_on[0:1, 0:128],
                                        b.c_ng[0:1, 0:z1 - z0], start=False, stop=True),
                                        inc="T")
                                else:
                                    tmm = prog.cnt["T"]
                                tsm[(par_h, j2)] = tmm
                        for par_h in range(2):
                            ptbase = (kg % 2) * 4 + par_h * 2
                            wts = [("T", tsm[(par_h, 0)]), ("T", tsm[(par_h, 1)])]
                            if t_av[ptbase] or t_av[ptbase + 1]:
                                wts.append(("T", max(t_av[ptbase], t_av[ptbase + 1])))
                            s_expg[(kg, par_h, c)] = ph.op("scalar", lambda e, par_h=par_h, ptbase=ptbase: e.activation(
                                pt[:, ptbase:ptbase + 2, :], pb[:, par_h * 2:par_h * 2 + 2, :],
                                AF.Exp, bias=b.s_cf[:, 2:3]), waits=wts, inc="S")
                    if kg >= 1:
                        kga = kg - 1
                        for par_h in range(2):
                            for j2 in range(2):
                                ka = 2 * kga + j2
                                kt = kts[ka][2]
                                slot = (kga % 2) * 4 + par_h * 2 + j2
                                obank = 6 + par_h
                                avw = [("S", s_expg[(kga, par_h, c)])]
                                if ka == 0:
                                    if v_last_pair.get(c - 1) is not None:
                                        avw.append(("V", v_last_pair[c - 1]))
                                    elif qb and c == 0:
                                        avw.append(("V", st["v_tr"][par_h]))
                                t_av[slot] = ph.op("tensor", lambda e, c=c, kt=kt, slot=slot, obank=obank, par_h=par_h, ka=ka, nk=nk, qpar=qpar: e.matmul(
                                    pb[0:65, obank, :], vTMl[:, kt, 2 * c + par_h, 0:65],
                                    pt[:, slot, :], start=(ka == 0), stop=(ka == nk - 1)),
                                    waits=avw, inc="T")
                                t_avl[par_h] = t_av[slot]
                h0, h1 = 2 * c, 2 * c + 1
                ph.op("vector", lambda e, h=h0: e.tensor_copy(
                    lsums[32 * (h // 4):32 * (h // 4) + 1, h % 4, :], pb[64:65, 6, :]),
                    waits=[("T", t_avl[0])], inc="V")
                ph.op("vector", lambda e, h=h1: e.tensor_copy(
                    lsums[32 * (h // 4):32 * (h // 4) + 1, h % 4, :], pb[64:65, 7, :]),
                    waits=[("T", t_avl[1])], inc="V")
                ph.op("vector", lambda e, c=c, qpar=qpar: e.tensor_copy(
                    qv2[0:64, qpar, c, :], pb[0:64, 6, :]), inc="V")
                v_oev = ph.op("vector", lambda e, c=c, qpar=qpar: e.tensor_copy(
                    qv2[64:128, qpar, c, :], pb[0:64, 7, :]), inc="V")
                v_last_pair[c] = v_oev
                for co in inter[c]:
                    emit_co_group(qb + 1, co)
            l_lc = None
            for g in range(2):
                gw = [("V", v_oev)]
                if g == 0 and st["t_rn_prev"]:
                    gw.append(("T", st["t_rn_prev"]))
                l_lc = ph.dma("sync", lsumb[4 * g:4 * g + 4, :],
                              lsums[32 * g:32 * g + 1, :, :], waits=gw, sem="Lx")
            st["l_lc"] = l_lc
            v_rcl = ph.op("vector", lambda e: e.reciprocal(lsumb, lsumb),
                          waits=[("Lx", l_lc)], inc="V")
            v_nrm = None
            for c in range(4):
                wts = [("V", v_rcl)]
                if v_nrm is not None:
                    wts.append(("V", v_nrm))
                if c == 0 and st["rb"][2] is not None:
                    wts.append(("V", st["rb"][2]))
                t_rn = ph.op("tensor", lambda e, c=c: e.matmul(
                    pb[:, 2, :], b.c_sel[0:8, c * 128:c * 128 + 128], lsumb,
                    start=True, stop=True), waits=wts, inc="T")
                v_nrm = ph.op("vector", lambda e, c=c, qpar=qpar: e.tensor_tensor(
                    qv2[:, qpar, c, :], qv2[:, qpar, c, :], pb[:, 2, :], ALU.mult),
                    waits=[("T", t_rn)], inc="V")
            st["t_rn_prev"] = t_rn
            st["rb"][2] = v_nrm
            for f in range(8):
                par = f % 2
                tlast = None
                waits = [("V", v_nrm)]
                if f == 0 and qb == 0:
                    waits.append(("Lg", l_wo))
                if f <= 1:
                    le = s_expg.get((ngr - 1, par, 3))
                    if le is not None:
                        waits.append(("S", le))
                if s_wo[par] is not None:
                    waits.append(("V", s_wo[par]))
                for ci in range(4):
                    tlast = ph.op("tensor", lambda e, ci=ci, par=par, f=f, qpar=qpar: e.matmul(
                        pb[:, par, :], wosb[:, ci, f * 128:f * 128 + 128],
                        qv2[:, qpar, ci, :], start=(ci == 0), stop=(ci == 3)),
                        waits=(waits if ci == 0 else []), inc="T" if ci == 3 else None)
                s_wo[par] = ph.op("vector", lambda e, f=f, par=par, qc=qc: e.tensor_tensor(
                    b.x[:, f, qc + 1:qc + 513], b.x[:, f, qc + 1:qc + 513],
                    pb[:, par, :], ALU.add), waits=[("T", tlast)], inc="V")
            if qb < NQB - 1:
                v_q = emit_post(qb + 1, 0)
                v_k = emit_post(qb + 1, 1)
                emit_v(qb + 1)
                prep_done[qb + 1] = (v_q, v_k)
        ph.run()


def phase_inner_ffn(prog, b, stack, l, pre=None):
    nc = prog.nc
    with contextlib.ExitStack() as fstack:
        sb = lambda name, shape, dt: fstack.enter_context(nc.sbuf_tensor(name + str(l), list(shape), dt))
        xh = sb("xhat", [128, DC, 1024], BF16)
        G = sb("Gbuf", [128, NJ, 1024], BF16)
        w1b = sb("w1b", [128, 2, DC, 256], BF16)
        w2b = sb("w2b", [128, 2, NJ, 128], BF16)
        gel = sb("gel", [128, 2, QB], BF16)
        xbndf = sb("xbndf", [128, DC, 2], F32)
        pb = fstack.enter_context(nc.psum_tensor("psF" + str(l), [128, 8, QB], F32))

        for hf in range(2):
            ph = Phase(prog, "F%d_%d" % (l, hf))
            if hf == 1 and pre:
                pre(ph)
            if hf == 0:
                ph.op("vector", lambda e: e.tensor_copy(
                    xbndf[:, :, 0:1], b.x[:, :, 1024:1025]), inc="V")
            v_app = None
            for sg in range(2):
                qb = 2 * hf + sg
                qc = qb * QB
                vw = ph.op("vector", lambda e, qb=qb: e.tensor_copy(
                    b.cast[:, :, 0:513], b.x[:, :, qb * QB:qb * QB + 513]),
                    waits=([("V", v_app)] if v_app else []), inc="V")
                if hf == 1 and sg == 0:
                    vw = ph.op("vector", lambda e: e.tensor_copy(
                        b.cast[:, :, 0:1], xbndf[:, :, 0:1]), inc="V")
                def src(c, qb=qb):
                    return (b.cast[:, c, 1:513] if c < 4 else b.cast[:, c, 0:512])
                v_r = emit_stats_qb(ph, b, pb, qb, src,
                                    b.stm[:, qc:qc + QB], b.str_[:, qc:qc + QB], vw)
                o = sg * QB
                v_a = ph.op("vector", lambda e, o=o, qc=qc: e.tensor_tensor(
                    xh[:, 0:4, o:o + QB], b.cast[:, 0:4, 1:513],
                    bc(b.stm[:, qc:qc + QB], 4), ALU.subtract),
                    waits=[("V", v_r)], inc="V")
                v_a = ph.op("vector", lambda e, o=o, qc=qc: e.tensor_tensor(
                    xh[:, 4:8, o:o + QB], b.cast[:, 4:8, 0:512],
                    bc(b.stm[:, qc:qc + QB], 4), ALU.subtract), inc="V")
                v_a = ph.op("vector", lambda e, o=o, qc=qc: e.tensor_tensor(
                    xh[:, :, o:o + QB], xh[:, :, o:o + QB],
                    bc(b.str_[:, qc:qc + QB], DC), ALU.mult), inc="V")
                for c in range(DC):
                    v_a = ph.op("vector", lambda e, c=c, o=o: e.tensor_scalar(
                        xh[:, c, o:o + QB], xh[:, c, o:o + QB],
                        b.s_lng[:, l, c:c + 1], b.s_lnb[:, l, c:c + 1],
                        ALU.mult, ALU.add), inc="V")
                v_app = v_a
            v_g = {0: None, 1: None}
            t_wcons = {0: 0, 1: 0}
            l_w1 = {}
            for j in range(NJ):
                jp = j % 2
                wts = []
                if t_wcons[jp]:
                    wts.append(("T", t_wcons[jp]))
                l_w1[j] = ph.dma("gpsimd", w1b[:, jp, :, :],
                                 b.w1i[l, :, 256 * j:256 * j + 256].rearrange(
                                     "(c p) f -> p c f", p=128), waits=wts,
                                 sem="Lw%d" % jp)
                for sg in range(2):
                    o = sg * QB
                    bA = sg
                    bG = 2 + sg
                    waits = [("Lw%d" % jp, l_w1[j]), ("V", v_app)]
                    if v_g[sg] is not None:
                        waits = [("Lw%d" % jp, l_w1[j]), ("V", v_g[sg])]
                    tl_a = None
                    for ci in range(8):
                        tl_a = ph.op("tensor", lambda e, ci=ci, jp=jp, bA=bA, o=o: e.matmul(
                            pb[:, bA, :], w1b[:, jp, ci, 0:128], xh[:, ci, o:o + QB],
                            start=(ci == 0), stop=(ci == 7)),
                            waits=(waits if ci == 0 else []), inc="T" if ci == 7 else None)
                    tl_g = None
                    for ci in range(8):
                        tl_g = ph.op("tensor", lambda e, ci=ci, jp=jp, bG=bG, o=o: e.matmul(
                            pb[:, bG, :], w1b[:, jp, ci, 128:256], xh[:, ci, o:o + QB],
                            start=(ci == 0), stop=(ci == 7)), inc="T" if ci == 7 else None)
                    t_wcons[jp] = tl_g
                    gw = [("T", tl_g)]
                    if v_g[sg] is not None:
                        gw.append(("V", v_g[sg]))
                    s_g = ph.op("scalar", lambda e, bG=bG, j=j, sg=sg: e.activation(
                        gel[:, sg, :], pb[:, bG, :], AF.Gelu, bias=b.s_b1i[:, l, 2 * j + 1:2 * j + 2]),
                        waits=gw, inc="S")
                    v_g[sg] = ph.op("vector", lambda e, bA=bA, j=j, o=o, s_g=s_g, sg=sg: e.scalar_tensor_tensor(
                        G[:, j, o:o + QB], pb[:, bA, :], b.s_b1i[:, l, 2 * j:2 * j + 1],
                        gel[:, sg, :], ALU.add, ALU.mult),
                        waits=[("S", s_g), ("T", tl_a)], inc="V")
            v_u = {0: None, 1: None, 2: None, 3: None}
            t_w2c = {0: 0, 1: 0}
            for f in range(8):
                fp = f % 2
                wts = []
                if t_w2c[fp]:
                    wts.append(("T", t_w2c[fp]))
                elif t_wcons[fp]:
                    wts.append(("T", t_wcons[fp]))
                l_w2 = ph.dma("gpsimd", w2b[:, fp, :, :],
                              b.w2p[l, :, 128 * f:128 * f + 128].rearrange(
                                  "(c p) f -> p c f", p=128), waits=wts,
                              sem="Lw%d" % fp)
                for sg in range(2):
                    o = sg * QB
                    bank = 4 + 2 * fp + sg
                    tl = None
                    waits = [("Lw%d" % fp, l_w2), ("V", v_g[0]), ("V", v_g[1])]
                    if v_u[bank - 4] is not None:
                        waits.append(("V", v_u[bank - 4]))
                    for j in range(NJ):
                        tl = ph.op("tensor", lambda e, j=j, fp=fp, bank=bank, o=o: e.matmul(
                            pb[:, bank, :], w2b[:, fp, j, :], G[:, j, o:o + QB],
                            start=(j == 0), stop=(j == NJ - 1)),
                            waits=(waits if j == 0 else []), inc="T" if j == NJ - 1 else None)
                    t_w2c[fp] = tl
                    qc = (2 * hf + sg) * QB
                    v_u[bank - 4] = ph.op("vector", lambda e, f=f, bank=bank, qc=qc: e.scalar_tensor_tensor(
                        b.x[:, f, qc + 1:qc + 513], pb[:, bank, :],
                        b.s_b2[:, l, f:f + 1], b.x[:, f, qc + 1:qc + 513],
                        ALU.add, ALU.add), waits=[("T", tl)], inc="V")
            ph.run()


def phase_out(prog, b, stack):
    nc = prog.nc
    with contextlib.ExitStack() as ostack:
        sb = lambda name, shape, dt: ostack.enter_context(nc.sbuf_tensor(name, list(shape), dt))
        xo = sb("xo", [128, 2, DC, QB], F32)
        tmp = sb("tmpo", [128, 2, DC, QB], F32)
        pb = ostack.enter_context(nc.psum_tensor("psO", [128, 2, QB], F32))
        ph = Phase(prog, "OUT")
        vws = []
        for qb in range(NQB):
            vw = ph.op("vector", lambda e, qb=qb: e.tensor_copy(
                b.cast[:, :, 0:513], b.x[:, :, qb * QB:qb * QB + 513]), inc="V")
            v_r = emit_stats_qb(ph, b, pb, qb,
                                lambda c: b.cast[:, c, 1:513],
                                b.stm[:, qb * QB:(qb + 1) * QB],
                                b.str_[:, qb * QB:(qb + 1) * QB], vw)
            vws.append(v_r)
        v_ds = {}
        st_st = {}
        l_prev = 0
        for qb in range(NQB):
            qc = qb * QB
            sl = qb % 2
            ssem = "Lx" if sl == 0 else "Ld"
            lw = [("Lg", l_prev)] if l_prev else [("Lg", prog.cnt["Lg"])]
            if qb >= 2:
                lw.append(("V", v_ds[qb - 2]))
            l_xo = ph.dma("gpsimd", xo[:, sl], rearr(b.xsave)[:, :, qc:qc + QB],
                          waits=lw, sem="Lg")
            l_prev = l_xo
            wts = [("V", vws[qb])]
            if qb >= 2:
                wts.append((ssem, st_st[qb - 2]))
            ph.op("vector", lambda e, qc=qc, sl=sl: e.tensor_tensor(
                tmp[:, sl], b.x[:, :, qc + 1:qc + 513], bc(b.stm[:, qc:qc + QB], DC),
                ALU.subtract), waits=wts, inc="V")
            ph.op("vector", lambda e, qc=qc, sl=sl: e.tensor_tensor(
                tmp[:, sl], tmp[:, sl], bc(b.str_[:, qc:qc + QB], DC), ALU.mult), inc="V")
            v_ds[qb] = ph.op("vector", lambda e, sl=sl: e.tensor_tensor(
                tmp[:, sl], tmp[:, sl], xo[:, sl], ALU.add), waits=[("Lg", l_xo)], inc="V")
            st_st[qb] = ph.dma("sync", rearr(b.yT)[:, :, qc:qc + QB], tmp[:, sl],
                               waits=[("V", v_ds[qb])], sem=ssem)
        ph.op("sync", lambda e: e.nop(),
              waits=[("Lx", prog.cnt["Lx"]), ("Ld", prog.cnt["Ld"])])
        ph.run()


def tap_x(prog, b, name="tapx"):
    ph = Phase(prog, name)
    ph.dma("sync", rearr(b.yT), b.x[:, :, 1:NT + 1])
    ph.op("sync", lambda e: e.nop(), waits=[("Ls", prog.cnt["Ls"])])
    ph.run()


def build(stop=None, taps=()):
    nc = bass.Bass("TRN2", target_bir_lowering=False)
    with contextlib.ExitStack() as stack:
        b = declare_dram(nc, stop)
        prog = Prog(nc, stack)
        alloc_globals(nc, stack, b)
        phase_load(prog, b)
        if stop == "load":
            tap_x(prog, b)
            return nc
        phase_dense(prog, b, stack, taps, stop)
        if stop in ("d0", "d1"):
            return nc
        if "kT" in taps:
            return nc
        if stop == "dense":
            tap_x(prog, b)
            return nc
        pf = {}

        def mk_pre(l):
            def pre(ph):
                pf[l] = ph.dma("gpsimd", pf[("buf", l)][:],
                               b.wqkv[l, :, :].rearrange("(c p) f -> p c f", p=128),
                               waits=[("Lg", prog.cnt["Lg"])])
            return pre

        with contextlib.ExitStack() as s_a0:
            pf[("buf", 0)] = s_a0.enter_context(
                nc.sbuf_tensor("wkvA0", [128, DC, 1536], BF16))
            phase_h2(prog, b, stack, pre=mk_pre(0))
            if stop == "h2":
                tap_x(prog, b)
                return nc
            phase_inner_attn(prog, b, stack, 0, wkv=pf[("buf", 0)], l_wkv_pre=pf[0])
        if stop == "attn0":
            tap_x(prog, b)
            return nc
        with contextlib.ExitStack() as s_a1:
            pf[("buf", 1)] = s_a1.enter_context(
                nc.sbuf_tensor("wkvA1", [128, DC, 1536], BF16))
            phase_inner_ffn(prog, b, stack, 0, pre=mk_pre(1))
            if stop == "ffn0":
                tap_x(prog, b)
                return nc
            phase_inner_attn(prog, b, stack, 1, wkv=pf[("buf", 1)], l_wkv_pre=pf[1])
        if stop == "attn1":
            tap_x(prog, b)
            return nc
        phase_inner_ffn(prog, b, stack, 1)
        if stop == "ffn1":
            tap_x(prog, b)
            return nc
        phase_out(prog, b, stack)
    return nc


def make_in_maps(I, n_cores=8):
    shared = host_prep(I)
    x = np.asarray(I["L0_attn_x"])
    maps = []
    for c in range(n_cores):
        m = dict(shared)
        m["xT"] = np.ascontiguousarray(x[c].T)
        maps.append(m)
    return maps


def kernel(**I):
    nc = build(stop=None, taps=())
    maps = make_in_maps(I, n_cores=8)
    res = run_bass_kernel_spmd(nc, maps, core_ids=list(range(8)))
    out = np.stack([np.asarray(r["yT"]).T for r in res.results])
    return np.ascontiguousarray(out, dtype=np.float32)


# version 22
# speedup vs baseline: 1.0163x; 1.0028x over previous
import contextlib
import numpy as np
import ml_dtypes

BFNP = ml_dtypes.bfloat16

D1LITE = ""

import concourse.bass as bass
import concourse.mybir as mybir
from concourse.bass_utils import run_bass_kernel_spmd

F32 = mybir.dt.float32
BF16 = mybir.dt.bfloat16
AF = mybir.ActivationFunctionType
ALU = mybir.AluOpType

NT = 2048
QB = 512
NQB = NT // QB
DC = 8
XW = 2056
HID = 2730
HIDP = 2816
NJ = HIDP // 128
EPS = 1e-5
C_DENSE = 10.0
C_LOC = 8.0
MBIG = -30000.0

ENGINES = ("tensor", "scalar", "vector", "gpsimd", "sync")


class Prog:
    def __init__(self, nc, stack):
        self.nc = nc
        self.stack = stack
        self.sems = {}
        self.cnt = {}
        for s in ("Lg", "Ls", "Lw0", "Lw1", "Lx", "Lo", "Ld", "T", "S", "V", "G"):
            self.sems[s] = stack.enter_context(nc.semaphore(s))
            self.cnt[s] = 0


class Phase:
    def __init__(self, prog, name):
        self.prog = prog
        self.name = name
        self.ops = {e: [] for e in ENGINES}
        self.entry = {e: [] for e in ENGINES}

    def op(self, eng, fn, waits=(), inc=None, n=1):
        waits = list(waits)
        if eng == "vector":
            if inc is None:
                inc = "V"
            waits.append(("V", self.prog.cnt["V"]))
        elif eng == "scalar":
            if inc is None:
                inc = "S"
            waits.append(("S", self.prog.cnt["S"]))
        self.ops[eng].append((waits, fn, inc, n))
        if inc is not None:
            self.prog.cnt[inc] += n
            return self.prog.cnt[inc]
        return None

    def dma(self, eng, out, in_, waits=(), sem=None):
        if sem is None:
            sem = "Lg" if eng == "gpsimd" else "Ls"
        return self.op(eng, lambda e, o=out, i=in_: e.dma_start(out=o, in_=i),
                       waits, sem, 16)

    def tapdma(self, out, in_, waits=()):
        w = list(waits) + [("Ld", self.prog.cnt["Ld"])]
        return self.op("gpsimd", lambda e, o=out, i=in_: e.dma_start(out=o, in_=i),
                       w, "Ld", 16)

    def run(self):
        nc = self.prog.nc
        prog = self.prog
        with nc.Block() as blk:
            for eng in ENGINES:
                ops = self.ops[eng]
                if not ops:
                    continue

                def body(e, ops=ops, ent=self.entry[eng]):
                    with nc.allow_low_precision(reason="bf16 storage by design; fp32 accum"):
                        for s, v in ent:
                            e.wait_ge(prog.sems[s], v)
                        for waits, fn, inc, n in ops:
                            for s, v in waits:
                                e.wait_ge(prog.sems[s], v)
                            instr = fn(e)
                            if inc is not None:
                                instr.then_inc(prog.sems[inc], n)
                getattr(blk, eng)(body)


def bc(ap2d, mid):
    return ap2d.unsqueeze(1).broadcast_to([ap2d.shape[0], mid, ap2d.shape[1]])



def host_prep(I):
    wdt = np.asarray(I["L0_attn_x"]).dtype
    f8 = np.float64

    def cast(a):
        return np.ascontiguousarray(np.asarray(a, dtype=wdt))

    qkv_w = np.asarray(I["L0_attn_qkv_w"])
    out = {}
    out["wqT"] = cast(qkv_w[0:1024].T)
    out["wkT"] = cast(qkv_w[1024:2048].T)
    out["wvT"] = cast(qkv_w[2048:3072].T)
    out["projT"] = cast(np.asarray(I["L0_attn_proj_w"]).T)
    qkv_b = np.asarray(I["L0_attn_qkv_b"])
    out["bq"] = cast(qkv_b[0:1024].reshape(8, 128).T)
    out["bk"] = cast(qkv_b[1024:2048].reshape(8, 128).T)
    out["bv"] = cast(qkv_b[2048:3072].reshape(8, 128).T)
    out["bproj"] = cast(np.asarray(I["L0_attn_proj_b"]).reshape(8, 128).T)

    out["wqkv"] = cast(I["L0_ffn_wqkv"])
    out["wo"] = cast(I["L0_ffn_wo"])
    out["qs"] = cast(np.tile(np.asarray(I["L0_ffn_q_scale"]), (1, 2)).T)
    out["ks"] = cast(np.tile(np.asarray(I["L0_ffn_k_scale"]), (1, 2)).T)
    out["lng"] = cast(np.asarray(I["L0_ffn_ln_g"]).reshape(2, 8, 128).transpose(2, 0, 1))
    out["lnb"] = cast(np.asarray(I["L0_ffn_ln_b"]).reshape(2, 8, 128).transpose(2, 0, 1))

    w1 = np.asarray(I["L0_ffn_w1"])
    b1 = np.asarray(I["L0_ffn_b1"])
    a = np.zeros((2, 1024, HIDP), dtype=wdt); a[:, :, :HID] = w1[:, :, :HID]
    g = np.zeros((2, 1024, HIDP), dtype=wdt); g[:, :, :HID] = w1[:, :, HID:]
    w1i = np.zeros((2, 1024, 2 * HIDP), dtype=wdt)
    b1a = np.zeros((2, HIDP), dtype=wdt); b1a[:, :HID] = b1[:, :HID]
    b1g = np.zeros((2, HIDP), dtype=wdt); b1g[:, :HID] = b1[:, HID:]
    b1i = np.zeros((128, 2, 2 * NJ), dtype=wdt)
    for j in range(NJ):
        w1i[:, :, 256 * j:256 * j + 128] = a[:, :, 128 * j:128 * j + 128]
        w1i[:, :, 256 * j + 128:256 * j + 256] = g[:, :, 128 * j:128 * j + 128]
        b1i[:, :, 2 * j] = b1a[:, 128 * j:128 * j + 128].T
        b1i[:, :, 2 * j + 1] = b1g[:, 128 * j:128 * j + 128].T
    out["w1i"] = np.ascontiguousarray(w1i)
    out["b1i"] = np.ascontiguousarray(b1i)
    w2p = np.zeros((2, HIDP, 1024), dtype=wdt)
    w2p[:, :HID] = np.asarray(I["L0_ffn_w2"])
    out["w2p"] = w2p
    out["b2"] = cast(np.asarray(I["L0_ffn_b2"]).reshape(2, 8, 128).transpose(2, 0, 1))

    i32 = np.arange(32, dtype=f8)
    inv_freq = 10000.0 ** (-(2 * i32) / 64.0)
    t = np.arange(NT, dtype=f8)
    fr = t[:, None] * inv_freq[None, :]
    base = (2 * i32 + 0.4 * 64) / (1.4 * 64)
    power = (t - NT // 2) / (512 // 2)
    sc = base[None, :] ** power[:, None]
    cos = np.cos(fr); sin = np.sin(fr)
    def dup(v):
        w = np.concatenate([v, v], axis=1)
        w = np.concatenate([w, w], axis=1)
        return np.ascontiguousarray(w.T)
    out["csq"] = dup(8.0 * cos * sc).astype(BFNP)
    out["snq"] = dup(8.0 * sin * sc).astype(BFNP)
    out["csk"] = dup(cos / sc).astype(BFNP)
    out["snk"] = dup(sin / sc).astype(BFNP)

    out["iden"] = cast(np.eye(128))
    out["ones"] = cast(np.ones((128, 128)))
    R = np.zeros((128, 128), dtype=f8)
    for B in (0, 64):
        for d in range(32):
            R[B + d + 32, B + d] = -1.0
            R[B + d, B + d + 32] = 1.0
    out["rrot"] = cast(R)
    out["mself"] = cast(np.triu(np.full((128, 128), MBIG), 1))
    out["cneg"] = cast(np.full((1, 512), MBIG))
    out["mprev"] = cast(np.tril(np.full((128, 128), MBIG), -1))
    sel = np.zeros((16, 8, 128), dtype=f8)
    for h in range(16):
        for c in range(8):
            for p in range(128):
                if h == 2 * c + (1 if p >= 64 else 0):
                    sel[h, c, p] = 1.0
    out["sel"] = cast(sel.reshape(16, 1024))
    el2 = np.zeros((128, 4, 8), dtype=f8)
    for p in range(128):
        for c in range(4):
            el2[p, c, 2 * c + (1 if p >= 64 else 0)] = 1.0
    out["el2"] = cast(el2.reshape(128, 32))
    selh = np.zeros((16, 16, 128), dtype=f8)
    for h in range(16):
        selh[h, h, :] = 1.0
    out["selh"] = cast(selh.transpose(1, 0, 2).reshape(16, 2048))
    el2d = np.zeros((128, 8, 16), dtype=f8)
    for p in range(128):
        for c in range(8):
            el2d[p, c, 2 * c + (1 if p >= 64 else 0)] = 1.0
    out["el2d"] = cast(el2d.reshape(128, 128))
    cf = np.zeros((128, 4), dtype=f8)
    cf[:, 0] = EPS
    cf[:, 1] = 1e-24
    cf[:, 2] = -C_LOC
    out["constf"] = cast(cf)
    return out



class B:
    pass


def declare_dram(nc, stop):
    b = B()
    ap = lambda h: h.ap()
    d = lambda name, shape: ap(nc.dram_tensor(name, list(shape), F32, kind="ExternalInput"))
    b.xT = d("xT", (1024, NT))
    b.wqT, b.wkT, b.wvT, b.projT = d("wqT", (1024, 1024)), d("wkT", (1024, 1024)), d("wvT", (1024, 1024)), d("projT", (1024, 1024))
    b.bq, b.bk, b.bv, b.bproj = d("bq", (128, 8)), d("bk", (128, 8)), d("bv", (128, 8)), d("bproj", (128, 8))
    b.wqkv = d("wqkv", (2, 1024, 1536))
    b.wo = d("wo", (2, 512, 1024))
    b.qs, b.ks = d("qs", (128, 2)), d("ks", (128, 2))
    b.lng, b.lnb = d("lng", (128, 2, 8)), d("lnb", (128, 2, 8))
    b.w1i, b.b1i = d("w1i", (2, 1024, 2 * HIDP)), d("b1i", (128, 2, 2 * NJ))
    b.w2p, b.b2 = d("w2p", (2, HIDP, 1024)), d("b2", (128, 2, 8))
    b.csq, b.snq, b.csk, b.snk = (
        ap(nc.dram_tensor(n, [128, NT], BF16, kind="ExternalInput"))
        for n in ("csq", "snq", "csk", "snk"))
    b.iden, b.ones, b.rrot = d("iden", (128, 128)), d("ones", (128, 128)), d("rrot", (128, 128))
    b.mself, b.mprev = d("mself", (128, 128)), d("mprev", (128, 128))
    b.cneg = d("cneg", (1, 512))
    b.sel, b.el2 = d("sel", (16, 1024)), d("el2", (128, 32))
    b.constf = d("constf", (128, 4))
    b.el2d = d("el2d", (128, 128))
    b.selh = d("selh", (16, 2048))
    b.xsave = ap(nc.dram_tensor("xsave", [1024, NT], F32, kind="Internal"))
    b.yT = ap(nc.dram_tensor("yT", [1024, NT], F32, kind="ExternalOutput"))
    b.dbg = ap(nc.dram_tensor("dbg", [1024, NT], F32, kind="ExternalOutput"))
    return b


def rearr(ap_, p=128):
    return ap_.rearrange("(c p) t -> p c t", p=p)


def alloc_globals(nc, stack, b):
    sb = lambda name, shape, dt: stack.enter_context(nc.sbuf_tensor(name, list(shape), dt))
    b.x = sb("x_res", [128, DC, XW], F32)
    b.cast = sb("castbuf", [128, DC, 520], BF16)
    b.sq = sb("sqbuf", [128, 4, QB], BF16)
    b.sqm = sb("sqm", [128, QB], F32)
    b.stm = sb("stat_m", [128, NT], BF16)
    b.str_ = sb("stat_r", [128, NT], BF16)
    b.st2m = sb("stat2_m", [128, QB], BF16)
    b.st2r = sb("stat2_r", [128, QB], BF16)
    b.c_id = sb("c_id", [128, 128], BF16)
    b.c_on = sb("c_on", [128, 128], BF16)
    b.c_rr = sb("c_rr", [128, 128], BF16)
    b.c_ms = sb("c_ms", [128, 128], BF16)
    b.c_mp = sb("c_mp", [128, 128], BF16)
    b.c_ng = sb("c_ng", [1, 512], BF16)
    b.c_sel = sb("c_sel", [16, 1024], BF16)
    b.c_el2 = sb("c_el2", [128, 32], BF16)
    b.s_bq = sb("s_bq", [128, 8], F32)
    b.s_bk = sb("s_bk", [128, 8], F32)
    b.s_bv = sb("s_bv", [128, 8], F32)
    b.s_bp = sb("s_bp", [128, 8], F32)
    b.s_qs = sb("s_qs", [128, 2], F32)
    b.s_ks = sb("s_ks", [128, 2], F32)
    b.s_lng = sb("s_lng", [128, 2, 8], F32)
    b.s_lnb = sb("s_lnb", [128, 2, 8], F32)
    b.s_b1i = sb("s_b1i", [128, 2, 2 * NJ], F32)
    b.s_b2 = sb("s_b2", [128, 2, 8], F32)
    b.s_cf = sb("s_cf", [128, 4], F32)
    b.c_el2d = sb("c_el2d", [128, 128], BF16)
    b.c_selh = sb("c_selh", [16, 2048], BF16)


def phase_load(prog, b):
    ph = Phase(prog, "load")
    ph.dma("sync", b.x[:, :, 1:NT + 1], rearr(b.xT))
    for dst, src in ((b.s_bq, b.bq), (b.s_bk, b.bk), (b.s_bv, b.bv), (b.s_bp, b.bproj)):
        ph.dma("sync", dst[:], src[:, :])
    ph.dma("sync", b.s_qs[:], b.qs[:, :])
    ph.dma("sync", b.s_ks[:], b.ks[:, :])
    ph.dma("sync", b.s_lng[:], b.lng[:, :, :])
    ph.dma("sync", b.s_lnb[:], b.lnb[:, :, :])
    ph.dma("sync", b.s_b1i[:], b.b1i[:, :, :])
    ph.dma("sync", b.s_b2[:], b.b2[:, :, :])
    ph.dma("sync", b.s_cf[:], b.constf[:, :])
    for dst, src in ((b.c_id, b.iden), (b.c_on, b.ones), (b.c_rr, b.rrot),
                     (b.c_ms, b.mself), (b.c_mp, b.mprev), (b.c_ng, b.cneg)):
        ph.dma("gpsimd", dst[:], src[:, :])
    ph.dma("gpsimd", b.c_sel[:], b.sel[:, :])
    ph.dma("gpsimd", b.c_el2[:], b.el2[:, :])
    ph.dma("gpsimd", b.c_el2d[:], b.el2d[:, :])
    ph.dma("gpsimd", b.c_selh[:], b.selh[:, :])
    ph.op("vector", lambda e: e.memset(b.x[:, :, 0:1], 0.0), inc="V")
    ph.run()
    prog.mark_ls = prog.cnt["Ls"]
    prog.mark_lg = prog.cnt["Lg"]



def emit_stats_qb(ph, b, pb, qb, src_slices, dst_m, dst_r, last_cast_wait):
    s_sq1 = None
    for c in range(4):
        s_sq1 = ph.op("scalar", lambda e, c=c: e.activation(
            b.sq[:, c, :], src_slices(c), AF.Square), waits=[("V", last_cast_wait)],
            inc="S")
    for c in range(4):
        ph.op("tensor", lambda e, c=c: e.matmul(
            pb[:, 0, :], b.c_on[:], src_slices(c), start=(c == 0), stop=False),
            waits=([("V", last_cast_wait)] if c == 0 else []))
    t_x2a = None
    for c in range(4):
        t_x2a = ph.op("tensor", lambda e, c=c: e.matmul(
            pb[:, 1, :], b.c_on[:], b.sq[:, c, :], start=(c == 0), stop=False),
            waits=([("S", s_sq1)] if c == 0 else []), inc="T" if c == 3 else None)
    s_sq2 = None
    for c in range(4, 8):
        s_sq2 = ph.op("scalar", lambda e, c=c: e.activation(
            b.sq[:, c - 4, :], src_slices(c), AF.Square),
            waits=([("T", t_x2a)] if c == 4 else []), inc="S")
    t_x = None
    for c in range(4, 8):
        t_x = ph.op("tensor", lambda e, c=c: e.matmul(
            pb[:, 0, :], b.c_on[:], src_slices(c), start=False, stop=(c == 7)),
            inc="T" if c == 7 else None)
    t_x2 = None
    for c in range(4, 8):
        t_x2 = ph.op("tensor", lambda e, c=c: e.matmul(
            pb[:, 1, :], b.c_on[:], b.sq[:, c - 4, :], start=False, stop=(c == 7)),
            waits=([("S", s_sq2)] if c == 4 else []), inc="T" if c == 7 else None)
    s_m = ph.op("scalar", lambda e: e.activation(
        dst_m, pb[:, 0, :], AF.Copy, scale=1.0 / 1024), waits=[("T", t_x)], inc="S")
    s_m2 = ph.op("scalar", lambda e: e.activation(
        b.sqm[:], dst_m, AF.Square), waits=[("S", s_m)], inc="S")
    v_var = ph.op("vector", lambda e: e.scalar_tensor_tensor(
        b.sqm[:], pb[:, 1, :], 1.0 / 1024, b.sqm[:], ALU.mult, ALU.subtract),
        waits=[("S", s_m2), ("T", t_x2)], inc="V")
    s_sd = ph.op("scalar", lambda e: e.activation(
        b.sqm[:], b.sqm[:], AF.Sqrt, bias=b.s_cf[:, 0:1]), waits=[("V", v_var)], inc="S")
    v_r = ph.op("vector", lambda e: e.reciprocal(dst_r, b.sqm[:]),
                waits=[("S", s_sd)], inc="V")
    return v_r



def phase_dense(prog, b, stack, taps, stop=None):
    nc = prog.nc
    with contextlib.ExitStack() as dstack:
        sb = lambda name, shape, dt: dstack.enter_context(nc.sbuf_tensor(name, list(shape), dt))
        kT = sb("kT", [128, DC, NT], BF16)
        vTM = sb("vTM", [128, 16, 16, 66], BF16)
        v_sb = sb("v_sb", [128, QB], BF16)
        kmax = sb("kmax", [16, 2], F32)
        knrm = sb("knrm", [16, QB], F32)

        wstack = contextlib.ExitStack()
        sbw = lambda name, shape, dt: wstack.enter_context(nc.sbuf_tensor(name, list(shape), dt))
        wA = sbw("wA", [128, DC, 1024], BF16)
        wB = sbw("wB", [128, DC, 1024], BF16)
        ph = Phase(prog, "D0")
        for _e in ("vector", "tensor", "scalar"):
            ph.entry[_e] = [("Ls", prog.mark_ls), ("Lg", prog.mark_lg)]
        l_wk = ph.dma("gpsimd", wA[:], rearr(b.wkT), waits=[("Lg", prog.mark_lg)])
        l_wv = ph.dma("gpsimd", wB[:], rearr(b.wvT))
        d0stack = contextlib.ExitStack()
        pb = d0stack.enter_context(nc.psum_tensor("psD0", [128, 2, QB], F32))
        for qb in range(NQB):
            vw = ph.op("vector", lambda e, qb=qb: e.tensor_copy(
                b.cast[:, :, 0:513], b.x[:, :, qb * QB:qb * QB + 513]), inc="V")
            emit_stats_qb(ph, b, pb, qb,
                          lambda c: b.cast[:, c, 1:513],
                          b.stm[:, qb * QB:(qb + 1) * QB],
                          b.str_[:, qb * QB:(qb + 1) * QB], vw)
        ph.run()
        d0stack.close()
        if stop == "d0":
            tap_x(prog, b)
            return

        ph = Phase(prog, "D1")
        d1stack = contextlib.ExitStack()
        pb = d1stack.enter_context(nc.psum_tensor("psD1", [128, 8, QB], F32))
        t_evt = {}
        last_users = 0
        tr_evt = {0: 0, 1: 0}
        s_last = {0: None, 1: None}
        v_tr = {0: None, 1: None}
        t_tr = {0: None, 1: None}
        s_knrm = None
        t_kn_hist = {}
        prev_mm = {}
        for qb in range(NQB):
            vw = ph.op("vector", lambda e, qb=qb: e.tensor_copy(
                b.cast[:, :, 0:513], b.x[:, :, qb * QB:qb * QB + 513]),
                waits=([("T", prev_mm.get("last", 0))] if qb else []), inc="V")
            vw = ph.op("vector", lambda e, qb=qb: e.tensor_tensor(
                b.cast[:, :, 1:513], b.cast[:, :, 1:513],
                bc(b.stm[:, qb * QB:(qb + 1) * QB], DC), ALU.subtract), inc="V")
            vw = ph.op("vector", lambda e, qb=qb: e.tensor_tensor(
                b.cast[:, :, 1:513], b.cast[:, :, 1:513],
                bc(b.str_[:, qb * QB:(qb + 1) * QB], DC), ALU.mult), inc="V")
            tlast = None
            for co in range(16):
                par = co % 2
                wsb = wA if co < 8 else wB
                wcol = co * 128 if co < 8 else (co - 8) * 128
                waits = [("V", vw)]
                if co == 0 and qb == 0:
                    waits.append(("Lg", l_wv))
                if s_last[par] is not None:
                    waits.append(("S", s_last[par]))
                for ci in range(8):
                    tlast = ph.op("tensor", lambda e, ci=ci, par=par, wsb=wsb, wcol=wcol: e.matmul(
                        pb[:, par, :], wsb[:, ci, wcol:wcol + 128],
                        b.cast[:, ci, 1:513], start=(ci == 0), stop=(ci == 7)),
                        waits=(waits if ci == 0 else []), inc="T" if ci == 7 else None)
                if co < 8:
                    s_ev = ph.op("scalar", lambda e, co=co, par=par, qb=qb: e.activation(
                        kT[:, co, qb * QB:(qb + 1) * QB], pb[:, par, :], AF.Identity,
                        bias=b.s_bk[:, co:co + 1]), waits=[("T", tlast)], inc="S")
                    if D1LITE in ("proj", "tr"):
                        s_last[par] = s_ev
                        continue
                    sqw = []
                    if co >= 4 and t_kn_hist.get(co - 4):
                        sqw.append(("T", t_kn_hist[co - 4]))
                    s_ksq = ph.op("scalar", lambda e, co=co, qb=qb: e.activation(
                        b.sq[:, co % 4, :], kT[:, co, qb * QB:(qb + 1) * QB],
                        AF.Square), waits=sqw, inc="S")
                    s_last[par] = s_ksq
                    kw = [("S", s_ksq)]
                    if co == 0 and s_knrm is not None:
                        kw.append(("S", s_knrm))
                    t_kn = ph.op("tensor", lambda e, co=co: e.matmul(
                        pb[0:16, 2, :], b.c_el2d[:, co * 16:co * 16 + 16],
                        b.sq[:, co % 4, :], start=(co == 0), stop=(co == 7)),
                        waits=kw, inc="T")
                    t_kn_hist[co] = t_kn
                    if co == 7:
                        s_knrm = ph.op("scalar", lambda e: e.activation(
                            knrm[:], pb[0:16, 2, :], AF.Sqrt), waits=[("T", t_kn)], inc="S")
                        if qb == 0:
                            v_km = ph.op("vector", lambda e: e.reduce_max(
                                kmax[:, 0:1], knrm[:], axis=mybir.AxisListType.X),
                                waits=[("S", s_knrm)], inc="V")
                        else:
                            v_km = ph.op("vector", lambda e: e.reduce_max(
                                kmax[:, 1:2], knrm[:], axis=mybir.AxisListType.X),
                                waits=[("S", s_knrm)], inc="V")
                            v_km = ph.op("vector", lambda e: e.tensor_max(
                                kmax[:, 0:1], kmax[:, 0:1], kmax[:, 1:2]), inc="V")
                else:
                    cv = co - 8
                    s_ev = ph.op("scalar", lambda e, par=par, cv=cv: e.activation(
                        v_sb[:], pb[:, par, :], AF.Identity, bias=b.s_bv[:, cv:cv + 1]),
                        waits=[("T", tlast)] + ([("T", t_tr.get("v_done", 0))] if False else []),
                        inc="S")
                    s_last[par] = s_ev
                    for tt in range(4):
                        if D1LITE == "proj":
                            continue
                        slot = tt % 2
                        twaits = [("S", s_ev)]
                        if v_tr[slot] is not None:
                            twaits.append(("V", v_tr[slot]))
                        t_tr[slot] = ph.op("tensor", lambda e, tt=tt, slot=slot: e.transpose(
                            pb[:, 4 + slot, 0:64].bitcast(BF16),
                            v_sb[:, tt * 128:tt * 128 + 128], b.c_id[:]),
                            waits=twaits, inc="T")
                        v_tr[slot] = ph.op("vector", lambda e, qb=qb, tt=tt, cv=cv, slot=slot: e.tensor_copy(
                            vTM[:, qb * 4 + tt, 2 * cv:2 * cv + 2, 0:64],
                            pb[:, 4 + slot, 0:64].bitcast(BF16)),
                            waits=[("T", t_tr[slot])], inc="V")
            prev_mm["last"] = tlast
        ph.op("vector", lambda e: e.memset(vTM[:, :, :, 64:65], 1.0), inc="V")
        ph.run()
        d1stack.close()
        wstack.close()
        if stop == "d1":
            tap_x(prog, b)
            return

        if "kT" in taps:
            tp = Phase(prog, "tap_kT")
            tp.dma("gpsimd", rearr(b.yT), kT[:, :, :])
            tp.op("sync", lambda e: e.nop(), waits=[("Lg", prog.cnt["Lg"])])
            tp.run()
            return

        ph = Phase(prog, "D2")
        d2stack = contextlib.ExitStack()
        sb2 = lambda name, shape, dt: d2stack.enter_context(nc.sbuf_tensor(name, list(shape), dt))
        wQ = sb2("wQ", [128, 2, DC, 128], BF16)
        wP = sb2("wP", [128, 2, DC, 128], BF16)
        qto = sb2("qTO", [128, DC, QB], BF16)
        qn = knrm
        bq16 = sb2("bq16", [16, QB], BF16)
        bnegb = b.sq
        tmp2 = sb2("tmp2", [128, 4, QB], F32)
        attb = sb2("attb", [128, DC, QB], BF16)
        sums = b.cast[:, 4:8, 0:512]
        pt = sb2("PTbuf", [128, 8, QB], BF16)
        sumsb = sb2("sumsb", [16, QB], BF16)
        pb = d2stack.enter_context(nc.psum_tensor("psD2", [128, 8, QB], F32))
        s_evq = {0: None, 1: None}
        s_att = {0: None, 1: None}
        t_wp_cons = {0: 0, 1: 0}
        t_wq_cons = {0: 0, 1: 0}
        l_sc_prev = 0
        v_resid = 0
        t_rn_prev = 0
        s_qn_prev = 0
        for qb in range(NQB):
            qc = qb * QB
            cw = []
            if qb:
                cw.append(("T", t_wq_cons[1] or 0))
                if l_sc_prev:
                    cw.append(("Lx", l_sc_prev))
            vw = ph.op("vector", lambda e, qb=qb: e.tensor_copy(
                b.cast[:, :, 1:513], b.x[:, :, qb * QB + 1:qb * QB + 513]),
                waits=cw, inc="V")
            vw = ph.op("vector", lambda e, qc=qc: e.tensor_tensor(
                b.cast[:, :, 1:513], b.cast[:, :, 1:513],
                bc(b.stm[:, qc:qc + QB], DC), ALU.subtract), inc="V")
            vw = ph.op("vector", lambda e, qc=qc: e.tensor_tensor(
                b.cast[:, :, 1:513], b.cast[:, :, 1:513],
                bc(b.str_[:, qc:qc + QB], DC), ALU.mult), inc="V")
            l_taph = 0
            if "d2" in taps and qb == 0:
                l_taph = ph.tapdma(rearr(b.dbg)[:, 0:1, 1536:2048],
                                   b.stm[0:128, 0:512].unsqueeze(1), waits=[("V", vw)])
                l_taph = ph.tapdma(rearr(b.dbg)[:, 1:2, 1536:2048],
                                   b.str_[0:128, 0:512].unsqueeze(1), waits=[("V", vw)])
                l_taph = ph.tapdma(rearr(b.dbg)[:, 2:4, 1536:2048],
                                   b.cast[:, 0:2, 1:513], waits=[("V", vw)])
            ev_co = {}
            for co in range(8):
                par = co % 2
                qw = []
                if t_wq_cons[par]:
                    qw.append(("T", t_wq_cons[par]))
                if t_wp_cons[par]:
                    qw.append(("T", t_wp_cons[par]))
                l_wq = ph.dma("gpsimd", wQ[:, par, :, :],
                              b.wqT[:, 128 * co:128 * co + 128].rearrange(
                                  "(c p) f -> p c f", p=128), waits=qw, sem="Lw%d" % par)
                waits = [("V", vw), ("Lw%d" % par, l_wq)]
                if s_evq[par] is not None:
                    waits.append(("V", s_evq[par]))
                if s_att[par] is not None:
                    waits.append(("S", s_att[par]))
                tlast = None
                for ci in range(8):
                    tlast = ph.op("tensor", lambda e, ci=ci, par=par, co=co: e.matmul(
                        pb[:, par, :], wQ[:, par, ci, :],
                        b.cast[:, ci, 1:513], start=(ci == 0), stop=(ci == 7)),
                        waits=(waits if ci == 0 else []), inc="T" if ci == 7 else None)
                t_wq_cons[par] = tlast
                s_evq[par] = ph.op("vector", lambda e, co=co, par=par: e.tensor_scalar(
                    qto[:, co, :], pb[:, par, :], b.s_bq[:, co:co + 1], 0.0,
                    ALU.add, ALU.add), waits=[("T", tlast)], inc="V")
                ev_co[co] = s_evq[par]
            l_tapq = 0
            if "d2" in taps and qb == 0:
                l_tapq = ph.tapdma(rearr(b.dbg)[:, :, 512:1024], qto[:, :, :],
                                   waits=[("V", s_evq[0]), ("V", s_evq[1])])
            t_qn = None
            s_qsq = None
            for grp in range(2):
                for c4 in range(4):
                    c = grp * 4 + c4
                    sqw = [("V", ev_co[c])]
                    if grp == 1 and c4 == 0:
                        sqw.append(("T", t_qn))
                    s_qsq = ph.op("scalar", lambda e, c=c, c4=c4: e.activation(
                        b.sq[:, c4, :], qto[:, c, :], AF.Square), waits=sqw, inc="S")
                for c4 in range(4):
                    c = grp * 4 + c4
                    qw = [("S", s_qsq)] if c4 == 0 else []
                    if grp == 0 and c4 == 0:
                        qw.append(("S", s_qn_prev))
                        if t_rn_prev:
                            qw.append(("T", t_rn_prev))
                    t_qn = ph.op("tensor", lambda e, c=c, c4=c4: e.matmul(
                        pb[0:16, 6, :], b.c_el2d[:, c * 16:c * 16 + 16],
                        b.sq[:, c4, :], start=(c == 0), stop=(c == 7)),
                        waits=qw, inc="T")
            s_qn_prev = ph.op("scalar", lambda e: e.activation(
                qn[:], pb[0:16, 6, :], AF.Sqrt), waits=[("T", t_qn)], inc="S")
            v_bq = ph.op("vector", lambda e: e.tensor_scalar(
                bq16[:], qn[:], kmax[:, 0:1], -1.0, ALU.mult, ALU.mult),
                waits=[("S", s_qn_prev)] + ([("T", t_rn_prev)] if t_rn_prev else []),
                inc="V")
            v_oev = None
            v_last_of_pair = {}
            s_expg = {}
            v_addg = {}
            v_bb = {}
            t_av = {i: 0 for i in range(8)}
            for c in range(8):
                obase = 4 + 2 * (c % 2)
                for par_h in range(2):
                    h = 2 * c + par_h
                    bslot = (c % 2) * 2 + par_h
                    wts = [("V", v_bq)]
                    if v_addg.get((7, par_h, c - 1)) is not None:
                        wts.append(("V", v_addg[(7, par_h, c - 1)]))
                    t_s = ph.op("tensor", lambda e, h=h, par_h=par_h: e.matmul(
                        pb[:, par_h * 2, :], b.c_selh[:, h * 128:h * 128 + 128],
                        bq16[:], start=True, stop=True), waits=wts, inc="T")
                    v_bb[bslot] = ph.op("vector", lambda e, bslot=bslot, par_h=par_h: e.tensor_copy(
                        bnegb[:, bslot, :], pb[:, par_h * 2, :]), waits=[("T", t_s)], inc="V")
                t_avl = {}
                for kg in range(9):
                    if kg < 8:
                        tsm = {}
                        for par_h in range(2):
                            for j2 in range(2):
                                kc = 2 * kg + j2
                                bank = par_h * 2 + j2
                                waits = []
                                if s_evq[0] is not None and kg == 0 and par_h == 0 and j2 == 0:
                                    waits += [("V", s_evq[0]), ("V", s_evq[1])]
                                prev_add = v_addg.get((kg - 1, par_h, c)) or v_addg.get((7, par_h, c - 1))
                                if kg == 0:
                                    waits.append(("V", v_bb[(c % 2) * 2 + par_h]))
                                if prev_add is not None:
                                    waits.append(("V", prev_add))
                                tsm[par_h] = ph.op("tensor", lambda e, kc=kc, r0=64 * par_h, bank=bank, c=c: e.matmul(
                                    pb[:, bank, :], kT[r0:r0 + 64, c, kc * 128:kc * 128 + 128],
                                    qto[r0:r0 + 64, c, :], start=True, stop=True,
                                    tile_position=(r0, 0)), waits=waits, inc="T")
                        for par_h in range(2):
                            bslot = (c % 2) * 2 + par_h
                            aw = [("T", tsm[par_h])]
                            prev_exp = s_expg.get((kg - 1, par_h, c)) or s_expg.get((7, par_h, c - 1))
                            if prev_exp is not None:
                                aw.append(("S", prev_exp))
                            v_addg[(kg, par_h, c)] = ph.op("vector", lambda e, par_h=par_h, bslot=bslot: e.tensor_tensor(
                                tmp2[:, par_h * 2:par_h * 2 + 2, :],
                                pb[:, par_h * 2:par_h * 2 + 2, :],
                                bnegb[:, bslot:bslot + 1, :].broadcast_to([128, 2, QB]),
                                ALU.add), waits=aw, inc="V")
                            ptbase = (kg % 2) * 4 + par_h * 2
                            wts = [("V", v_addg[(kg, par_h, c)])]
                            if t_av[ptbase] or t_av[ptbase + 1]:
                                wts.append(("T", max(t_av[ptbase], t_av[ptbase + 1])))
                            s_expg[(kg, par_h, c)] = ph.op("scalar", lambda e, par_h=par_h, ptbase=ptbase: e.activation(
                                pt[:, ptbase:ptbase + 2, :], tmp2[:, par_h * 2:par_h * 2 + 2, :],
                                AF.Exp, scale=0.125), waits=wts, inc="S")
                    if kg >= 1:
                        kga = kg - 1
                        for par_h in range(2):
                            for j2 in range(2):
                                ka = 2 * kga + j2
                                slot = (kga % 2) * 4 + par_h * 2 + j2
                                obank = obase + par_h
                                avw = [("S", s_expg[(kga, par_h, c)])]
                                if ka == 0:
                                    if v_last_of_pair.get(c - 2) is not None:
                                        avw.append(("V", v_last_of_pair[c - 2]))
                                    if c == 1 and s_qn_prev:
                                        avw.append(("S", s_qn_prev))
                                t_av[slot] = ph.op("tensor", lambda e, c=c, ka=ka, slot=slot, obank=obank, par_h=par_h: e.matmul(
                                    pb[0:65, obank, :], vTM[:, ka, 2 * c + par_h, 0:65],
                                    pt[:, slot, :], start=(ka == 0), stop=(ka == 15)),
                                    waits=avw, inc="T")
                                t_avl[par_h] = t_av[slot]
                h0, h1 = 2 * c, 2 * c + 1
                ph.op("vector", lambda e, h=h0, obase=obase: e.tensor_copy(
                    sums[32 * (h // 4):32 * (h // 4) + 1, h % 4, :], pb[64:65, obase, :]),
                    waits=[("T", t_avl[0])], inc="V")
                ph.op("vector", lambda e, h=h1, obase=obase: e.tensor_copy(
                    sums[32 * (h // 4):32 * (h // 4) + 1, h % 4, :], pb[64:65, obase + 1, :]),
                    waits=[("T", t_avl[1])], inc="V")
                ph.op("vector", lambda e, c=c, obase=obase: e.tensor_copy(
                    qto[0:64, c, :], pb[0:64, obase, :]),
                    waits=([("Ld", l_tapq)] if (c == 0 and l_tapq) else []), inc="V")
                v_oev = ph.op("vector", lambda e, c=c, obase=obase: e.tensor_copy(
                    qto[64:128, c, :], pb[0:64, obase + 1, :]), inc="V")
                v_last_of_pair[c] = v_oev
            l_sc = None
            for g in range(4):
                gw = [("V", v_oev)]
                if g == 0 and t_rn_prev:
                    gw.append(("T", t_rn_prev))
                l_sc = ph.dma("sync", sumsb[4 * g:4 * g + 4, :],
                              sums[32 * g:32 * g + 1, :, :], waits=gw, sem="Lx")
            l_sc_prev = l_sc
            v_rc = ph.op("vector", lambda e: e.reciprocal(sumsb[:], sumsb[:]),
                         waits=[("Lx", l_sc)], inc="V")
            v_nrm = None
            t_rn = None
            for c in range(8):
                wts = [("V", v_rc)]
                if v_nrm is not None:
                    wts.append(("V", v_nrm))
                t_rn = ph.op("tensor", lambda e, c=c: e.matmul(
                    pb[:, 6, :], b.c_sel[:, c * 128:c * 128 + 128], sumsb[:],
                    start=True, stop=True), waits=wts, inc="T")
                v_nrm = ph.op("vector", lambda e, c=c: e.tensor_tensor(
                    qto[:, c, :], qto[:, c, :], pb[:, 6, :], ALU.mult),
                    waits=[("T", t_rn)], inc="V")
            t_rn_prev = t_rn
            if "d2" in taps and qb == 0:
                ph.tapdma(rearr(b.dbg)[:, :, 0:512], qto[:, :, :],
                          waits=[("V", v_nrm)])
            for f in range(8):
                par = f % 2
                wts = []
                if t_wp_cons[par]:
                    wts.append(("T", t_wp_cons[par]))
                if t_wq_cons[par]:
                    wts.append(("T", t_wq_cons[par]))
                l_wp = ph.dma("gpsimd", wP[:, par, :, :],
                              b.projT[:, 128 * f:128 * f + 128].rearrange(
                                  "(c p) f -> p c f", p=128), waits=wts,
                              sem="Lw%d" % par)
                tlast = None
                waits = [("V", v_nrm), ("Lw%d" % par, l_wp)]
                if s_att[par] is not None:
                    waits.append(("S", s_att[par]))
                for ci in range(8):
                    tlast = ph.op("tensor", lambda e, ci=ci, par=par: e.matmul(
                        pb[:, par, :], wP[:, par, ci, :],
                        qto[:, ci, :], start=(ci == 0), stop=(ci == 7)),
                        waits=(waits if ci == 0 else []), inc="T" if ci == 7 else None)
                t_wp_cons[par] = tlast
                s_att[par] = ph.op("scalar", lambda e, f=f, par=par: e.activation(
                    attb[:, f, :], pb[:, par, :], AF.Identity,
                    bias=b.s_bp[:, f:f + 1]),
                    waits=[("T", tlast)] + ([("Ld", l_taph)] if (l_taph and f < 2) else []),
                    inc="S")
            l_tapa = 0
            if "d2" in taps and qb == 0:
                l_tapa = ph.tapdma(rearr(b.dbg)[:, :, 1024:1536], attb[:, :, :],
                                   waits=[("S", s_att[0]), ("S", s_att[1])])
            s_sq1 = None
            for c in range(4):
                sqw2 = []
                if c == 0:
                    sqw2 = [("S", s_att[0]), ("S", s_att[1])]
                    for ph_ in range(2):
                        la = v_addg.get((7, ph_, 7))
                        if la is not None:
                            sqw2.append(("V", la))
                s_sq1 = ph.op("scalar", lambda e, c=c: e.activation(
                    b.sq[:, c, :], attb[:, c, :], AF.Square),
                    waits=sqw2, inc="S")
            for c in range(4):
                ph.op("tensor", lambda e, c=c: e.matmul(
                    pb[:, 6, :], b.c_on[:], attb[:, c, :],
                    start=(c == 0), stop=False),
                    waits=([("S", s_att[0]), ("S", s_att[1]), ("V", v_nrm)] if c == 0 else []))
            t_x2a = None
            for c in range(4):
                t_x2a = ph.op("tensor", lambda e, c=c: e.matmul(
                    pb[:, 7, :], b.c_on[:], b.sq[:, c, :], start=(c == 0), stop=False),
                    waits=([("S", s_sq1)] if c == 0 else []), inc="T" if c == 3 else None)
            s_sq2 = None
            for c in range(4, 8):
                s_sq2 = ph.op("scalar", lambda e, c=c: e.activation(
                    b.sq[:, c - 4, :], attb[:, c, :], AF.Square),
                    waits=([("T", t_x2a)] if c == 4 else []), inc="S")
            t_x = None
            for c in range(4, 8):
                t_x = ph.op("tensor", lambda e, c=c: e.matmul(
                    pb[:, 6, :], b.c_on[:], attb[:, c, :], start=False,
                    stop=(c == 7)), inc="T" if c == 7 else None)
            t_x2 = None
            for c in range(4, 8):
                t_x2 = ph.op("tensor", lambda e, c=c: e.matmul(
                    pb[:, 7, :], b.c_on[:], b.sq[:, c - 4, :], start=False,
                    stop=(c == 7)),
                    waits=([("S", s_sq2)] if c == 4 else []), inc="T" if c == 7 else None)
            s_m = ph.op("scalar", lambda e: e.activation(
                b.st2m[:], pb[:, 6, :], AF.Copy, scale=1.0 / 1024),
                waits=[("T", t_x)], inc="S")
            s_m2 = ph.op("scalar", lambda e: e.activation(
                b.sqm[:], b.st2m[:], AF.Square), waits=[("S", s_m)], inc="S")
            v_var = ph.op("vector", lambda e: e.scalar_tensor_tensor(
                b.sqm[:], pb[:, 7, :], 1.0 / 1024, b.sqm[:], ALU.mult, ALU.subtract),
                waits=[("S", s_m2), ("T", t_x2)], inc="V")
            s_sd = ph.op("scalar", lambda e: e.activation(
                b.sqm[:], b.sqm[:], AF.Sqrt, bias=b.s_cf[:, 0:1]), waits=[("V", v_var)], inc="S")
            v_r = ph.op("vector", lambda e: e.reciprocal(b.st2r[:], b.sqm[:]),
                        waits=[("S", s_sd)], inc="V")
            v_mr = ph.op("vector", lambda e: e.tensor_tensor(
                b.sqm[:], b.st2m[:], b.st2r[:], ALU.mult), waits=[("V", v_r)], inc="V")
            ph.op("vector", lambda e: e.tensor_tensor(
                attb[:, :, :], attb[:, :, :], bc(b.st2r[:], DC), ALU.mult),
                waits=[("V", v_r)] + ([("Ld", l_tapa)] if l_tapa else []), inc="V")
            ph.op("vector", lambda e, qc=qc: e.tensor_tensor(
                b.x[:, :, qc + 1:qc + 513], b.x[:, :, qc + 1:qc + 513],
                attb[:, :, :], ALU.add), inc="V")
            v_resid = ph.op("vector", lambda e, qc=qc: e.tensor_tensor(
                b.x[:, :, qc + 1:qc + 513], b.x[:, :, qc + 1:qc + 513],
                bc(b.sqm[:], DC), ALU.subtract), waits=[("V", v_mr)], inc="V")
        ph.run()
        d2stack.close()


def phase_h2(prog, b, stack, pre=None):
    nc = prog.nc
    with contextlib.ExitStack() as hstack:
        pb = hstack.enter_context(nc.psum_tensor("psH2", [128, 2, QB], F32))
        ph = Phase(prog, "H2")
        if pre:
            pre(ph)
        l_sv = {}
        prevq = {"sync": prog.cnt["Ls"], "gpsimd": prog.cnt["Lg"]}
        for qb in range(NQB):
            qc = qb * QB
            eng = "sync" if qb % 2 == 0 else "gpsimd"
            sem = "Ls" if eng == "sync" else "Lg"
            l_sv[qb] = (sem, ph.dma(eng, rearr(b.xsave)[:, :, qc:qc + QB],
                                    b.x[:, :, qc + 1:qc + 513],
                                    waits=[(sem, prevq[eng])]))
            prevq[eng] = l_sv[qb][1]
        vws = []
        for qb in range(NQB):
            vw = ph.op("vector", lambda e, qb=qb: e.tensor_copy(
                b.cast[:, :, 0:513], b.x[:, :, qb * QB:qb * QB + 513]), inc="V")
            v_r = emit_stats_qb(ph, b, pb, qb,
                                lambda c: b.cast[:, c, 1:513],
                                b.stm[:, qb * QB:(qb + 1) * QB],
                                b.str_[:, qb * QB:(qb + 1) * QB], vw)
            vws.append(v_r)
        for qb in range(NQB):
            qc = qb * QB
            ph.op("vector", lambda e, qc=qc, qb=qb: e.tensor_tensor(
                b.x[:, :, qc + 1:qc + 513], b.x[:, :, qc + 1:qc + 513],
                bc(b.stm[:, qc:qc + QB], DC), ALU.subtract),
                waits=[l_sv[qb], ("V", vws[qb])], inc="V")
            ph.op("vector", lambda e, qc=qc: e.tensor_tensor(
                b.x[:, :, qc + 1:qc + 513], b.x[:, :, qc + 1:qc + 513],
                bc(b.str_[:, qc:qc + QB], DC), ALU.mult), inc="V")
        ph.run()


def phase_inner_attn(prog, b, stack, l, wkv=None, l_wkv_pre=None):
    nc = prog.nc
    with contextlib.ExitStack() as astack:
        sb = lambda name, shape, dt: astack.enter_context(nc.sbuf_tensor(name + str(l), list(shape), dt))
        r_csq = sb("r_csq", [128, NT], BF16)
        r_snq = sb("r_snq", [128, NT], BF16)
        r_csk = sb("r_csk", [128, NT], BF16)
        r_snk = sb("r_snk", [128, NT], BF16)
        kb = sb("kb", [128, 4, NT], BF16)
        vTMl = sb("vTMl", [128, 16, 8, 66], BF16)
        qv2 = sb("qv2", [128, 2, 4, QB], BF16)
        wosb = sb("wosb", [128, 4, 1024], BF16)
        v_sb = sb("v_sbA", [128, QB], BF16)
        u1b = sb("u1b", [128, 2, QB], BF16)
        u2b = sb("u2b", [128, 2, QB], BF16)
        pt = sb("PTA", [128, 8, QB], BF16)
        rnb = sb("rnb", [8, 2, QB], BF16)
        lsum = b.sqm[0:8, :]
        lsums = sb("lsumsA", [128, 4, QB], BF16)
        lsumb = b.st2m[0:8, :]
        xbnd = sb("xbndA", [128, DC, 4], F32)
        pb = astack.enter_context(nc.psum_tensor("psA" + str(l), [128, 8, QB], F32))
        pl = pb[0:8, 5, :]

        ph = Phase(prog, "A%d" % l)
        for dst, src in ((r_csq, b.csq), (r_snq, b.snq), (r_csk, b.csk), (r_snk, b.snk)):
            l_rc = ph.dma("sync", dst[:], src[:, :], sem="Lo")
        if wkv is None:
            wkv = astack.enter_context(nc.sbuf_tensor("wkvA" + str(l), [128, DC, 1536], BF16))
            l_wkv_pre = ph.dma("gpsimd", wkv[:],
                               b.wqkv[l, :, :].rearrange("(c p) f -> p c f", p=128),
                               waits=[("Lg", prog.cnt["Lg"])])
        l_wo = ph.dma("gpsimd", wosb[:], b.wo[l, :, :].rearrange("(c p) f -> p c f", p=128),
                      waits=[("Lg", l_wkv_pre)])
        l_wkv = l_wkv_pre
        v_ones = ph.op("vector", lambda e: e.memset(vTMl[:, :, :, 64:65], 1.0), inc="V")
        ph.entry["vector"] = [("Lo", l_rc)]
        v_bnd = ph.op("vector", lambda e: e.tensor_copy(
            xbnd[:, :, 0:3], b.x[:, :, 512:1537:512]), inc="V")

        st = {
            "s_prep": {0: None, 1: None},
            "v_tr": {0: None, 1: None},
            "s_ln": {},
            "t_pl": {},
            "s_pl_read": 0,
            "rb": {2: None, 3: None},
            "t_cast": 0,
            "l_lc": 0,
            "t_rn_prev": 0,
            "vw": {},
        }

        def emit_cast(qb):
            qc = qb * QB
            vw = ph.op("vector", lambda e, qb=qb: e.tensor_copy(
                b.cast[:, :, 0:513], b.x[:, :, qb * QB:qb * QB + 513]),
                waits=([("T", st["t_cast"])] if qb else []), inc="V")
            if qb > 0:
                vw = ph.op("vector", lambda e, qb=qb: e.tensor_copy(
                    b.cast[:, :, 0:1], xbnd[:, :, qb - 1:qb]), inc="V")
            st["vw"][qb] = vw

        def emit_co_group(qb, co):
            qc = qb * QB
            grp = 0 if co < 4 else 1
            c4 = co % 4
            waits = [("V", st["vw"][qb])]
            if qb == 0 and co == 0:
                waits.append(("Lg", l_wkv))
            if st["s_prep"][0] is not None:
                waits.append(("S", st["s_prep"][0]))
            tlast = None
            for ci in range(8):
                sl = (b.cast[:, ci, 1:513] if ci < 4 else b.cast[:, ci, 0:512])
                tlast = ph.op("tensor", lambda e, ci=ci, co=co, sl=sl: e.matmul(
                    pb[:, 4, :], wkv[:, ci, co * 128:co * 128 + 128], sl,
                    start=(ci == 0), stop=(ci == 7)),
                    waits=(waits if ci == 0 else []), inc="T" if ci == 7 else None)
            st["t_cast"] = tlast
            dst = (qv2[:, qb % 2, c4, :] if grp == 0 else kb[:, c4, qc:qc + QB])
            s_ev = ph.op("scalar", lambda e, dst=dst: e.activation(
                dst, pb[:, 4, :], AF.Copy), waits=[("T", tlast)], inc="S")
            sq_w = []
            if co == 0 and st["l_lc"]:
                sq_w.append(("Lx", st["l_lc"]))
            s_sq = ph.op("scalar", lambda e, c4=c4: e.activation(
                b.sq[:, c4, :], pb[:, 4, :], AF.Square), waits=sq_w, inc="S")
            st["s_prep"][0] = s_sq
            kw = [("S", s_sq)]
            if c4 == 0:
                kw.append(("S", st["s_pl_read"]))
            t_pl = ph.op("tensor", lambda e, c4=c4: e.matmul(
                pl, b.c_el2[:, c4 * 8:c4 * 8 + 8], b.sq[:, c4, :],
                start=(c4 == 0), stop=(c4 == 3)), waits=kw, inc="T")
            if c4 == 3:
                st["t_pl"][(qb, grp)] = t_pl
                s1 = ph.op("scalar", lambda e: e.activation(
                    lsum, pl, AF.Ln, bias=b.s_cf[0:8, 1:2]),
                    waits=[("T", t_pl)], inc="S")
                st["s_pl_read"] = s1
                st["s_ln"][(qb, grp)] = ph.op("scalar", lambda e, grp=grp: e.activation(
                    rnb[:, grp, :], lsum, AF.Exp, scale=-0.5),
                    waits=[("S", s1)], inc="S")

        def emit_post(qb, grp):
            qc = qb * QB
            scl = b.s_qs if grp == 0 else b.s_ks
            cs_ = r_csq if grp == 0 else r_csk
            sn_ = r_snq if grp == 0 else r_snk
            v_pr_last = None
            for c2 in range(4):
                rbank = 2 + (c2 % 2)
                wts2 = [("S", st["s_ln"][(qb, grp)])]
                if st["rb"][rbank] is not None:
                    wts2.append(("V", st["rb"][rbank]))
                t_rn = ph.op("tensor", lambda e, c2=c2, rbank=rbank, grp=grp: e.matmul(
                    pb[:, rbank, :], b.c_sel[0:8, c2 * 128:c2 * 128 + 128],
                    rnb[:, grp, :], start=True, stop=True), waits=wts2, inc="T")
                if grp == 0:
                    dst2 = qv2[:, qb % 2, c2, :]
                else:
                    dst2 = kb[:, c2, qc:qc + QB]
                v_a = ph.op("vector", lambda e, dst2=dst2, scl=scl, rbank=rbank: e.scalar_tensor_tensor(
                    dst2, dst2, scl[:, l:l + 1], pb[:, rbank, :],
                    ALU.mult, ALU.mult), waits=[("T", t_rn)], inc="V")
                g_u1 = ph.op("gpsimd", lambda e, dst2=dst2, cs_=cs_, c2=c2 % 2, qc=qc: e.tensor_tensor(
                    u1b[:, c2, :], dst2, cs_[:, qc:qc + QB], ALU.mult),
                    waits=[("V", v_a), ("Lo", l_rc)], inc="G")
                g_u2 = ph.op("gpsimd", lambda e, dst2=dst2, sn_=sn_, c2=c2 % 2, qc=qc: e.tensor_tensor(
                    u2b[:, c2, :], dst2, sn_[:, qc:qc + QB], ALU.mult),
                    inc="G")
                t_rt = ph.op("tensor", lambda e, c2=c2 % 2, rbank=rbank: e.matmul(
                    pb[:, rbank, :], b.c_rr[:], u2b[:, c2, :], start=True, stop=True),
                    waits=[("G", g_u2)], inc="T")
                v_pr_last = ph.op("vector", lambda e, dst2=dst2, c2=c2 % 2, rbank=rbank: e.tensor_tensor(
                    dst2, u1b[:, c2, :], pb[:, rbank, :], ALU.add),
                    waits=[("T", t_rt), ("G", g_u1)], inc="V")
                st["rb"][rbank] = v_pr_last
            return v_pr_last

        def emit_v(qb):
            qc = qb * QB
            for c4 in range(4):
                co = 8 + c4
                waits = [("V", st["vw"][qb])]
                if st["s_prep"][0] is not None:
                    waits.append(("S", st["s_prep"][0]))
                tlast = None
                for ci in range(8):
                    sl = (b.cast[:, ci, 1:513] if ci < 4 else b.cast[:, ci, 0:512])
                    tlast = ph.op("tensor", lambda e, ci=ci, co=co, sl=sl: e.matmul(
                        pb[:, 4, :], wkv[:, ci, co * 128:co * 128 + 128], sl,
                        start=(ci == 0), stop=(ci == 7)),
                        waits=(waits if ci == 0 else []), inc="T" if ci == 7 else None)
                st["t_cast"] = tlast
                s_ev = ph.op("scalar", lambda e: e.activation(
                    v_sb[:], pb[:, 4, :], AF.Copy), waits=[("T", tlast)], inc="S")
                st["s_prep"][0] = s_ev
                for tt in range(4):
                    slot = tt % 2
                    twaits = [("S", s_ev)]
                    if qb == 0 and c4 == 0 and tt == 0:
                        twaits.append(("V", v_ones))
                    if st["v_tr"][slot] is not None:
                        twaits.append(("V", st["v_tr"][slot]))
                    t_tr = ph.op("tensor", lambda e, tt=tt, slot=slot: e.transpose(
                        pb[:, 6 + slot, 0:64].bitcast(BF16),
                        v_sb[:, tt * 128:tt * 128 + 128], b.c_id[:]),
                        waits=twaits, inc="T")
                    st["v_tr"][slot] = ph.op("vector", lambda e, qb=qb, tt=tt, c4=c4, slot=slot: e.tensor_copy(
                        vTMl[:, qb * 4 + tt, 2 * c4:2 * c4 + 2, 0:64],
                        pb[:, 6 + slot, 0:64].bitcast(BF16)),
                        waits=[("T", t_tr)], inc="V")

        emit_cast(0)
        for co in range(8):
            emit_co_group(0, co)
        v_q0 = emit_post(0, 0)
        v_k0 = emit_post(0, 1)
        emit_v(0)
        prep_done = {0: (v_q0, v_k0)}

        s_wo = {0: None, 1: None}
        for qb in range(NQB):
            qc = qb * QB
            qpar = qb % 2
            kts = []
            if qb > 0:
                kts += [("prev", j, 4 * (qb - 1) + j) for j in range(4)]
            kts += [("self", j, 4 * qb + j) for j in range(4)]
            nk = len(kts)
            ngr = nk // 2
            if qb < NQB - 1:
                emit_cast(qb + 1)
            inter = [[] for _ in range(4)]
            if qb < NQB - 1:
                for co in range(8):
                    inter[co // 2].append(co)
            v_oev = None
            v_last_pair = {}
            s_expg = {}
            t_av = {i: 0 for i in range(8)}
            v_q_done, v_k_done = prep_done[qb]
            v_rot_done = [("V", v_q_done), ("V", v_k_done),
                          ("V", st["v_tr"][0]), ("V", st["v_tr"][1])]
            for par in range(2):
                if s_wo[par] is not None:
                    v_rot_done.append(("V", s_wo[par]))
            for c in range(4):
                t_avl = {}
                for kg in range(ngr + 1):
                    if kg < ngr:
                        tsm = {}
                        for par_h in range(2):
                            for j2 in range(2):
                                ji = 2 * kg + j2
                                side, j, kt = kts[ji]
                                bank = par_h * 2 + j2
                                r0 = 64 * par_h
                                waits = []
                                if c == 0 and kg == 0 and par_h == 0 and j2 == 0:
                                    waits += v_rot_done
                                prev_exp = s_expg.get((kg - 1, par_h, c)) or s_expg.get((ngr - 1, par_h, c - 1))
                                if prev_exp is not None:
                                    waits.append(("S", prev_exp))
                                ph.op("tensor", lambda e, c=c, kt=kt, r0=r0, bank=bank, qpar=qpar: e.matmul(
                                    pb[:, bank, :], kb[r0:r0 + 64, c, kt * 128:kt * 128 + 128],
                                    qv2[r0:r0 + 64, qpar, c, :], start=True, stop=False,
                                    tile_position=(r0, 0)), waits=waits)
                                msk = b.c_ms if side == "self" else b.c_mp
                                if side == "self":
                                    z0, z1 = 0, 128 * j
                                else:
                                    z0, z1 = 128 * j + 128, 512
                                has_span = z1 > z0
                                ph.op("tensor", lambda e, bank=bank, j=j, msk=msk, stp=not has_span: e.matmul(
                                    pb[:, bank, 128 * j:128 * j + 128], msk[:], b.c_id[:],
                                    start=False, stop=stp), inc=None if has_span else "T")
                                if has_span:
                                    tmm = ph.op("tensor", lambda e, bank=bank, z0=z0, z1=z1: e.matmul(
                                        pb[:, bank, z0:z1], b.c_on[0:1, 0:128],
                                        b.c_ng[0:1, 0:z1 - z0], start=False, stop=True),
                                        inc="T")
                                else:
                                    tmm = prog.cnt["T"]
                                tsm[(par_h, j2)] = tmm
                        for par_h in range(2):
                            ptbase = (kg % 2) * 4 + par_h * 2
                            wts = [("T", tsm[(par_h, 0)]), ("T", tsm[(par_h, 1)])]
                            if t_av[ptbase] or t_av[ptbase + 1]:
                                wts.append(("T", max(t_av[ptbase], t_av[ptbase + 1])))
                            s_expg[(kg, par_h, c)] = ph.op("scalar", lambda e, par_h=par_h, ptbase=ptbase: e.activation(
                                pt[:, ptbase:ptbase + 2, :], pb[:, par_h * 2:par_h * 2 + 2, :],
                                AF.Exp, bias=b.s_cf[:, 2:3]), waits=wts, inc="S")
                    if kg >= 1:
                        kga = kg - 1
                        for par_h in range(2):
                            for j2 in range(2):
                                ka = 2 * kga + j2
                                kt = kts[ka][2]
                                slot = (kga % 2) * 4 + par_h * 2 + j2
                                obank = 6 + par_h
                                avw = [("S", s_expg[(kga, par_h, c)])]
                                if ka == 0:
                                    if v_last_pair.get(c - 1) is not None:
                                        avw.append(("V", v_last_pair[c - 1]))
                                    elif qb and c == 0:
                                        avw.append(("V", st["v_tr"][par_h]))
                                t_av[slot] = ph.op("tensor", lambda e, c=c, kt=kt, slot=slot, obank=obank, par_h=par_h, ka=ka, nk=nk, qpar=qpar: e.matmul(
                                    pb[0:65, obank, :], vTMl[:, kt, 2 * c + par_h, 0:65],
                                    pt[:, slot, :], start=(ka == 0), stop=(ka == nk - 1)),
                                    waits=avw, inc="T")
                                t_avl[par_h] = t_av[slot]
                h0, h1 = 2 * c, 2 * c + 1
                ph.op("vector", lambda e, h=h0: e.tensor_copy(
                    lsums[32 * (h // 4):32 * (h // 4) + 1, h % 4, :], pb[64:65, 6, :]),
                    waits=[("T", t_avl[0])], inc="V")
                ph.op("vector", lambda e, h=h1: e.tensor_copy(
                    lsums[32 * (h // 4):32 * (h // 4) + 1, h % 4, :], pb[64:65, 7, :]),
                    waits=[("T", t_avl[1])], inc="V")
                ph.op("vector", lambda e, c=c, qpar=qpar: e.tensor_copy(
                    qv2[0:64, qpar, c, :], pb[0:64, 6, :]), inc="V")
                v_oev = ph.op("vector", lambda e, c=c, qpar=qpar: e.tensor_copy(
                    qv2[64:128, qpar, c, :], pb[0:64, 7, :]), inc="V")
                v_last_pair[c] = v_oev
                for co in inter[c]:
                    emit_co_group(qb + 1, co)
            l_lc = None
            for g in range(2):
                gw = [("V", v_oev)]
                if g == 0 and st["t_rn_prev"]:
                    gw.append(("T", st["t_rn_prev"]))
                l_lc = ph.dma("sync", lsumb[4 * g:4 * g + 4, :],
                              lsums[32 * g:32 * g + 1, :, :], waits=gw, sem="Lx")
            st["l_lc"] = l_lc
            v_rcl = ph.op("vector", lambda e: e.reciprocal(lsumb, lsumb),
                          waits=[("Lx", l_lc)], inc="V")
            v_nrm = None
            for c in range(4):
                wts = [("V", v_rcl)]
                if v_nrm is not None:
                    wts.append(("V", v_nrm))
                if c == 0 and st["rb"][2] is not None:
                    wts.append(("V", st["rb"][2]))
                t_rn = ph.op("tensor", lambda e, c=c: e.matmul(
                    pb[:, 2, :], b.c_sel[0:8, c * 128:c * 128 + 128], lsumb,
                    start=True, stop=True), waits=wts, inc="T")
                v_nrm = ph.op("vector", lambda e, c=c, qpar=qpar: e.tensor_tensor(
                    qv2[:, qpar, c, :], qv2[:, qpar, c, :], pb[:, 2, :], ALU.mult),
                    waits=[("T", t_rn)], inc="V")
            st["t_rn_prev"] = t_rn
            st["rb"][2] = v_nrm
            for f in range(8):
                par = f % 2
                tlast = None
                waits = [("V", v_nrm)]
                if f == 0 and qb == 0:
                    waits.append(("Lg", l_wo))
                if f <= 1:
                    le = s_expg.get((ngr - 1, par, 3))
                    if le is not None:
                        waits.append(("S", le))
                if s_wo[par] is not None:
                    waits.append(("V", s_wo[par]))
                for ci in range(4):
                    tlast = ph.op("tensor", lambda e, ci=ci, par=par, f=f, qpar=qpar: e.matmul(
                        pb[:, par, :], wosb[:, ci, f * 128:f * 128 + 128],
                        qv2[:, qpar, ci, :], start=(ci == 0), stop=(ci == 3)),
                        waits=(waits if ci == 0 else []), inc="T" if ci == 3 else None)
                s_wo[par] = ph.op("vector", lambda e, f=f, par=par, qc=qc: e.tensor_tensor(
                    b.x[:, f, qc + 1:qc + 513], b.x[:, f, qc + 1:qc + 513],
                    pb[:, par, :], ALU.add), waits=[("T", tlast)], inc="V")
            if qb < NQB - 1:
                v_q = emit_post(qb + 1, 0)
                v_k = emit_post(qb + 1, 1)
                emit_v(qb + 1)
                prep_done[qb + 1] = (v_q, v_k)
        ph.run()


def phase_inner_ffn(prog, b, stack, l, pre=None):
    nc = prog.nc
    with contextlib.ExitStack() as fstack:
        sb = lambda name, shape, dt: fstack.enter_context(nc.sbuf_tensor(name + str(l), list(shape), dt))
        xh = sb("xhat", [128, DC, 1024], BF16)
        G = sb("Gbuf", [128, NJ, 1024], BF16)
        w1b = sb("w1b", [128, 2, DC, 256], BF16)
        w2b = sb("w2b", [128, 2, NJ, 128], BF16)
        gel = sb("gel", [128, 2, QB], BF16)
        xbndf = sb("xbndf", [128, DC, 2], F32)
        pb = fstack.enter_context(nc.psum_tensor("psF" + str(l), [128, 8, QB], F32))

        for hf in range(2):
            ph = Phase(prog, "F%d_%d" % (l, hf))
            if hf == 1 and pre:
                pre(ph)
            if hf == 0:
                ph.op("vector", lambda e: e.tensor_copy(
                    xbndf[:, :, 0:1], b.x[:, :, 1024:1025]), inc="V")
            v_app = None
            for sg in range(2):
                qb = 2 * hf + sg
                qc = qb * QB
                vw = ph.op("vector", lambda e, qb=qb: e.tensor_copy(
                    b.cast[:, :, 0:513], b.x[:, :, qb * QB:qb * QB + 513]),
                    waits=([("V", v_app)] if v_app else []), inc="V")
                if hf == 1 and sg == 0:
                    vw = ph.op("vector", lambda e: e.tensor_copy(
                        b.cast[:, :, 0:1], xbndf[:, :, 0:1]), inc="V")
                def src(c, qb=qb):
                    return (b.cast[:, c, 1:513] if c < 4 else b.cast[:, c, 0:512])
                v_r = emit_stats_qb(ph, b, pb, qb, src,
                                    b.stm[:, qc:qc + QB], b.str_[:, qc:qc + QB], vw)
                o = sg * QB
                v_a = ph.op("vector", lambda e, o=o, qc=qc: e.tensor_tensor(
                    xh[:, 0:4, o:o + QB], b.cast[:, 0:4, 1:513],
                    bc(b.stm[:, qc:qc + QB], 4), ALU.subtract),
                    waits=[("V", v_r)], inc="V")
                v_a = ph.op("vector", lambda e, o=o, qc=qc: e.tensor_tensor(
                    xh[:, 4:8, o:o + QB], b.cast[:, 4:8, 0:512],
                    bc(b.stm[:, qc:qc + QB], 4), ALU.subtract), inc="V")
                v_a = ph.op("vector", lambda e, o=o, qc=qc: e.tensor_tensor(
                    xh[:, :, o:o + QB], xh[:, :, o:o + QB],
                    bc(b.str_[:, qc:qc + QB], DC), ALU.mult), inc="V")
                for c in range(DC):
                    v_a = ph.op("vector", lambda e, c=c, o=o: e.tensor_scalar(
                        xh[:, c, o:o + QB], xh[:, c, o:o + QB],
                        b.s_lng[:, l, c:c + 1], b.s_lnb[:, l, c:c + 1],
                        ALU.mult, ALU.add), inc="V")
                v_app = v_a
            v_g = {0: None, 1: None}
            t_wcons = {0: 0, 1: 0}
            l_w1 = {}
            for j in range(NJ):
                jp = j % 2
                wts = []
                if t_wcons[jp]:
                    wts.append(("T", t_wcons[jp]))
                l_w1[j] = ph.dma("gpsimd", w1b[:, jp, :, :],
                                 b.w1i[l, :, 256 * j:256 * j + 256].rearrange(
                                     "(c p) f -> p c f", p=128), waits=wts,
                                 sem="Lw%d" % jp)
                for sg in range(2):
                    o = sg * QB
                    bA = sg
                    bG = 2 + sg
                    waits = [("Lw%d" % jp, l_w1[j]), ("V", v_app)]
                    if v_g[sg] is not None:
                        waits = [("Lw%d" % jp, l_w1[j]), ("V", v_g[sg])]
                    tl_a = None
                    for ci in range(8):
                        tl_a = ph.op("tensor", lambda e, ci=ci, jp=jp, bA=bA, o=o: e.matmul(
                            pb[:, bA, :], w1b[:, jp, ci, 0:128], xh[:, ci, o:o + QB],
                            start=(ci == 0), stop=(ci == 7)),
                            waits=(waits if ci == 0 else []), inc="T" if ci == 7 else None)
                    tl_g = None
                    for ci in range(8):
                        tl_g = ph.op("tensor", lambda e, ci=ci, jp=jp, bG=bG, o=o: e.matmul(
                            pb[:, bG, :], w1b[:, jp, ci, 128:256], xh[:, ci, o:o + QB],
                            start=(ci == 0), stop=(ci == 7)), inc="T" if ci == 7 else None)
                    t_wcons[jp] = tl_g
                    gw = [("T", tl_g)]
                    if v_g[sg] is not None:
                        gw.append(("V", v_g[sg]))
                    s_g = ph.op("scalar", lambda e, bG=bG, j=j, sg=sg: e.activation(
                        gel[:, sg, :], pb[:, bG, :], AF.Gelu, bias=b.s_b1i[:, l, 2 * j + 1:2 * j + 2]),
                        waits=gw, inc="S")
                    v_g[sg] = ph.op("vector", lambda e, bA=bA, j=j, o=o, s_g=s_g, sg=sg: e.scalar_tensor_tensor(
                        G[:, j, o:o + QB], pb[:, bA, :], b.s_b1i[:, l, 2 * j:2 * j + 1],
                        gel[:, sg, :], ALU.add, ALU.mult),
                        waits=[("S", s_g), ("T", tl_a)], inc="V")
            v_u = {0: None, 1: None, 2: None, 3: None}
            t_w2c = {0: 0, 1: 0}
            for f in range(8):
                fp = f % 2
                wts = []
                if t_w2c[fp]:
                    wts.append(("T", t_w2c[fp]))
                elif t_wcons[fp]:
                    wts.append(("T", t_wcons[fp]))
                l_w2 = ph.dma("gpsimd", w2b[:, fp, :, :],
                              b.w2p[l, :, 128 * f:128 * f + 128].rearrange(
                                  "(c p) f -> p c f", p=128), waits=wts,
                              sem="Lw%d" % fp)
                for sg in range(2):
                    o = sg * QB
                    bank = 4 + 2 * fp + sg
                    tl = None
                    waits = [("Lw%d" % fp, l_w2), ("V", v_g[0]), ("V", v_g[1])]
                    if v_u[bank - 4] is not None:
                        waits.append(("V", v_u[bank - 4]))
                    for j in range(NJ):
                        tl = ph.op("tensor", lambda e, j=j, fp=fp, bank=bank, o=o: e.matmul(
                            pb[:, bank, :], w2b[:, fp, j, :], G[:, j, o:o + QB],
                            start=(j == 0), stop=(j == NJ - 1)),
                            waits=(waits if j == 0 else []), inc="T" if j == NJ - 1 else None)
                    t_w2c[fp] = tl
                    qc = (2 * hf + sg) * QB
                    v_u[bank - 4] = ph.op("vector", lambda e, f=f, bank=bank, qc=qc: e.scalar_tensor_tensor(
                        b.x[:, f, qc + 1:qc + 513], pb[:, bank, :],
                        b.s_b2[:, l, f:f + 1], b.x[:, f, qc + 1:qc + 513],
                        ALU.add, ALU.add), waits=[("T", tl)], inc="V")
            ph.run()


def phase_out(prog, b, stack):
    nc = prog.nc
    with contextlib.ExitStack() as ostack:
        sb = lambda name, shape, dt: ostack.enter_context(nc.sbuf_tensor(name, list(shape), dt))
        xo = sb("xo", [128, 2, DC, QB], F32)
        tmp = sb("tmpo", [128, 2, DC, QB], F32)
        pb = ostack.enter_context(nc.psum_tensor("psO", [128, 2, QB], F32))
        ph = Phase(prog, "OUT")
        vws = []
        for qb in range(NQB):
            vw = ph.op("vector", lambda e, qb=qb: e.tensor_copy(
                b.cast[:, :, 0:513], b.x[:, :, qb * QB:qb * QB + 513]), inc="V")
            v_r = emit_stats_qb(ph, b, pb, qb,
                                lambda c: b.cast[:, c, 1:513],
                                b.stm[:, qb * QB:(qb + 1) * QB],
                                b.str_[:, qb * QB:(qb + 1) * QB], vw)
            vws.append(v_r)
        v_ds = {}
        st_st = {}
        l_prev = 0
        for qb in range(NQB):
            qc = qb * QB
            sl = qb % 2
            ssem = "Lx" if sl == 0 else "Ld"
            lw = [("Lo", l_prev)] if l_prev else []
            if qb >= 2:
                lw.append(("V", v_ds[qb - 2]))
            l_xo = ph.dma("sync", xo[:, sl], rearr(b.xsave)[:, :, qc:qc + QB],
                          waits=lw, sem="Lo")
            l_prev = l_xo
            wts = [("V", vws[qb])]
            if qb >= 2:
                wts.append((ssem, st_st[qb - 2]))
            ph.op("vector", lambda e, qc=qc, sl=sl: e.tensor_tensor(
                tmp[:, sl], b.x[:, :, qc + 1:qc + 513], bc(b.stm[:, qc:qc + QB], DC),
                ALU.subtract), waits=wts, inc="V")
            ph.op("vector", lambda e, qc=qc, sl=sl: e.tensor_tensor(
                tmp[:, sl], tmp[:, sl], bc(b.str_[:, qc:qc + QB], DC), ALU.mult), inc="V")
            v_ds[qb] = ph.op("vector", lambda e, sl=sl: e.tensor_tensor(
                tmp[:, sl], tmp[:, sl], xo[:, sl], ALU.add), waits=[("Lo", l_xo)], inc="V")
            st_st[qb] = ph.dma("sync", rearr(b.yT)[:, :, qc:qc + QB], tmp[:, sl],
                               waits=[("V", v_ds[qb])], sem=ssem)
        ph.op("sync", lambda e: e.nop(),
              waits=[("Lx", prog.cnt["Lx"]), ("Ld", prog.cnt["Ld"])])
        ph.run()


def tap_x(prog, b, name="tapx"):
    ph = Phase(prog, name)
    ph.dma("sync", rearr(b.yT), b.x[:, :, 1:NT + 1])
    ph.op("sync", lambda e: e.nop(), waits=[("Ls", prog.cnt["Ls"])])
    ph.run()


def build(stop=None, taps=()):
    nc = bass.Bass("TRN2", target_bir_lowering=False)
    with contextlib.ExitStack() as stack:
        b = declare_dram(nc, stop)
        prog = Prog(nc, stack)
        alloc_globals(nc, stack, b)
        phase_load(prog, b)
        if stop == "load":
            tap_x(prog, b)
            return nc
        phase_dense(prog, b, stack, taps, stop)
        if stop in ("d0", "d1"):
            return nc
        if "kT" in taps:
            return nc
        if stop == "dense":
            tap_x(prog, b)
            return nc
        pf = {}

        def mk_pre(l):
            def pre(ph):
                pf[l] = ph.dma("gpsimd", pf[("buf", l)][:],
                               b.wqkv[l, :, :].rearrange("(c p) f -> p c f", p=128),
                               waits=[("Lg", prog.cnt["Lg"])])
            return pre

        with contextlib.ExitStack() as s_a0:
            pf[("buf", 0)] = s_a0.enter_context(
                nc.sbuf_tensor("wkvA0", [128, DC, 1536], BF16))
            phase_h2(prog, b, stack, pre=mk_pre(0))
            if stop == "h2":
                tap_x(prog, b)
                return nc
            phase_inner_attn(prog, b, stack, 0, wkv=pf[("buf", 0)], l_wkv_pre=pf[0])
        if stop == "attn0":
            tap_x(prog, b)
            return nc
        with contextlib.ExitStack() as s_a1:
            pf[("buf", 1)] = s_a1.enter_context(
                nc.sbuf_tensor("wkvA1", [128, DC, 1536], BF16))
            phase_inner_ffn(prog, b, stack, 0, pre=mk_pre(1))
            if stop == "ffn0":
                tap_x(prog, b)
                return nc
            phase_inner_attn(prog, b, stack, 1, wkv=pf[("buf", 1)], l_wkv_pre=pf[1])
        if stop == "attn1":
            tap_x(prog, b)
            return nc
        phase_inner_ffn(prog, b, stack, 1)
        if stop == "ffn1":
            tap_x(prog, b)
            return nc
        phase_out(prog, b, stack)
    return nc


def make_in_maps(I, n_cores=8):
    shared = host_prep(I)
    x = np.asarray(I["L0_attn_x"])
    maps = []
    for c in range(n_cores):
        m = dict(shared)
        m["xT"] = np.ascontiguousarray(x[c].T)
        maps.append(m)
    return maps


def kernel(**I):
    nc = build(stop=None, taps=())
    maps = make_in_maps(I, n_cores=8)
    res = run_bass_kernel_spmd(nc, maps, core_ids=list(range(8)))
    out = np.stack([np.asarray(r["yT"]).T for r in res.results])
    return np.ascontiguousarray(out, dtype=np.float32)
